# version 24
# baseline (speedup 1.0000x reference)
import numpy as np
import ml_dtypes
import concourse.bass as bass
import concourse.mybir as mybir
from concourse.bass_utils import run_bass_kernel_spmd


F32 = mybir.dt.float32
BF16 = mybir.dt.bfloat16
AF = mybir.ActivationFunctionType
ALU = mybir.AluOpType
AX = mybir.AxisListType

D = 1024
T = 2048
NDC = 8
NTT = 4
TS = 512
NSC = 16
DFF = 4096
NFC = 32
CHS = 2064
NEG = -1.0e30
EPS_LN = 1.0e-5
EPS_BN = 1.0e-5
EPS_LD = 1.0e-8

C_BD, C_OS, C_DEC, C_LA, C_GAM, C_LG, C_NLG, C_ALPHA = 0, 1, 2, 3, 4, 5, 6, 7
C_G12, C_G12B = 8, 10
C_R12 = 12
C_WMX, C_WMXR = 14, 18
C_R0, C_S0 = 19, 20
C_TMP = 21
C_E5, C_NH, C_E8, C_I32, C_Q4, C_HALFP = 22, 23, 24, 25, 26, 27


def build_nc():
    nc = bass.Bass("TRN2", target_bir_lowering=False)
    Q = {"qg": 0, "qs": 0, "da": 0, "db": 0, "dc_": 0, "dd": 0,
         "de": 0, "df": 0, "dg": 0, "dh": 0, "di": 0, "dj": 0,
         "dk": 0, "dl": 0, "dm": 0, "dn": 0, "t": 0, "v": 0, "s": 0}

    def din(name, shape, dt=F32):
        return nc.dram_tensor(name, shape, dt, kind="ExternalInput").ap()

    xT = din("xT", [D, T])
    Wq = din("Wq", [D, D])
    Wk = din("Wk", [D, D])
    WvT = din("WvT", [D, D])
    WdT = din("WdT", [D, 1])
    WoT = din("WoT", [D, D])
    kern = din("kern", [D, D])
    a_m = din("a_m", [D, 16])
    b_m = din("b_m", [16, D])
    WwrT = din("WwrT", [D, D])
    WrdT = din("WrdT", [D, D])
    W1T = din("W1T", [D, DFF])
    W2T = din("W2T", [DFF, D])
    cblobF_in = din("cblobF", [128, 2608])
    scIn_in = din("scIn", [128, 4])
    nmB = din("nmB", [128, 2048])
    cblobB_in = din("cblobB", [128, 384], BF16)

    outT = nc.dram_tensor("outT", [D, T], F32, kind="ExternalOutput").ap()
    x1S = nc.dram_tensor("x1S", [D, T], F32, kind="Internal").ap()
    W1qS = nc.dram_tensor("W1qS", [NFC, 128, 1024], BF16, kind="Internal").ap()
    W2qS = nc.dram_tensor("W2qS", [NDC, 128, 4096], BF16, kind="Internal").ap()
    ldS = nc.dram_tensor("ldS", [1, T], F32, kind="Internal").ap()
    bncS = nc.dram_tensor("bncS", [1, 16], F32, kind="Internal").ap()

    ctx = []

    def sb(name, shape, dt):
        cm = nc.sbuf_tensor(name, shape, dt)
        h = cm.__enter__()
        ctx.append(cm)
        return h

    def psum(name, shape):
        cm = nc.psum_tensor(name, shape, F32)
        h = cm.__enter__()
        ctx.append(cm)
        return h

    def sem(name):
        cm = nc.semaphore(name)
        h = cm.__enter__()
        ctx.append(cm)
        return h

    xf = sb("xf", [128, 4096], F32)
    xbf = sb("xbf", [128, 4096], BF16)
    xn = sb("xn", [128, NDC * CHS], BF16)
    bigA = sb("bigA", [128, 16384], BF16)
    bigB = sb("bigB", [128, 16384], BF16)
    stm = sb("stm", [128, 8192], BF16)
    rt = sb("rt", [128, 4096], BF16)
    wqA = sb("wqA", [128, 8192], BF16)
    projW = sb("projW", [128, 8192], BF16)
    cstF = sb("cstF", [128, 2672], F32)
    Bcs = cstF.ap()[:, 0:512]
    nm = cstF.ap()[:, 512:2560]
    biasL = cstF.ap()[:, 2560:2640]
    sc = cstF.ap()[:, 2640:2672]
    cstB = sb("cstB", [128, 384], BF16)
    idP = cstB.ap()[:, 0:128]
    idN = cstB.ap()[:, 128:256]
    inv1024 = cstB.ap()[:, 256:384]
    a_sb = sb("a_sb", [128, 128], BF16)
    b_sb = sb("b_sb", [16, 1024], BF16)
    wd_sb = sb("wd_sb", [128, 8], BF16)
    ld_cols = sb("ld_cols", [128, 16], F32)
    biascols = sb("biascols", [128, 16], F32)
    statsT = sb("statsT", [128, 2048], F32)
    maskb = sb("maskb", [128, 2048], F32)
    xa_bf = sb("xa_bf", [16, 512], BF16)

    statsB = statsT.ap().bitcast(BF16)
    rtF = rt.ap().bitcast(F32)
    stmF = stm.ap().bitcast(F32)

    pb = [psum(f"pb{k}", [128, 512]) for k in range(8)]

    sQg = sem("sQg")
    sQs = sem("sQs")
    sDa = sem("sDa")
    sDb = sem("sDb")
    sDc = sem("sDc")
    sDd = sem("sDd")
    sDe = sem("sDe")
    sDf = sem("sDf")
    sDg = sem("sDg")
    sDh = sem("sDh")
    sDi = sem("sDi")
    sDj = sem("sDj")
    sDk = sem("sDk")
    sDl = sem("sDl")
    sDm = sem("sDm")
    sDn = sem("sDn")
    sT = sem("sT")
    sV = sem("sV")
    sS = sem("sS")
    sG = sem("sG")

    def _dma_track(key, semh, n0, inst):
        inst.then_inc(semh, 16)
        add = 0
        for nmx in nc.inst_names()[n0:]:
            ins2 = nc.find_inst_by_name(nmx)
            si2 = getattr(ins2, "sync_info", None)
            if si2 is None:
                continue
            for u in si2.on_update:
                if u.ant_name == semh.name:
                    add += u.update_value
        assert add >= 16 and add % 16 == 0, (key, add)
        Q[key] += add
        return Q[key]

    class _QIssue:
        def __init__(self, key, semh):
            self.key, self.semh = key, semh

        def __call__(self, eng, dst, src, **kw):
            n0 = nc.n_instructions()
            inst = eng.dma_start(dst, src, **kw)
            return _dma_track(self.key, self.semh, n0, inst)

    qgD = _QIssue("qg", sQg)
    qsD = _QIssue("qs", sQs)
    qaD = _QIssue("da", sDa)
    qbD = _QIssue("db", sDb)
    qcD = _QIssue("dc_", sDc)
    qdD = _QIssue("dd", sDd)

    def slab_q(u):
        return (qaD, sDa) if u % 2 == 0 else (qbD, sDb)

    def store_q(u):
        return (qcD, sDc) if u % 2 == 0 else (qdD, sDd)

    qeD = _QIssue("de", sDe)
    qfD = _QIssue("df", sDf)

    def slab4_q(u):
        return [(qaD, sDa), (qbD, sDb), (qeD, sDe), (qfD, sDf)][u % 4]

    qgD2 = _QIssue("dg", sDg)
    qhD2 = _QIssue("dh", sDh)
    qiD = _QIssue("di", sDi)
    qjD = _QIssue("dj", sDj)
    qkD = _QIssue("dk", sDk)
    qlD = _QIssue("dl", sDl)
    qmD = _QIssue("dm", sDm)
    qnD = _QIssue("dn", sDn)

    def gam4_q(u):
        return [(qiD, sDi), (qjD, sDj), (qkD, sDk), (qlD, sDl)][u % 4]


    def store4_q(u):
        return [(qcD, sDc), (qdD, sDd), (qgD2, sDg), (qhD2, sDh)][u % 4]

    def ti(inst):
        Q["t"] += 1
        inst.then_inc(sT, 1)
        return Q["t"]

    def vi(inst):
        Q["v"] += 1
        inst.then_inc(sV, 1)
        return Q["v"]

    def si(inst):
        Q["s"] += 1
        inst.then_inc(sS, 1)
        return Q["s"]

    def xnp(dc, t0, shift):
        return dc * CHS + shift + t0

    def full_mat_src(t):
        return bass.AP(t.tensor, 0, [[D, 128], [128 * D, 8], [1, D]])

    def tile_src(t, j):
        return bass.AP(t.tensor, j * TS, [[T, 128], [T * 128, 8], [1, TS]])

    def tile_dst(t, j):
        return bass.AP(t.tensor, j * TS, [[T, 128], [T * 128, 8], [1, TS]])

    bigAF = bigA.ap().bitcast(F32)

    def kern_quarter_src(qr):
        return bass.AP(kern.tensor, qr * 2 * 128 * D, [[D, 128], [128 * D, 2], [1, D]])

    def unit_srcs(u):
        if u >= 16:
            dc, hh = divmod(u - 16, 2)
            return [(bass.AP(W1T.tensor, 128 * dc * DFF + hh * 2048,
                             [[DFF, 128], [1, 2048]]), 0, 2048)]
        fc = 2 * u
        return [(bass.AP(W2T.tensor, 128 * fc * D, [[D, 128], [1, 1024]]), 0, 1024),
                (bass.AP(W2T.tensor, 128 * (fc + 1) * D, [[D, 128], [1, 1024]]),
                 1024, 1024)]

    def unit_stores(u):
        if u >= 16:
            dc, hh = divmod(u - 16, 2)
            out = []
            for piece in range(2):
                fc0 = hh * 16 + piece * 8
                out.append((bass.AP(W1qS.tensor, fc0 * 131072 + dc * 128,
                                    [[1024, 128], [131072, 8], [1, 128]]),
                            piece * 1024, 1024))
            return out
        fc = 2 * u
        return [(bass.AP(W2qS.tensor, fc * 128,
                         [[4096, 128], [524288, 8], [1, 128]]), 0, 1024),
                (bass.AP(W2qS.tensor, (fc + 1) * 128,
                         [[4096, 128], [524288, 8], [1, 128]]), 1024, 1024)]

    NU = 32
    THRS = [m + 0.5 for m in range(-7, 7)]

    with nc.Block() as blk:
        v0, s0, t0 = Q["v"], Q["s"], Q["t"]
        qg_c = [0]
        qb_mx = [0]
        qb_gm = [0]
        q_mx = [0] * 4
        q_cm = [0] * 4
        q_gm = [0] * NU
        v_qred = [v0 + qr + 1 for qr in range(4)]
        v_comb = v0 + 5
        v_r0 = v0 + 8
        def v_cmp(ci):
            q, m = divmod(ci, 14)
            return v0 + 8 + 18 * q + m + 1
        v_qev = [v0 + 8 + 18 * q + 18 for q in range(4)]
        v_gred = v0 + 8 + 72 + 2
        s_tok = [s0 + 3 + u + 1 for u in range(NU)]
        def t_grp(q, m):
            return t0 + 56 * q + 4 * (m + 1)

        @blk.gpsimd
        def _(g):
            qgD(g, cstF.ap()[:, 0:2608], cblobF_in)
            qgD(g, cstB[:, :], cblobB_in)
            qgD(g, a_sb[:, :],
                bass.AP(a_m.tensor, 0, [[16, 128], [2048, 8], [1, 16]]))
            qgD(g, b_sb[:, :], b_m)
            qgD(g, wd_sb[:, :], bass.AP(WdT.tensor, 0, [[1, 128], [128, 8]]),
                allow_slow_non_contiguous=True)
            qg_c[0] = qgD(g, sc[:, 0:4], scIn_in)
            for u in range(NU):
                if u == 8:
                    g.wait_ge(sV, v_comb)
                    g.tensor_reduce(sc[0:1, C_WMXR:C_WMXR + 1],
                                    sc[:, C_TMP:C_TMP + 1], axis=AX.C, op=ALU.max)
                    g.drain()
                    tb1 = qgD(g, bncS[0:1, 0:1], sc[0:1, C_WMXR:C_WMXR + 1])
                    g.wait_ge(sQg, tb1)
                    qb_mx[0] = qgD(g, sc[:, C_WMXR:C_WMXR + 1],
                                   bass.AP(bncS.tensor, 0, [[0, 128], [1, 1]]))
                if u >= 4:
                    g.wait_ge(sS, s_tok[u - 4])
                slot = (u % 4) * 2048
                qfn, _sh = gam4_q(u)
                tok = None
                for srcu, so, w in unit_srcs(u):
                    tok = qfn(g, bigA[:, slot + so:slot + so + w], srcu)
                q_gm[u] = tok
            g.wait_ge(sV, v_gred)
            g.tensor_reduce(sc[0:1, C_G12B:C_G12B + 2], sc[:, C_G12:C_G12 + 2],
                            axis=AX.C, op=ALU.add)
            g.drain()
            tb2 = qgD(g, bncS[0:1, 2:4], sc[0:1, C_G12B:C_G12B + 2])
            g.wait_ge(sQg, tb2)
            qb_gm[0] = qgD(g, sc[:, C_G12B:C_G12B + 2],
                           bass.AP(bncS.tensor, 2, [[0, 128], [1, 2]]))
            g.wait_ge(sQg, qb_gm[0])

        xfF = xf.ap().bitcast(F32)

        def kq_slot(qr):
            if qr < 2:
                return stmF[:, qr * 2048:qr * 2048 + 2048]
            return xfF[:, (qr - 2) * 2048:(qr - 2) * 2048 + 2048]

        kq_q = [(qsD, sQs), (qaD, sDa), (qbD, sDb), (qcD, sDc)]

        @blk.sync
        def _(s):
            for qr in range(4):
                q_mx[qr] = kq_q[qr][0](s, kq_slot(qr), kern_quarter_src(qr))

        @blk.scalar
        def _(s):
            s.wait_ge(sQg, qg_c[0])
            si(s.activation(sc[:, C_GAM:C_GAM + 1], sc[:, C_DEC:C_DEC + 1], AF.Sigmoid))
            s.drain()
            si(s.activation(sc[:, C_LG:C_LG + 1], sc[:, C_GAM:C_GAM + 1], AF.Ln))
            s.drain()
            si(s.activation(sc[:, C_ALPHA:C_ALPHA + 1], sc[:, C_LA:C_LA + 1], AF.Exp))
            for u in range(NU):
                s.wait_ge(gam4_q(u)[1], q_gm[u])
                slot = (u % 4) * 2048
                s.drain()
                got = si(s.activation(
                    bigA[:, 8192:8192 + 2048], bigA[:, slot:slot + 2048], AF.Abs,
                    accum_out=statsT[:, u:u + 1]))
                assert got == s_tok[u]

        @blk.vector
        def _(v):
            v.memset(sc[:, C_E5:C_E5 + 1], EPS_LN)
            v.memset(sc[:, C_NH:C_NH + 1], -0.5)
            v.memset(sc[:, C_E8:C_E8 + 1], EPS_LD)
            v.memset(sc[:, C_I32:C_I32 + 1], 1.0 / 32.0)
            v.memset(sc[:, C_Q4:C_Q4 + 1], 0.25)
            v.memset(sc[:, C_HALFP:C_HALFP + 1], 0.5)
            v.wait_ge(sQg, qg_c[0])
            v.tensor_scalar_mul(biasL[:, 48:80], biasL[:, 8:40], 0.5)
            v.drain()
            for qr in range(4):
                v.wait_ge(kq_q[qr][1], q_mx[qr])
                got = vi(v.reduce_max(
                    sc[:, C_WMX + qr:C_WMX + qr + 1],
                    kq_slot(qr),
                    axis=AX.X, apply_absolute_value=True))
                assert got == v_qred[qr]
            v.drain()
            got = vi(v.reduce_max(sc[:, C_TMP:C_TMP + 1], sc[:, C_WMX:C_WMX + 4],
                                  axis=AX.X))
            assert got == v_comb
            v.wait_ge(sQg, qb_mx[0])
            vi(v.reciprocal(sc[:, C_TMP:C_TMP + 1], sc[:, C_WMXR:C_WMXR + 1]))
            v.drain()
            vi(v.tensor_scalar_mul(sc[:, C_R0:C_R0 + 1], sc[:, C_TMP:C_TMP + 1], 7.0))
            got = vi(v.tensor_scalar_mul(sc[:, C_S0:C_S0 + 1],
                                         sc[:, C_WMXR:C_WMXR + 1], 1.0 / 7.0))
            assert got == v_r0
            v.wait_ge(sS, s0 + 2)
            v.tensor_scalar_mul(sc[:, C_NLG:C_NLG + 1], sc[:, C_LG:C_LG + 1], -1.0)
            v.drain()
            for qr in range(4):
                for m in range(14):
                    ci = 14 * qr + m
                    if ci >= 4:
                        q2, m2 = divmod(ci - 4, 14)
                        v.wait_ge(sT, t_grp(q2, m2))
                    cslot = (rt[:, (ci % 4 // 2) * 2048:(ci % 4 // 2) * 2048 + 2048]
                             if ci % 2 == 0 else
                             xbf[:, (ci % 4 // 2) * 2048:(ci % 4 // 2) * 2048 + 2048])
                    got = vi(v.tensor_scalar(
                        cslot, kq_slot(qr),
                        sc[:, C_R0:C_R0 + 1], THRS[m], ALU.mult, ALU.is_gt))
                    assert got == v_cmp(ci)
                v.wait_ge(sT, t_grp(qr, 13))
                for h in range(4):
                    got = vi(v.tensor_scalar(
                        wqA[:, qr * 2048 + h * 512: qr * 2048 + (h + 1) * 512],
                        pb[h][:, :], -7.0, sc[:, C_S0:C_S0 + 1], ALU.add, ALU.mult))
                assert got == v_qev[qr]
            v.wait_ge(sS, s_tok[NU - 1])
            vi(v.reduce_sum(sc[:, C_G12:C_G12 + 1], statsT[:, 16:32], axis=AX.X))
            got = vi(v.reduce_sum(sc[:, C_G12 + 1:C_G12 + 2], statsT[:, 0:16],
                                  axis=AX.X))
            assert got == v_gred
            v.wait_ge(sQg, qb_gm[0])
            v.tensor_scalar(sc[:, C_TMP:C_TMP + 2], sc[:, C_G12B:C_G12B + 2],
                            1.0 / (DFF * D), EPS_BN, ALU.mult, ALU.add)
            v.drain()
            vi(v.reciprocal(sc[:, C_R12:C_R12 + 2], sc[:, C_TMP:C_TMP + 2]))

        @blk.tensor
        def _(t):
            for qr in range(4):
                if qr >= 1:
                    t.wait_ge(sV, v_qev[qr - 1])
                for m in range(14):
                    ci = 14 * qr + m
                    t.wait_ge(sV, v_cmp(ci))
                    cslot = (rt[:, (ci % 4 // 2) * 2048:(ci % 4 // 2) * 2048 + 2048]
                             if ci % 2 == 0 else
                             xbf[:, (ci % 4 // 2) * 2048:(ci % 4 // 2) * 2048 + 2048])
                    got = None
                    for h in range(4):
                        got = ti(t.matmul(pb[h][:, :], idP[:, :],
                                          cslot[:, h * 512:(h + 1) * 512],
                                          start=(m == 0), stop=(m == 13)))
                    assert got == t_grp(qr, m)

    def ln_proj(layer):
        shift = 0 if layer == 0 else 1
        src_t = xT if layer == 0 else x1S
        with nc.Block() as blk:
            v0, s0, t0 = Q["v"], Q["s"], Q["t"]
            qg_snap = Q["qg"]
            v_sq = [v0 + 27 * j + 8 for j in range(NTT)]
            v_mcopy = [v0 + 27 * j + 9 for j in range(NTT)]
            v_var = [v0 + 27 * j + 11 for j in range(NTT)]
            v_end = [v0 + 27 * j + 27 for j in range(NTT)]
            q_x = [0] * NTT
            q_b = [0] * NTT
            q_w = [0] * 3

            plan = []
            if layer == 0:
                plan += [("lnm", 0), ("lns", 0), ("lnm", 1), ("lns", 1)]
                plan += [("wg", gi) for gi in range(16)]
                plan += [("lnm", 2), ("lns", 2)]
                plan += [("g", ec, 0) for ec in range(8)]
                plan += [("lnm", 3), ("lns", 3)]
                plan += [("vtm", tc, et) for tc in range(0, 4) for et in range(2)]
                plan += [("g", ec, 1) for ec in range(8)]
                plan += [("vtm", tc, et) for tc in range(4, 8) for et in range(2)]
                plan += [("g", ec, 2) for ec in range(8)]
                plan += [("g", ec, 3) for ec in range(8)]
                plan += [("vtm", tc, et) for tc in range(8, 16) for et in range(2)]
                plan += [("ld", tt) for tt in range(4)]
            else:
                plan += [("lnm", 0), ("lns", 0), ("lnm", 1), ("lns", 1)]
                plan += [("vtm", tc, et) for tc in range(0, 4) for et in range(2)]
                plan += [("lnm", 2), ("lns", 2)]
                plan += [("vtm", tc, et) for tc in range(4, 8) for et in range(2)]
                plan += [("lnm", 3), ("lns", 3)]
                plan += [("vtm", tc, et) for tc in range(8, 16) for et in range(2)]
            t_tok = {}
            for i, e in enumerate(plan):
                t_tok[e] = t0 + 8 * (i + 1)
            t_m = [t_tok[("lnm", j)] for j in range(NTT)]
            t_s2 = [t_tok[("lns", j)] for j in range(NTT)]
            pseq = [e for e in plan if e[0] not in ("lnm", "lns")]
            pidx = {e: i for i, e in enumerate(pseq)}
            s_seq = []
            for e in plan:
                if e[0] == "lns":
                    s_seq.append(("rstd", e[1]))
                elif e[0] != "lnm":
                    s_seq.append(e)
            s_tok = {}
            cnt = s0
            for e in s_seq:
                cnt += 2 if e[0] == "rstd" else 1
                s_tok[e] = cnt
            s_rstd = [s_tok[("rstd", j)] for j in range(NTT)]

            NUQ = 32
            q_ld = [0] * NUQ
            st_tok = [0] * NUQ
            sg_tok = [0] * NUQ
            vc_tok = [0] * NUQ

            def ld_q(u):
                return [(qbD, sDb), (qeD, sDe), (qfD, sDf), (qmD, sDm)][u % 4]

            def ldslot(u):
                return bigAF[:, (u % 4) * 2048:(u % 4) * 2048 + 2048]

            def qslot_b(u):
                return bigA[:, (u % 4) * 4096:(u % 4) * 4096 + 2048]

            def tslot(u, k):
                if u < 16:
                    return wqA[:, ((u % 2) * 2 + k) * 2048:
                               ((u % 2) * 2 + k) * 2048 + 2048]
                p = (u - 16) % 4
                if p < 2:
                    return wqA[:, (p * 2 + k) * 2048:(p * 2 + k) * 2048 + 2048]
                return stm[:, ((p - 2) * 2 + k) * 2048:((p - 2) * 2 + k) * 2048 + 2048]


            def xslot(j):
                return xf[:, :]

            xfB = xf.ap().bitcast(BF16)

            def bslot(j):
                if j == 0:
                    return xbf[:, 0:4096]
                if j == 1:
                    return rt[:, 0:4096]
                return xfB[:, (j - 2) * 4096:(j - 2) * 4096 + 4096]

            def sqslot(j):
                return stm[:, (j % 2) * 4096:(j % 2) * 4096 + 4096]

            def mslot(j):
                return statsT[:, (j % 2) * 512:(j % 2) * 512 + 512]

            def rslot(j):
                return statsB[:, 2048 + (j % 2) * 512:2048 + (j % 2) * 512 + 512]

            maskbB = maskb.ap().bitcast(BF16)

            def nscr(j):
                return maskbB[:, 3072 + (j % 2) * 512:3072 + (j % 2) * 512 + 512]

            def varslot(j):
                return maskb[:, (j % 2) * 512:(j % 2) * 512 + 512]

            def scrslot(j):
                return maskb[:, 1024 + (j % 2) * 512:1024 + (j % 2) * 512 + 512]

            def pj_dst(e):
                if e[0] == "wg":
                    dc, et = divmod(e[1], 2)
                    return bigB[:, 8192 + dc * 1024 + et * 512:
                                8192 + dc * 1024 + (et + 1) * 512]
                if e[0] == "g":
                    ec, tt = e[1], e[2]
                    return bigA[:, ec * 2048 + tt * 512: ec * 2048 + (tt + 1) * 512]
                if e[0] == "vtm":
                    tc, et = e[1], e[2]
                    return bigB[:, tc * 1024 + et * 512: tc * 1024 + (et + 1) * 512]
                return None

            @blk.gpsimd
            def _(g):
                if shift == 1:
                    g.wait_ge(sQg, qg_snap)
                    g.wait_ge(sDi, Q["di"])
                    g.wait_ge(sDj, Q["dj"])
                    g.wait_ge(sDk, Q["dk"])
                    g.wait_ge(sDl, Q["dl"])
                qfn0 = [qiD, qjD]
                q_b[0] = qfn0[0](g, bslot(0), tile_src(src_t, 0))
                q_b[1] = qfn0[1](g, bslot(1), tile_src(src_t, 1))
                if layer == 0:
                    qgD(g, bigB[:, 0:8192], full_mat_src(Wq))
                    q_w[0] = qgD(g, projW[:, :], full_mat_src(Wk))
                else:
                    q_w[0] = qgD(g, bigA[:, 0:8192], full_mat_src(WwrT))
                    q_w[1] = qnD(g, projW[:, :], full_mat_src(WrdT))
                    q_w[2] = qlD(g, nm[:, :], nmB)
                for j in (1, 2, 3):
                    if j < 3:
                        jj = j + 1
                        g.wait_ge((sDi if jj % 2 == 0 else sDj), q_b[jj - 2])
                        q_b[jj] = qfn0[jj % 2](g, bslot(jj), tile_src(src_t, jj))
                if layer == 0:
                    g.wait_ge(sT, t_tok[("wg", 15)])
                    q_w[1] = qgD(g, projW[:, :], full_mat_src(WvT))
                    g.wait_ge(sS, s_tok[("ld", 3)] + 1)
                    tb = qgD(g, ldS[0:1, :], stmF[0:1, 0:2048])
                    g.wait_ge(sQg, tb)
                    tb = qgD(g, ld_cols[:, :],
                             bass.AP(ldS.tensor, 0, [[1, 128], [128, 16]]),
                             allow_slow_non_contiguous=True)
                    g.wait_ge(sQg, tb)
                else:
                    g.wait_ge(sDn, q_w[1])
                    g.wait_ge(sDl, q_w[2])

            @blk.sync
            def _(s):
                return

            @blk.tensor
            def _(t):
                np_ = 0
                for e in plan:
                    if e[0] == "lnm":
                        j = e[1]
                        t.wait_ge((sDi if j % 2 == 0 else sDj), q_b[j])
                        if j >= 2:
                            t.wait_ge(sV, v_mcopy[j - 2])
                        bs = bslot(j)
                        got = None
                        for dc in range(NDC):
                            got = ti(t.matmul(pb[(j % 2) * 2][:, :], inv1024[:, :],
                                              bs[:, dc * 512:(dc + 1) * 512],
                                              start=(dc == 0), stop=(dc == NDC - 1)))
                        assert got == t_m[j]
                        continue
                    if e[0] == "lns":
                        j = e[1]
                        t.wait_ge(sV, v_sq[j])
                        if j >= 2:
                            t.wait_ge(sV, v_var[j - 2])
                        sq = sqslot(j)
                        got = None
                        for dc in range(NDC):
                            got = ti(t.matmul(pb[(j % 2) * 2 + 1][:, :], inv1024[:, :],
                                              sq[:, dc * 512:(dc + 1) * 512],
                                              start=(dc == 0), stop=(dc == NDC - 1)))
                        assert got == t_s2[j]
                        continue
                    p = pidx[e]
                    if p >= 4:
                        t.wait_ge(sS, s_tok[pseq[p - 4]])
                    bank = pb[4 + p % 4]
                    if e[0] == "wg":
                        dc, et = divmod(e[1], 2)
                        t.wait_ge(sQg, q_w[0])
                        got = None
                        for fc in range(NDC):
                            got = ti(t.matmul(
                                bank[:, :],
                                bigB[:, fc * 1024 + dc * 128: fc * 1024 + (dc + 1) * 128],
                                projW[:, fc * 1024 + et * 512: fc * 1024 + (et + 1) * 512],
                                start=(fc == 0), stop=(fc == NDC - 1)))
                    elif e[0] == "g":
                        ec, tt = e[1], e[2]
                        t.wait_ge(sV, v_end[tt])
                        t.wait_ge(sS, s_tok[("wg", 15)])
                        got = None
                        for dc in range(NDC):
                            p0 = xnp(dc, tt * TS, 0)
                            got = ti(t.matmul(
                                bank[:, :],
                                bigB[:, 8192 + dc * 1024 + ec * 128:
                                     8192 + dc * 1024 + (ec + 1) * 128],
                                xn[:, p0:p0 + TS],
                                start=(dc == 0), stop=(dc == NDC - 1)))
                    elif e[0] == "vtm":
                        tc, et = e[1], e[2]
                        t.wait_ge(sV, v_end[tc // 4])
                        if layer == 0:
                            t.wait_ge(sQg, q_w[1])
                            wsrc = projW
                        else:
                            t.wait_ge(sQg, q_w[0])
                            wsrc = bigA
                        got = None
                        for dc in range(NDC):
                            p0 = xnp(dc, tc * 128, shift)
                            got = ti(t.matmul(
                                bank[:, :], xn[:, p0:p0 + 128],
                                wsrc[:, dc * 1024 + et * 512: dc * 1024 + (et + 1) * 512],
                                start=(dc == 0), stop=(dc == NDC - 1)))
                    else:
                        tt = e[1]
                        t.wait_ge(sV, v_end[tt])
                        got = None
                        for dc in range(NDC):
                            p0 = xnp(dc, tt * TS, 0)
                            got = ti(t.matmul(bank[0:1, :], wd_sb[:, dc:dc + 1],
                                              xn[:, p0:p0 + TS],
                                              start=(dc == 0), stop=(dc == NDC - 1)))
                    assert got == t_tok[e]

            @blk.vector
            def _(v):
                if shift == 1:
                    for dc in range(NDC):
                        v.memset(xn[:, dc * CHS:dc * CHS + 1], 0.0)
                for j in range(NTT):
                    if j >= 2:
                        v.wait_ge(sT, t_s2[j - 2])
                    v.wait_ge((sDi if j % 2 == 0 else sDj), q_b[j])
                    bs, sq = bslot(j), sqslot(j)
                    got = None
                    for dc in range(NDC):
                        got = vi(v.tensor_mul(sq[:, dc * 512:(dc + 1) * 512],
                                              bs[:, dc * 512:(dc + 1) * 512],
                                              bs[:, dc * 512:(dc + 1) * 512]))
                    assert got == v_sq[j], ('vsq', j, got, v_sq)
                    v.wait_ge(sT, t_m[j])
                    got = vi(v.tensor_copy(mslot(j), pb[(j % 2) * 2][:, :]))
                    assert got == v_mcopy[j]
                    v.drain()
                    vi(v.tensor_mul(scrslot(j), mslot(j), mslot(j)))
                    v.drain()
                    v.wait_ge(sT, t_s2[j])
                    got = vi(v.tensor_sub(varslot(j), pb[(j % 2) * 2 + 1][:, :],
                                          scrslot(j)))
                    assert got == v_var[j]
                    v.wait_ge(sS, s_rstd[j])
                    bs = bslot(j)
                    for dc in range(NDC):
                        v.drain()
                        vi(v.tensor_sub(nscr(j), bs[:, dc * 512:(dc + 1) * 512],
                                        mslot(j)))
                        v.drain()
                        p0 = xnp(dc, j * TS, shift)
                        got = vi(v.tensor_mul(xn[:, p0:p0 + TS], nscr(j), rslot(j)))
                    assert got == v_end[j]

            @blk.scalar
            def _(s):
                for e in s_seq:
                    if e[0] == "rstd":
                        j = e[1]
                        s.wait_ge(sV, v_var[j])
                        s.drain()
                        si(s.activation(scrslot(j), varslot(j), AF.Ln,
                                        bias=sc[:, C_E5:C_E5 + 1]))
                        s.drain()
                        got = si(s.activation(rslot(j), scrslot(j), AF.Exp,
                                              scale=sc[:, C_NH:C_NH + 1]))
                        assert got == s_rstd[j]
                        continue
                    p = pidx[e]
                    bank = pb[4 + p % 4]
                    s.wait_ge(sT, t_tok[e])
                    s.drain()
                    if e[0] == "wg":
                        got = si(s.activation(pj_dst(e), bank[:, :], AF.Copy,
                                              scale=sc[:, C_I32:C_I32 + 1]))
                    elif e[0] in ("g", "vtm"):
                        got = si(s.copy(pj_dst(e), bank[:, :]))
                    else:
                        tt = e[1]
                        got = si(s.activation(statsT[0:1, tt * 512:(tt + 1) * 512],
                                              bank[0:1, :], AF.Sigmoid,
                                              bias=sc[0:1, C_BD:C_BD + 1]))
                    assert got == s_tok[e]
                if layer == 0:
                    s.drain()
                    si(s.activation(stmF[0:1, 0:2048], statsT[0:1, 0:2048],
                                    AF.Ln, bias=sc[0:1, C_E8:C_E8 + 1]))


    def PROJ0():
        ln_proj(0)

    def PROJ1():
        ln_proj(1)
        with nc.Block() as blk:
            @blk.gpsimd
            def _(g):
                for semh, key in ((sDb, "db"), (sDe, "de"), (sDf, "df"),
                                  (sDm, "dm"), (sDc, "dc_"), (sDd, "dd"),
                                  (sDg, "dg"), (sDh, "dh")):
                    g.wait_ge(semh, Q[key])

    PROJ0()

    QCARRY = {}

    def attn_tile(layer, j, blk, prev):
        shift = 0 if layer == 0 else 1
        if layer == 0:
            schunks = list(range(4 * j, NSC))

            def strad(i):
                return i < 4 * j + 4

            def cc(i):
                return float(128 * i - 512 * j - 1)
        else:
            schunks = list(range(0, 4 * j + 4))

            def strad(i):
                return i >= 4 * j

            def cc(i):
                return float(512 * j - 128 * i - 1)
        NS = len(schunks)

        v0, s0, t0 = Q["v"], Q["s"], Q["t"]
        prefA = []
        tot = 0
        for i in schunks:
            tot += 1 + (1 if strad(i) else 0)
            prefA.append(v0 + tot)
        totalA = tot
        if layer == 0:
            QUC = [10, 9, 7, 6]
            u0 = sum(QUC[:j])
            units_j = list(range(u0, u0 + QUC[j]))
            NH = 2 * len(units_j)
            NSL = [2, 3, 4, 4][j]
            anchors = ([("st", idx) for idx in range(NS)] + [("xa", 0)]
                       + [("R", ec) for ec in range(8)]
                       + [("res", dc) for dc in range(8)])
            NA = len(anchors)
            vplan = []
            qdone = 0
            for a, anc in enumerate(anchors):
                vplan.append(anc)
                want = min(NH, (NH * (a - 3)) // (NA - 7)) if a >= 4 else 0
                while qdone < want:
                    vplan.append(("q", qdone))
                    qdone += 1
            while qdone < NH:
                vplan.append(("q", qdone))
                qdone += 1
            vcnt = v0 + totalA
            v_mult = [0] * NS
            v_R = [0] * 8
            v_res = [0] * 8
            cmb_tok = [0] * NH
            v_xa = 0
            for e in vplan:
                vcnt += 1
                if e[0] == "q":
                    cmb_tok[e[1]] = vcnt
                elif e[0] == "st":
                    v_mult[e[1]] = vcnt
                elif e[0] == "xa":
                    v_xa = vcnt
                elif e[0] == "R":
                    v_R[e[1]] = vcnt
                else:
                    v_res[e[1]] = vcnt
            splan = []
            for e in vplan:
                if e[0] == "st":
                    splan.append(("m", e[1]))
                elif e[0] == "q":
                    splan.append(("g", e[1]))
            scnt = s0
            s_mask = [0] * NS
            sg_tok = [0] * NH
            for e in splan:
                if e[0] == "m":
                    scnt += 1
                    s_mask[e[1]] = scnt
                else:
                    scnt += 2
                    sg_tok[e[1]] = scnt
            t_xa = t0 + 8
            t_st = [t_xa + 8 * (idx + 1) for idx in range(NS)]
            t_R = [t_st[NS - 1] + NS * (ec + 1) for ec in range(8)]
            t_res = [t_R[7] + 17 * (dc + 1) for dc in range(8)]
            xbfF = xbf.ap().bitcast(F32)
            stmF2 = stm.ap().bitcast(F32)

            def qldF(h):
                s_ = h % NSL
                if s_ < 2:
                    return xbfF[:, s_ * 1024:s_ * 1024 + 1024]
                return stmF2[:, (s_ - 2) * 1024:(s_ - 2) * 1024 + 1024]

            def qout(h):
                s_ = h % NSL
                if s_ < 2:
                    return xbf[:, s_ * 2048:s_ * 2048 + 1024]
                return stm[:, (s_ - 2) * 2048:(s_ - 2) * 2048 + 1024]

            def qt1c(h):
                return statsB[:, (h % 2) * 2048:(h % 2) * 2048 + 1024]

            def qt2c(h):
                return statsB[:, (h % 2) * 2048 + 1024:(h % 2) * 2048 + 2048]

            def q_src(h):
                u = units_j[h // 2]
                half = h % 2
                if u >= 16:
                    dcq, hh = divmod(u - 16, 2)
                    return bass.AP(W1T.tensor,
                                   128 * dcq * DFF + hh * 2048 + half * 1024,
                                   [[DFF, 128], [1, 1024]])
                fc = 2 * u + half
                return bass.AP(W2T.tensor, 128 * fc * D, [[D, 128], [1, 1024]])

            def q_dst(h):
                u = units_j[h // 2]
                half = h % 2
                if u >= 16:
                    dcq, hh = divmod(u - 16, 2)
                    fc0 = hh * 16 + half * 8
                    return bass.AP(W1qS.tensor, fc0 * 131072 + dcq * 128,
                                   [[1024, 128], [131072, 8], [1, 128]])
                fc = 2 * u + half
                return bass.AP(W2qS.tensor, fc * 128,
                               [[4096, 128], [524288, 8], [1, 128]])

            def qld_q(h):
                return [(qbD, sDb), (qeD, sDe), (qfD, sDf), (qmD, sDm)][h % 4]

            def qst_q(h):
                return [(qiD, sDi), (qjD, sDj), (qkD, sDk), (qlD, sDl)][h % 4]

            qst_keys = ["di", "dj", "dk", "dl"]
            exp_qst = [Q[qst_keys[h % 4]] + 16 * (h // 4 + 1) for h in range(NH)]
        else:
            v_mult = [v0 + totalA + idx + 1 for idx in range(NS)]
            vp = v0 + totalA + NS
            v_R = [vp + ec + 1 for ec in range(8)]
            v_res = [vp + 8 + dc + 1 for dc in range(8)]
            t_st = [t0 + 8 * (idx + 1) for idx in range(NS)]
            t_R = [t_st[NS - 1] + NS * (ec + 1) for ec in range(8)]
            t_h = [t_R[7] + 8 * (fc + 1) for fc in range(NFC)]
            t_res = [t_h[NFC - 1] + 40 * (dc + 1) for dc in range(8)]
        if layer == 1:
            s_mask = [s0 + idx + 1 for idx in range(NS)]
            s_h = [s0 + NS + fc + 1 for fc in range(NFC)]

        q_x = [0]
        q_w1 = [0] * NFC
        q_w2 = [0] * 8
        q_qld = [0] * 64
        q_qst = [0] * 64
        if layer == 1:
            w2q_q = [(qiD, sDi), (qjD, sDj)]
            exp_w2 = [Q[["di", "dj"][dc % 2]] + 16 * (dc // 2 + 1)
                      for dc in range(8)]

        @blk.sync
        def _(s):
            s.wait_ge(sQg, Q["qg"])
            src = xT if layer == 0 else x1S
            q_x[0] = qsD(s, xf[:, :], tile_src(src, j))
            if layer == 0:
                if j > 0:
                    for kq, tokq in QCARRY["fin"]:
                        s.wait_ge(kq, tokq)
                stm_gd = [False]

                def qld_issue(h):
                    if prev and not stm_gd[0] and (h % NSL) >= 2:
                        s.wait_ge(sT, prev["t_R"][7])
                        stm_gd[0] = True
                    q_qld[h] = qld_q(h)[0](s, qldF(h), q_src(h))

                for h in range(min(NSL, NH)):
                    qld_issue(h)
                for h in range(NSL, NH):
                    s.wait_ge(qst_q(h - NSL)[1], exp_qst[h - NSL])
                    qld_issue(h)
                fin = {}
                for h in range(NH):
                    fin[qst_q(h)[1].name] = (qst_q(h)[1], exp_qst[h])
                QCARRY["fin"] = list(fin.values())
            if layer == 1:
                if j == 0:
                    s.wait_ge(sDi, Q["di"])
                    s.wait_ge(sDj, Q["dj"])
                    s.wait_ge(sDk, Q["dk"])
                    s.wait_ge(sDl, Q["dl"])
                for fc in range(NFC):
                    if fc >= 4:
                        s.wait_ge(sT, t_h[fc - 4])
                    elif prev:
                        s.wait_ge(sT, prev["t_h"][28 + fc])
                    qfn, _sh = slab4_q(fc)
                    q_w1[fc] = qfn(s,
                        xbf[:, (fc % 4) * 1024:(fc % 4) * 1024 + 1024],
                        bass.AP(W1qS.tensor, fc * 128 * 1024,
                                [[1024, 128], [1, 1024]]))

        @blk.vector
        def _(v):
            if prev:
                v.wait_ge(sS, prev["s_last"] if layer == 0
                          else prev["s_mask"][-1])
                v.wait_ge(sT, prev["t_R"][7])
            for idx, i in enumerate(schunks):
                scl = ld_cols[:, i:i + 1] if layer == 0 else sc[:, C_LG:C_LG + 1]
                vi(v.tensor_scalar_mul(biascols[:, i:i + 1], scl, cc(i)))
                if strad(i):
                    k = i - 4 * j
                    escl = (ld_cols[:, i:i + 1] if layer == 0
                            else sc[:, C_NLG:C_NLG + 1])
                    vi(v.scalar_tensor_tensor(
                        statsB[:, k * 512:(k + 1) * 512], Bcs[:, :], escl,
                        nm[:, k * 512:(k + 1) * 512], ALU.mult, ALU.add))
            assert Q["v"] == v0 + totalA
            if layer == 0:
                for e in vplan:
                    if e[0] == "q":
                        h = e[1]
                        v.wait_ge(sS, sg_tok[h])
                        got = vi(v.tensor_add(qout(h), qt1c(h), qt2c(h)))
                        assert got == cmb_tok[h]
                    elif e[0] == "st":
                        idx = e[1]
                        i = schunks[idx]
                        v.wait_ge(sT, t_st[idx])
                        v.wait_ge(sS, s_mask[idx])
                        got = vi(v.tensor_mul(
                            stm[:, i * 512:(i + 1) * 512], pb[idx % 4][:, :],
                            maskb[:, (idx % 4) * 512:(idx % 4) * 512 + 512]))
                        assert got == v_mult[idx]
                    elif e[0] == "xa":
                        v.wait_ge(sT, t_xa)
                        got = vi(v.tensor_copy(xa_bf[:, :], pb[4][0:16, :]))
                        assert got == v_xa
                    elif e[0] == "R":
                        ec = e[1]
                        v.wait_ge(sT, t_R[ec])
                        got = vi(v.tensor_scalar_mul(
                            rt[:, ec * 512:(ec + 1) * 512],
                            pb[4 + ec % 4][:, :], sc[:, C_OS:C_OS + 1]))
                        assert got == v_R[ec]
                    else:
                        dc = e[1]
                        v.wait_ge(sT, t_res[dc])
                        v.wait_ge(sQs, q_x[0])
                        got = vi(v.scalar_tensor_tensor(
                            xf[:, dc * 512:(dc + 1) * 512], pb[dc % 2][:, :],
                            biasL[:, dc:dc + 1],
                            xf[:, dc * 512:(dc + 1) * 512], ALU.add, ALU.add))
                        assert got == v_res[dc]
            else:
                for idx, i in enumerate(schunks):
                    v.wait_ge(sT, t_st[idx])
                    v.wait_ge(sS, s_mask[idx])
                    got = vi(v.tensor_mul(
                        stm[:, i * 512:(i + 1) * 512], pb[idx % 4][:, :],
                        maskb[:, (idx % 4) * 512:(idx % 4) * 512 + 512]))
                    assert got == v_mult[idx]
                for ec in range(8):
                    v.wait_ge(sT, t_R[ec])
                    got = vi(v.tensor_scalar_mul(
                        rt[:, ec * 512:(ec + 1) * 512], pb[4 + ec % 4][:, :],
                        sc[:, C_ALPHA:C_ALPHA + 1]))
                    assert got == v_R[ec]
                for dc in range(8):
                    v.wait_ge(sT, t_res[dc])
                    v.wait_ge(sQs, q_x[0])
                    got = vi(v.scalar_tensor_tensor(
                        xf[:, dc * 512:(dc + 1) * 512], pb[2 + dc % 2][:, :],
                        biasL[:, 40 + dc:41 + dc],
                        xf[:, dc * 512:(dc + 1) * 512], ALU.add, ALU.add))
                    assert got == v_res[dc]

        def emit_mask(s, idx):
            i = schunks[idx]
            s.wait_ge(sV, prefA[idx])
            if idx >= 4:
                s.wait_ge(sV, v_mult[idx - 4])
            elif prev:
                s.wait_ge(sV, prev["v_mult"][prev["NS"] - 4 + idx])
            mslot = maskb[:, (idx % 4) * 512:(idx % 4) * 512 + 512]
            if strad(i):
                k = i - 4 * j
                got = si(s.activation(mslot, statsB[:, k * 512:(k + 1) * 512],
                                      AF.Exp, bias=biascols[:, i:i + 1]))
            else:
                scl = (ld_cols[:, i:i + 1] if layer == 0
                       else sc[:, C_NLG:C_NLG + 1])
                got = si(s.activation(mslot, Bcs[:, :], AF.Exp,
                                      bias=biascols[:, i:i + 1], scale=scl))
            assert got == s_mask[idx]

        @blk.scalar
        def _(s):
            if layer == 0:
                for e in splan:
                    if e[0] == "m":
                        emit_mask(s, e[1])
                        continue
                    h = e[1]
                    s.wait_ge(qld_q(h)[1], q_qld[h])
                    if h >= 2:
                        s.wait_ge(sV, cmb_tok[h - 2])
                    elif prev and prev["NH"]:
                        s.wait_ge(sV, prev["cmb"][prev["NH"] - 2 + h])
                    u = units_j[h // 2]
                    rcol = (sc[:, C_R12:C_R12 + 1] if u >= 16
                            else sc[:, C_R12 + 1:C_R12 + 2])
                    s.drain()
                    si(s.activation(qt1c(h), qldF(h), AF.Sign,
                                    scale=rcol, bias=sc[:, C_NH:C_NH + 1]))
                    got = si(s.activation(qt2c(h), qldF(h), AF.Sign,
                                          scale=rcol,
                                          bias=sc[:, C_HALFP:C_HALFP + 1]))
                    assert got == sg_tok[h]
            else:
                for idx in range(NS):
                    emit_mask(s, idx)
            if layer == 1:
                for fc in range(NFC):
                    s.wait_ge(sT, t_h[fc])
                    got = si(s.activation(
                        bigA[:, fc * 512:(fc + 1) * 512], pb[fc % 2][:, :],
                        AF.Relu, bias=biasL[:, 48 + fc:49 + fc],
                        scale=sc[:, C_Q4:C_Q4 + 1]))
                    assert got == s_h[fc]

        @blk.tensor
        def _(t):
            if layer == 0:
                if prev:
                    t.wait_ge(sV, prev["v_R"][4])
                got = None
                for dc in range(NDC):
                    p0 = xnp(dc, j * TS, 0)
                    got = ti(t.matmul(pb[4][0:16, :], a_sb[:, dc * 16:(dc + 1) * 16],
                                      xn[:, p0:p0 + TS],
                                      start=(dc == 0), stop=(dc == NDC - 1)))
                assert got == t_xa
            for idx, i in enumerate(schunks):
                if idx >= 4:
                    t.wait_ge(sV, v_mult[idx - 4])
                elif prev:
                    if layer == 0:
                        pw = [("v", prev["v_res"][6]), ("v", prev["v_res"][7]),
                              ("v", prev["v_mult"][prev["NS"] - 2]),
                              ("v", prev["v_mult"][prev["NS"] - 1])][idx]
                    else:
                        pw = [("s", prev["s_h"][30]), ("s", prev["s_h"][31]),
                              ("v", prev["v_res"][6]),
                              ("v", prev["v_res"][7])][idx]
                    t.wait_ge(sV if pw[0] == "v" else sS, pw[1])
                bank = pb[idx % 4]
                got = None
                for ec in range(NDC):
                    if layer == 0:
                        p0 = xnp(ec, i * 128, 0)
                        lhs = xn[:, p0:p0 + 128]
                        rhs = bigA[:, ec * 2048 + j * 512: ec * 2048 + j * 512 + 512]
                    else:
                        lhs = xn[:, ec * CHS + i * 128: ec * CHS + i * 128 + 128]
                        rhs = xn[:, ec * CHS + 1 + j * 512: ec * CHS + 1 + j * 512 + 512]
                    got = ti(t.matmul(bank[:, :], lhs, rhs,
                                      start=(ec == 0), stop=(ec == NDC - 1)))
                assert got == t_st[idx]
            for ec in range(8):
                if ec == 0:
                    t.wait_ge(sV, v_mult[NS - 1])
                    if layer == 0:
                        t.wait_ge(sV, v_xa)
                if ec >= 4:
                    t.wait_ge(sV, v_R[ec - 4])
                elif prev:
                    t.wait_ge(sV, prev["v_R"][ec + 4])
                bank = pb[4 + ec % 4]
                got = None
                for n, i in enumerate(schunks):
                    got = ti(t.matmul(
                        bank[:, :],
                        bigB[:, i * 1024 + ec * 128: i * 1024 + (ec + 1) * 128],
                        stm[:, i * 512:(i + 1) * 512],
                        start=(n == 0), stop=(n == NS - 1)))
                assert got == t_R[ec]
            if layer == 1:
                for fc in range(NFC):
                    t.wait_ge(slab4_q(fc)[1], q_w1[fc])
                    if fc == 0:
                        t.wait_ge(sV, v_mult[NS - 1])
                    if fc >= 2:
                        t.wait_ge(sS, s_h[fc - 2])
                    bank = pb[fc % 2]
                    got = None
                    for dc in range(NDC):
                        p0 = xnp(dc, j * TS, 1)
                        got = ti(t.matmul(
                            bank[:, :],
                            xbf[:, (fc % 4) * 1024 + dc * 128:
                                 (fc % 4) * 1024 + (dc + 1) * 128],
                            xn[:, p0:p0 + TS],
                            start=(dc == 0), stop=(dc == NDC - 1)))
                    assert got == t_h[fc]
            rb = 0 if layer == 0 else 2
            for dc in range(8):
                if dc == 0:
                    t.wait_ge(sV, v_mult[NS - 1])
                    t.wait_ge(sV, v_R[7])
                    if layer == 1:
                        t.wait_ge(sS, s_h[NFC - 1])
                if dc >= 2:
                    t.wait_ge(sV, v_res[dc - 2])
                if layer == 1:
                    t.wait_ge(w2q_q[dc % 2][1], exp_w2[dc])
                bank = pb[rb + dc % 2]
                first = True
                got = None
                for ec in range(8):
                    got = ti(t.matmul(
                        bank[:, :],
                        projW[:, ec * 1024 + dc * 128: ec * 1024 + (dc + 1) * 128],
                        rt[:, ec * 512:(ec + 1) * 512],
                        start=first, stop=False))
                    first = False
                if layer == 0:
                    for d2 in range(8):
                        p0 = xnp(d2, j * TS, 0)
                        got = ti(t.matmul(
                            bank[:, :],
                            wqA[:, d2 * 1024 + dc * 128: d2 * 1024 + (dc + 1) * 128],
                            xn[:, p0:p0 + TS], start=False, stop=False))
                    got = ti(t.matmul(bank[:, :],
                                      b_sb[0:16, dc * 128:(dc + 1) * 128],
                                      xa_bf[0:16, :], start=False, stop=True))
                else:
                    for fc in range(NFC):
                        got = ti(t.matmul(
                            bank[:, :],
                            wqA[:, (dc % 2) * 4096 + fc * 128:
                                 (dc % 2) * 4096 + (fc + 1) * 128],
                            bigA[:, fc * 512:(fc + 1) * 512],
                            start=False, stop=(fc == NFC - 1)))
                assert got == t_res[dc]

        @blk.gpsimd
        def _(g):
            if layer == 1:
                for dc in range(8):
                    if dc >= 2:
                        g.wait_ge(sT, t_res[dc - 2])
                    elif prev:
                        g.wait_ge(sT, prev["t_res"][6 + dc])
                    got = w2q_q[dc % 2][0](g,
                        wqA[:, (dc % 2) * 4096:(dc % 2) * 4096 + 4096],
                        bass.AP(W2qS.tensor, dc * 128 * 4096,
                                [[4096, 128], [1, 4096]]))
                    assert got == exp_w2[dc], (dc, got, exp_w2[dc])
            if layer == 0:
                for h in range(NH):
                    if h >= 4:
                        g.wait_ge(qst_q(h - 4)[1], q_qst[h - 4])
                    g.wait_ge(sV, cmb_tok[h])
                    q_qst[h] = qst_q(h)[0](g, q_dst(h), qout(h))
                    assert q_qst[h] == exp_qst[h], (h, q_qst[h], exp_qst[h])
            g.wait_ge(sV, v_res[7])
            dst = x1S if layer == 0 else outT
            qgD(g, tile_dst(dst, j), xf[:, :])

        return dict(v_R=v_R, v_res=v_res, v_mult=v_mult,
                    v_xa=(v_xa if layer == 0 else 0),
                    s_mask=s_mask, s_h=(s_h if layer == 1 else None),
                    s_last=(scnt if layer == 0 else s_h[NFC - 1]),
                    cmb=(cmb_tok if layer == 0 else None),
                    NH=(NH if layer == 0 else 0),
                    t_R=t_R, t_res=t_res,
                    t_h=(t_h if layer == 1 else None), NS=NS)

    for jp in ((0, 1), (2, 3)):
        with nc.Block() as blk:
            prevt = None
            for j in jp:
                prevt = attn_tile(0, j, blk, prevt)

    PROJ1()

    for jp in ((0, 1), (2, 3)):
        with nc.Block() as blk:
            prevt = None
            for j in jp:
                prevt = attn_tile(1, j, blk, prevt)

    with nc.Block() as blk:
        @blk.gpsimd
        def _(g):
            g.wait_ge(sQg, Q["qg"])
            g.wait_ge(sQs, Q["qs"])

    for cm in reversed(ctx):
        cm.__exit__(None, None, None)
    return nc


def _host_consts():
    p = np.arange(128, dtype=np.float32)[:, None]
    jj = np.arange(512, dtype=np.float32)[None, :]
    Bc = (p - jj).astype(np.float32)
    nmA = np.empty((128, 2048), dtype=np.float32)
    nmB = np.empty((128, 2048), dtype=np.float32)
    for k in range(4):
        vA = np.where(p - jj + 128 * k - 1 >= 0, 0.0, NEG).astype(np.float32)
        vB = np.where(jj - p - 128 * k - 1 >= 0, 0.0, NEG).astype(np.float32)
        nmA[:, k * 512:(k + 1) * 512] = vA
        nmB[:, k * 512:(k + 1) * 512] = vB
    idP = np.eye(128, dtype=np.float32).astype(ml_dtypes.bfloat16)
    idN = (-np.eye(128, dtype=np.float32)).astype(ml_dtypes.bfloat16)
    inv1024 = np.full((128, 128), 1.0 / 1024.0, dtype=np.float32).astype(ml_dtypes.bfloat16)
    return Bc, nmA, nmB, idP, idN, inv1024


def kernel(L0_attn_x, L0_attn_Wq, L0_attn_Wk, L0_attn_Wv, L0_attn_Wo,
           L0_attn_Wd, L0_attn_bd, L0_attn_out_scale, L0_ffn_kernel,
           L0_ffn_bias, L0_ffn_a, L0_ffn_b, L1_attn_W_write, L1_attn_W_read,
           L1_attn_decay, L1_attn_log_alpha, L1_ffn_W1, L1_ffn_b1, L1_ffn_W2,
           L1_ffn_b2):
    f = np.float32

    def c(x):
        return np.ascontiguousarray(np.asarray(x, dtype=f))

    x = c(L0_attn_x)
    Bc, nmA, nmB, idP, idN, inv1024 = _host_consts()

    def rep(v):
        return np.full((128, 1), float(np.asarray(v, dtype=f).reshape(-1)[0]), dtype=f)

    cblobF = np.concatenate(
        [Bc, nmA,
         c(np.asarray(L0_ffn_bias).reshape(8, 128).T),
         c(np.asarray(L1_ffn_b1).reshape(32, 128).T),
         c(np.asarray(L1_ffn_b2).reshape(8, 128).T)], axis=1)
    scIn = np.concatenate(
        [rep(L0_attn_bd), rep(L0_attn_out_scale),
         rep(L1_attn_decay), rep(L1_attn_log_alpha)], axis=1)
    cblobB = np.concatenate([idP, idN, inv1024], axis=1)
    shared = {
        "Wq": c(L0_attn_Wq),
        "Wk": c(L0_attn_Wk),
        "WvT": c(np.asarray(L0_attn_Wv).T),
        "WdT": c(np.asarray(L0_attn_Wd).reshape(1, D).T),
        "WoT": c(np.asarray(L0_attn_Wo).T),
        "kern": c(L0_ffn_kernel),
        "a_m": c(L0_ffn_a),
        "b_m": c(L0_ffn_b),
        "WwrT": c(np.asarray(L1_attn_W_write).T),
        "WrdT": c(np.asarray(L1_attn_W_read).T),
        "W1T": c(np.asarray(L1_ffn_W1).T),
        "W2T": c(np.asarray(L1_ffn_W2).T),
        "cblobF": cblobF,
        "scIn": scIn,
        "nmB": nmB,
        "cblobB": cblobB,
    }

    n = x.shape[0]
    in_maps = []
    for ci in range(n):
        m = dict(shared)
        m["xT"] = np.ascontiguousarray(x[ci].T)
        in_maps.append(m)

    nc = build_nc()
    res = run_bass_kernel_spmd(nc, in_maps, core_ids=list(range(n)))
    out = np.stack([np.ascontiguousarray(np.asarray(r["outT"]).T)
                    for r in res.results], axis=0)
    return out.astype(np.float32)


# version 25
# speedup vs baseline: 1.0370x; 1.0370x over previous
import numpy as np
import ml_dtypes
import concourse.bass as bass
import concourse.mybir as mybir
from concourse.bass_utils import run_bass_kernel_spmd


F32 = mybir.dt.float32
BF16 = mybir.dt.bfloat16
AF = mybir.ActivationFunctionType
ALU = mybir.AluOpType
AX = mybir.AxisListType

D = 1024
T = 2048
NDC = 8
NTT = 4
TS = 512
NSC = 16
DFF = 4096
NFC = 32
CHS = 2064
NEG = -1.0e30
EPS_LN = 1.0e-5
EPS_BN = 1.0e-5
EPS_LD = 1.0e-8

C_BD, C_OS, C_DEC, C_LA, C_GAM, C_LG, C_NLG, C_ALPHA = 0, 1, 2, 3, 4, 5, 6, 7
C_G12, C_G12B = 8, 10
C_R12 = 12
C_WMX, C_WMXR = 14, 18
C_R0, C_S0 = 19, 20
C_TMP = 21
C_E5, C_NH, C_E8, C_I32, C_Q4, C_HALFP = 22, 23, 24, 25, 26, 27


def build_nc():
    nc = bass.Bass("TRN2", target_bir_lowering=False)
    Q = {"qg": 0, "qs": 0, "da": 0, "db": 0, "dc_": 0, "dd": 0,
         "de": 0, "df": 0, "dg": 0, "dh": 0, "di": 0, "dj": 0,
         "dk": 0, "dl": 0, "dm": 0, "dn": 0, "t": 0, "v": 0, "s": 0}

    def din(name, shape, dt=F32):
        return nc.dram_tensor(name, shape, dt, kind="ExternalInput").ap()

    xT = din("xT", [D, T])
    Wq = din("Wq", [D, D])
    Wk = din("Wk", [D, D])
    WvT = din("WvT", [D, D])
    WdT = din("WdT", [D, 1])
    WoT = din("WoT", [D, D])
    kern = din("kern", [D, D])
    a_m = din("a_m", [D, 16])
    b_m = din("b_m", [16, D])
    WwrT = din("WwrT", [D, D])
    WrdT = din("WrdT", [D, D])
    W1T = din("W1T", [D, DFF])
    W2T = din("W2T", [DFF, D])
    cblobF_in = din("cblobF", [128, 2608])
    scIn_in = din("scIn", [128, 4])
    nmB = din("nmB", [128, 2048])
    cblobB_in = din("cblobB", [128, 384], BF16)

    outT = nc.dram_tensor("outT", [D, T], F32, kind="ExternalOutput").ap()
    x1S = nc.dram_tensor("x1S", [D, T], F32, kind="Internal").ap()
    W1qS = nc.dram_tensor("W1qS", [NFC, 128, 1024], BF16, kind="Internal").ap()
    W2qS = nc.dram_tensor("W2qS", [NDC, 128, 4096], BF16, kind="Internal").ap()
    ldS = nc.dram_tensor("ldS", [1, T], F32, kind="Internal").ap()
    bncS = nc.dram_tensor("bncS", [1, 16], F32, kind="Internal").ap()

    ctx = []

    def sb(name, shape, dt):
        cm = nc.sbuf_tensor(name, shape, dt)
        h = cm.__enter__()
        ctx.append(cm)
        return h

    def psum(name, shape):
        cm = nc.psum_tensor(name, shape, F32)
        h = cm.__enter__()
        ctx.append(cm)
        return h

    def sem(name):
        cm = nc.semaphore(name)
        h = cm.__enter__()
        ctx.append(cm)
        return h

    xf = sb("xf", [128, 4096], F32)
    xbf = sb("xbf", [128, 4096], BF16)
    xn = sb("xn", [128, NDC * CHS], BF16)
    bigA = sb("bigA", [128, 16384], BF16)
    bigB = sb("bigB", [128, 16384], BF16)
    stm = sb("stm", [128, 8192], BF16)
    rt = sb("rt", [128, 4096], BF16)
    wqA = sb("wqA", [128, 8192], BF16)
    projW = sb("projW", [128, 8192], BF16)
    cstF = sb("cstF", [128, 2672], F32)
    Bcs = cstF.ap()[:, 0:512]
    nm = cstF.ap()[:, 512:2560]
    biasL = cstF.ap()[:, 2560:2640]
    sc = cstF.ap()[:, 2640:2672]
    cstB = sb("cstB", [128, 384], BF16)
    idP = cstB.ap()[:, 0:128]
    idN = cstB.ap()[:, 128:256]
    inv1024 = cstB.ap()[:, 256:384]
    a_sb = sb("a_sb", [128, 128], BF16)
    b_sb = sb("b_sb", [16, 1024], BF16)
    wd_sb = sb("wd_sb", [128, 8], BF16)
    ld_cols = sb("ld_cols", [128, 16], F32)
    biascols = sb("biascols", [128, 16], F32)
    statsT = sb("statsT", [128, 2048], F32)
    maskb = sb("maskb", [128, 2048], F32)
    xa_bf = sb("xa_bf", [16, 512], BF16)

    statsB = statsT.ap().bitcast(BF16)
    rtF = rt.ap().bitcast(F32)
    stmF = stm.ap().bitcast(F32)

    pb = [psum(f"pb{k}", [128, 512]) for k in range(8)]

    sQg = sem("sQg")
    sQs = sem("sQs")
    sDa = sem("sDa")
    sDb = sem("sDb")
    sDc = sem("sDc")
    sDd = sem("sDd")
    sDe = sem("sDe")
    sDf = sem("sDf")
    sDg = sem("sDg")
    sDh = sem("sDh")
    sDi = sem("sDi")
    sDj = sem("sDj")
    sDk = sem("sDk")
    sDl = sem("sDl")
    sDm = sem("sDm")
    sDn = sem("sDn")
    sT = sem("sT")
    sV = sem("sV")
    sS = sem("sS")
    sG = sem("sG")

    def _dma_track(key, semh, n0, inst):
        inst.then_inc(semh, 16)
        add = 0
        for nmx in nc.inst_names()[n0:]:
            ins2 = nc.find_inst_by_name(nmx)
            si2 = getattr(ins2, "sync_info", None)
            if si2 is None:
                continue
            for u in si2.on_update:
                if u.ant_name == semh.name:
                    add += u.update_value
        assert add >= 16 and add % 16 == 0, (key, add)
        Q[key] += add
        return Q[key]

    class _QIssue:
        def __init__(self, key, semh):
            self.key, self.semh = key, semh

        def __call__(self, eng, dst, src, **kw):
            n0 = nc.n_instructions()
            inst = eng.dma_start(dst, src, **kw)
            return _dma_track(self.key, self.semh, n0, inst)

    qgD = _QIssue("qg", sQg)
    qsD = _QIssue("qs", sQs)
    qaD = _QIssue("da", sDa)
    qbD = _QIssue("db", sDb)
    qcD = _QIssue("dc_", sDc)
    qdD = _QIssue("dd", sDd)

    def slab_q(u):
        return (qaD, sDa) if u % 2 == 0 else (qbD, sDb)

    def store_q(u):
        return (qcD, sDc) if u % 2 == 0 else (qdD, sDd)

    qeD = _QIssue("de", sDe)
    qfD = _QIssue("df", sDf)

    def slab4_q(u):
        return [(qaD, sDa), (qbD, sDb), (qeD, sDe), (qfD, sDf)][u % 4]

    qgD2 = _QIssue("dg", sDg)
    qhD2 = _QIssue("dh", sDh)
    qiD = _QIssue("di", sDi)
    qjD = _QIssue("dj", sDj)
    qkD = _QIssue("dk", sDk)
    qlD = _QIssue("dl", sDl)
    qmD = _QIssue("dm", sDm)
    qnD = _QIssue("dn", sDn)

    def gam4_q(u):
        return [(qiD, sDi), (qjD, sDj), (qkD, sDk), (qlD, sDl)][u % 4]


    def store4_q(u):
        return [(qcD, sDc), (qdD, sDd), (qgD2, sDg), (qhD2, sDh)][u % 4]

    def ti(inst):
        Q["t"] += 1
        inst.then_inc(sT, 1)
        return Q["t"]

    def vi(inst):
        Q["v"] += 1
        inst.then_inc(sV, 1)
        return Q["v"]

    def si(inst):
        Q["s"] += 1
        inst.then_inc(sS, 1)
        return Q["s"]

    def xnp(dc, t0, shift):
        return dc * CHS + shift + t0

    def full_mat_src(t):
        return bass.AP(t.tensor, 0, [[D, 128], [128 * D, 8], [1, D]])

    def tile_src(t, j):
        return bass.AP(t.tensor, j * TS, [[T, 128], [T * 128, 8], [1, TS]])

    def tile_dst(t, j):
        return bass.AP(t.tensor, j * TS, [[T, 128], [T * 128, 8], [1, TS]])

    bigAF = bigA.ap().bitcast(F32)

    def kern_quarter_src(qr):
        return bass.AP(kern.tensor, qr * 2 * 128 * D, [[D, 128], [128 * D, 2], [1, D]])

    def unit_srcs(u):
        if u >= 16:
            dc, hh = divmod(u - 16, 2)
            return [(bass.AP(W1T.tensor, 128 * dc * DFF + hh * 2048,
                             [[DFF, 128], [1, 2048]]), 0, 2048)]
        fc = 2 * u
        return [(bass.AP(W2T.tensor, 128 * fc * D, [[D, 128], [1, 1024]]), 0, 1024),
                (bass.AP(W2T.tensor, 128 * (fc + 1) * D, [[D, 128], [1, 1024]]),
                 1024, 1024)]

    def unit_stores(u):
        if u >= 16:
            dc, hh = divmod(u - 16, 2)
            out = []
            for piece in range(2):
                fc0 = hh * 16 + piece * 8
                out.append((bass.AP(W1qS.tensor, fc0 * 131072 + dc * 128,
                                    [[1024, 128], [131072, 8], [1, 128]]),
                            piece * 1024, 1024))
            return out
        fc = 2 * u
        return [(bass.AP(W2qS.tensor, fc * 128,
                         [[4096, 128], [524288, 8], [1, 128]]), 0, 1024),
                (bass.AP(W2qS.tensor, (fc + 1) * 128,
                         [[4096, 128], [524288, 8], [1, 128]]), 1024, 1024)]

    NU = 32
    THRS = [m + 0.5 for m in range(-7, 7)]

    with nc.Block() as blk:
        v0, s0, t0 = Q["v"], Q["s"], Q["t"]
        qg_c = [0]
        qb_mx = [0]
        qb_gm = [0]
        q_mx = [0] * 4
        q_cm = [0] * 4
        q_gm = [0] * NU
        v_qred = [v0 + qr + 1 for qr in range(4)]
        v_comb = v0 + 5
        v_r0 = v0 + 8
        def v_cmp(ci):
            q, m = divmod(ci, 14)
            return v0 + 8 + 18 * q + m + 1
        v_qev = [v0 + 8 + 18 * q + 18 for q in range(4)]
        v_gred = v0 + 8 + 72 + 2
        s_tok = [s0 + 3 + u + 1 for u in range(NU)]
        def t_grp(q, m):
            return t0 + 56 * q + 4 * (m + 1)

        @blk.gpsimd
        def _(g):
            qgD(g, cstF.ap()[:, 0:2608], cblobF_in)
            qgD(g, cstB[:, :], cblobB_in)
            qgD(g, a_sb[:, :],
                bass.AP(a_m.tensor, 0, [[16, 128], [2048, 8], [1, 16]]))
            qgD(g, b_sb[:, :], b_m)
            qgD(g, wd_sb[:, :], bass.AP(WdT.tensor, 0, [[1, 128], [128, 8]]),
                allow_slow_non_contiguous=True)
            qg_c[0] = qgD(g, sc[:, 0:4], scIn_in)
            for u in range(NU):
                if u == 8:
                    g.wait_ge(sV, v_comb)
                    g.tensor_reduce(sc[0:1, C_WMXR:C_WMXR + 1],
                                    sc[:, C_TMP:C_TMP + 1], axis=AX.C, op=ALU.max)
                    g.drain()
                    tb1 = qgD(g, bncS[0:1, 0:1], sc[0:1, C_WMXR:C_WMXR + 1])
                    g.wait_ge(sQg, tb1)
                    qb_mx[0] = qgD(g, sc[:, C_WMXR:C_WMXR + 1],
                                   bass.AP(bncS.tensor, 0, [[0, 128], [1, 1]]))
                if u >= 4:
                    g.wait_ge(sS, s_tok[u - 4])
                slot = (u % 4) * 2048
                qfn, _sh = gam4_q(u)
                tok = None
                for srcu, so, w in unit_srcs(u):
                    tok = qfn(g, bigA[:, slot + so:slot + so + w], srcu)
                q_gm[u] = tok
            g.wait_ge(sV, v_gred)
            g.tensor_reduce(sc[0:1, C_G12B:C_G12B + 2], sc[:, C_G12:C_G12 + 2],
                            axis=AX.C, op=ALU.add)
            g.drain()
            tb2 = qgD(g, bncS[0:1, 2:4], sc[0:1, C_G12B:C_G12B + 2])
            g.wait_ge(sQg, tb2)
            qb_gm[0] = qgD(g, sc[:, C_G12B:C_G12B + 2],
                           bass.AP(bncS.tensor, 2, [[0, 128], [1, 2]]))
            g.wait_ge(sQg, qb_gm[0])

        xfF = xf.ap().bitcast(F32)

        def kq_slot(qr):
            if qr < 2:
                return stmF[:, qr * 2048:qr * 2048 + 2048]
            return xfF[:, (qr - 2) * 2048:(qr - 2) * 2048 + 2048]

        kq_q = [(qsD, sQs), (qaD, sDa), (qbD, sDb), (qcD, sDc)]

        @blk.sync
        def _(s):
            for qr in range(4):
                q_mx[qr] = kq_q[qr][0](s, kq_slot(qr), kern_quarter_src(qr))

        @blk.scalar
        def _(s):
            s.wait_ge(sQg, qg_c[0])
            si(s.activation(sc[:, C_GAM:C_GAM + 1], sc[:, C_DEC:C_DEC + 1], AF.Sigmoid))
            s.drain()
            si(s.activation(sc[:, C_LG:C_LG + 1], sc[:, C_GAM:C_GAM + 1], AF.Ln))
            s.drain()
            si(s.activation(sc[:, C_ALPHA:C_ALPHA + 1], sc[:, C_LA:C_LA + 1], AF.Exp))
            for u in range(NU):
                s.wait_ge(gam4_q(u)[1], q_gm[u])
                slot = (u % 4) * 2048
                s.drain()
                got = si(s.activation(
                    bigA[:, 8192:8192 + 2048], bigA[:, slot:slot + 2048], AF.Abs,
                    accum_out=statsT[:, u:u + 1]))
                assert got == s_tok[u]

        @blk.vector
        def _(v):
            v.memset(sc[:, C_E5:C_E5 + 1], EPS_LN)
            v.memset(sc[:, C_NH:C_NH + 1], -0.5)
            v.memset(sc[:, C_E8:C_E8 + 1], EPS_LD)
            v.memset(sc[:, C_I32:C_I32 + 1], 1.0 / 32.0)
            v.memset(sc[:, C_Q4:C_Q4 + 1], 0.25)
            v.memset(sc[:, C_HALFP:C_HALFP + 1], 0.5)
            v.wait_ge(sQg, qg_c[0])
            v.tensor_scalar_mul(biasL[:, 48:80], biasL[:, 8:40], 0.5)
            v.drain()
            for qr in range(4):
                v.wait_ge(kq_q[qr][1], q_mx[qr])
                got = vi(v.reduce_max(
                    sc[:, C_WMX + qr:C_WMX + qr + 1],
                    kq_slot(qr),
                    axis=AX.X, apply_absolute_value=True))
                assert got == v_qred[qr]
            v.drain()
            got = vi(v.reduce_max(sc[:, C_TMP:C_TMP + 1], sc[:, C_WMX:C_WMX + 4],
                                  axis=AX.X))
            assert got == v_comb
            v.wait_ge(sQg, qb_mx[0])
            vi(v.reciprocal(sc[:, C_TMP:C_TMP + 1], sc[:, C_WMXR:C_WMXR + 1]))
            v.drain()
            vi(v.tensor_scalar_mul(sc[:, C_R0:C_R0 + 1], sc[:, C_TMP:C_TMP + 1], 7.0))
            got = vi(v.tensor_scalar_mul(sc[:, C_S0:C_S0 + 1],
                                         sc[:, C_WMXR:C_WMXR + 1], 1.0 / 7.0))
            assert got == v_r0
            v.wait_ge(sS, s0 + 2)
            v.tensor_scalar_mul(sc[:, C_NLG:C_NLG + 1], sc[:, C_LG:C_LG + 1], -1.0)
            v.drain()
            for qr in range(4):
                for m in range(14):
                    ci = 14 * qr + m
                    if ci >= 4:
                        q2, m2 = divmod(ci - 4, 14)
                        v.wait_ge(sT, t_grp(q2, m2))
                    cslot = (rt[:, (ci % 4 // 2) * 2048:(ci % 4 // 2) * 2048 + 2048]
                             if ci % 2 == 0 else
                             xbf[:, (ci % 4 // 2) * 2048:(ci % 4 // 2) * 2048 + 2048])
                    got = vi(v.tensor_scalar(
                        cslot, kq_slot(qr),
                        sc[:, C_R0:C_R0 + 1], THRS[m], ALU.mult, ALU.is_gt))
                    assert got == v_cmp(ci)
                v.wait_ge(sT, t_grp(qr, 13))
                for h in range(4):
                    got = vi(v.tensor_scalar(
                        wqA[:, qr * 2048 + h * 512: qr * 2048 + (h + 1) * 512],
                        pb[h][:, :], -7.0, sc[:, C_S0:C_S0 + 1], ALU.add, ALU.mult))
                assert got == v_qev[qr]
            v.wait_ge(sS, s_tok[NU - 1])
            vi(v.reduce_sum(sc[:, C_G12:C_G12 + 1], statsT[:, 16:32], axis=AX.X))
            got = vi(v.reduce_sum(sc[:, C_G12 + 1:C_G12 + 2], statsT[:, 0:16],
                                  axis=AX.X))
            assert got == v_gred
            v.wait_ge(sQg, qb_gm[0])
            v.tensor_scalar(sc[:, C_TMP:C_TMP + 2], sc[:, C_G12B:C_G12B + 2],
                            1.0 / (DFF * D), EPS_BN, ALU.mult, ALU.add)
            v.drain()
            vi(v.reciprocal(sc[:, C_R12:C_R12 + 2], sc[:, C_TMP:C_TMP + 2]))

        @blk.tensor
        def _(t):
            for qr in range(4):
                if qr >= 1:
                    t.wait_ge(sV, v_qev[qr - 1])
                for m in range(14):
                    ci = 14 * qr + m
                    t.wait_ge(sV, v_cmp(ci))
                    cslot = (rt[:, (ci % 4 // 2) * 2048:(ci % 4 // 2) * 2048 + 2048]
                             if ci % 2 == 0 else
                             xbf[:, (ci % 4 // 2) * 2048:(ci % 4 // 2) * 2048 + 2048])
                    got = None
                    for h in range(4):
                        got = ti(t.matmul(pb[h][:, :], idP[:, :],
                                          cslot[:, h * 512:(h + 1) * 512],
                                          start=(m == 0), stop=(m == 13)))
                    assert got == t_grp(qr, m)

    def ln_proj(layer):
        shift = 0 if layer == 0 else 1
        src_t = xT if layer == 0 else x1S
        with nc.Block() as blk:
            v0, s0, t0 = Q["v"], Q["s"], Q["t"]
            qg_snap = Q["qg"]
            v_sq = [v0 + 27 * j + 8 for j in range(NTT)]
            v_mcopy = [v0 + 27 * j + 9 for j in range(NTT)]
            v_var = [v0 + 27 * j + 11 for j in range(NTT)]
            v_end = [v0 + 27 * j + 27 for j in range(NTT)]
            q_x = [0] * NTT
            q_b = [0] * NTT
            q_w = [0] * 3

            plan = []
            if layer == 0:
                plan += [("lnm", 0), ("lns", 0), ("lnm", 1), ("lns", 1)]
                plan += [("wg", gi) for gi in range(16)]
                plan += [("lnm", 2), ("lns", 2)]
                plan += [("g", ec, 0) for ec in range(8)]
                plan += [("lnm", 3), ("lns", 3)]
                plan += [("vtm", tc, et) for tc in range(0, 4) for et in range(2)]
                plan += [("g", ec, 1) for ec in range(8)]
                plan += [("vtm", tc, et) for tc in range(4, 8) for et in range(2)]
                plan += [("g", ec, 2) for ec in range(8)]
                plan += [("g", ec, 3) for ec in range(8)]
                plan += [("vtm", tc, et) for tc in range(8, 16) for et in range(2)]
                plan += [("ld", tt) for tt in range(4)]
            else:
                plan += [("lnm", 0), ("lns", 0), ("lnm", 1), ("lns", 1)]
                plan += [("vtm", tc, et) for tc in range(0, 4) for et in range(2)]
                plan += [("lnm", 2), ("lns", 2)]
                plan += [("vtm", tc, et) for tc in range(4, 8) for et in range(2)]
                plan += [("lnm", 3), ("lns", 3)]
                plan += [("vtm", tc, et) for tc in range(8, 16) for et in range(2)]
            t_tok = {}
            for i, e in enumerate(plan):
                t_tok[e] = t0 + 8 * (i + 1)
            t_m = [t_tok[("lnm", j)] for j in range(NTT)]
            t_s2 = [t_tok[("lns", j)] for j in range(NTT)]
            pseq = [e for e in plan if e[0] not in ("lnm", "lns")]
            pidx = {e: i for i, e in enumerate(pseq)}
            s_seq = []
            for e in plan:
                if e[0] == "lns":
                    s_seq.append(("rstd", e[1]))
                elif e[0] != "lnm":
                    s_seq.append(e)
            s_tok = {}
            cnt = s0
            for e in s_seq:
                cnt += 2 if e[0] == "rstd" else 1
                s_tok[e] = cnt
            s_rstd = [s_tok[("rstd", j)] for j in range(NTT)]

            NUQ = 32
            q_ld = [0] * NUQ
            st_tok = [0] * NUQ
            sg_tok = [0] * NUQ
            vc_tok = [0] * NUQ

            def ld_q(u):
                return [(qbD, sDb), (qeD, sDe), (qfD, sDf), (qmD, sDm)][u % 4]

            def ldslot(u):
                return bigAF[:, (u % 4) * 2048:(u % 4) * 2048 + 2048]

            def qslot_b(u):
                return bigA[:, (u % 4) * 4096:(u % 4) * 4096 + 2048]

            def tslot(u, k):
                if u < 16:
                    return wqA[:, ((u % 2) * 2 + k) * 2048:
                               ((u % 2) * 2 + k) * 2048 + 2048]
                p = (u - 16) % 4
                if p < 2:
                    return wqA[:, (p * 2 + k) * 2048:(p * 2 + k) * 2048 + 2048]
                return stm[:, ((p - 2) * 2 + k) * 2048:((p - 2) * 2 + k) * 2048 + 2048]


            def xslot(j):
                return xf[:, :]

            xfB = xf.ap().bitcast(BF16)

            def bslot(j):
                if j == 0:
                    return xbf[:, 0:4096]
                if j == 1:
                    return rt[:, 0:4096]
                return xfB[:, (j - 2) * 4096:(j - 2) * 4096 + 4096]

            def sqslot(j):
                return stm[:, (j % 2) * 4096:(j % 2) * 4096 + 4096]

            def mslot(j):
                return statsT[:, (j % 2) * 512:(j % 2) * 512 + 512]

            def rslot(j):
                return statsB[:, 2048 + (j % 2) * 512:2048 + (j % 2) * 512 + 512]

            maskbB = maskb.ap().bitcast(BF16)

            def nscr(j):
                return maskbB[:, 3072 + (j % 2) * 512:3072 + (j % 2) * 512 + 512]

            def varslot(j):
                return maskb[:, (j % 2) * 512:(j % 2) * 512 + 512]

            def scrslot(j):
                return maskb[:, 1024 + (j % 2) * 512:1024 + (j % 2) * 512 + 512]

            def pj_dst(e):
                if e[0] == "wg":
                    dc, et = divmod(e[1], 2)
                    return bigB[:, 8192 + dc * 1024 + et * 512:
                                8192 + dc * 1024 + (et + 1) * 512]
                if e[0] == "g":
                    ec, tt = e[1], e[2]
                    return bigA[:, ec * 2048 + tt * 512: ec * 2048 + (tt + 1) * 512]
                if e[0] == "vtm":
                    tc, et = e[1], e[2]
                    return bigB[:, tc * 1024 + et * 512: tc * 1024 + (et + 1) * 512]
                return None

            @blk.gpsimd
            def _(g):
                if shift == 1:
                    g.wait_ge(sQg, qg_snap)
                    g.wait_ge(sDi, Q["di"])
                    g.wait_ge(sDj, Q["dj"])
                    g.wait_ge(sDk, Q["dk"])
                    g.wait_ge(sDl, Q["dl"])
                qfn0 = [qiD, qjD]
                q_b[0] = qfn0[0](g, bslot(0), tile_src(src_t, 0))
                q_b[1] = qfn0[1](g, bslot(1), tile_src(src_t, 1))
                if layer == 0:
                    qgD(g, bigB[:, 0:8192], full_mat_src(Wq))
                    q_w[0] = qgD(g, projW[:, :], full_mat_src(Wk))
                else:
                    q_w[0] = qgD(g, bigA[:, 0:8192], full_mat_src(WwrT))
                    q_w[1] = qnD(g, projW[:, :], full_mat_src(WrdT))
                    q_w[2] = qlD(g, nm[:, :], nmB)
                for j in (1, 2, 3):
                    if j < 3:
                        jj = j + 1
                        g.wait_ge((sDi if jj % 2 == 0 else sDj), q_b[jj - 2])
                        q_b[jj] = qfn0[jj % 2](g, bslot(jj), tile_src(src_t, jj))
                if layer == 0:
                    g.wait_ge(sT, t_tok[("wg", 15)])
                    q_w[1] = qgD(g, projW[:, :], full_mat_src(WvT))
                    g.wait_ge(sS, s_tok[("ld", 3)] + 1)
                    tb = qgD(g, ldS[0:1, :], stmF[0:1, 0:2048])
                    g.wait_ge(sQg, tb)
                    tb = qgD(g, ld_cols[:, :],
                             bass.AP(ldS.tensor, 0, [[1, 128], [128, 16]]),
                             allow_slow_non_contiguous=True)
                    g.wait_ge(sQg, tb)
                else:
                    g.wait_ge(sDn, q_w[1])
                    g.wait_ge(sDl, q_w[2])

            @blk.sync
            def _(s):
                return

            @blk.tensor
            def _(t):
                np_ = 0
                for e in plan:
                    if e[0] == "lnm":
                        j = e[1]
                        t.wait_ge((sDi if j % 2 == 0 else sDj), q_b[j])
                        if j >= 2:
                            t.wait_ge(sV, v_mcopy[j - 2])
                        bs = bslot(j)
                        got = None
                        for dc in range(NDC):
                            got = ti(t.matmul(pb[(j % 2) * 2][:, :], inv1024[:, :],
                                              bs[:, dc * 512:(dc + 1) * 512],
                                              start=(dc == 0), stop=(dc == NDC - 1)))
                        assert got == t_m[j]
                        continue
                    if e[0] == "lns":
                        j = e[1]
                        t.wait_ge(sV, v_sq[j])
                        if j >= 2:
                            t.wait_ge(sV, v_var[j - 2])
                        sq = sqslot(j)
                        got = None
                        for dc in range(NDC):
                            got = ti(t.matmul(pb[(j % 2) * 2 + 1][:, :], inv1024[:, :],
                                              sq[:, dc * 512:(dc + 1) * 512],
                                              start=(dc == 0), stop=(dc == NDC - 1)))
                        assert got == t_s2[j]
                        continue
                    p = pidx[e]
                    if p >= 4:
                        t.wait_ge(sS, s_tok[pseq[p - 4]])
                    bank = pb[4 + p % 4]
                    if e[0] == "wg":
                        dc, et = divmod(e[1], 2)
                        t.wait_ge(sQg, q_w[0])
                        got = None
                        for fc in range(NDC):
                            got = ti(t.matmul(
                                bank[:, :],
                                bigB[:, fc * 1024 + dc * 128: fc * 1024 + (dc + 1) * 128],
                                projW[:, fc * 1024 + et * 512: fc * 1024 + (et + 1) * 512],
                                start=(fc == 0), stop=(fc == NDC - 1)))
                    elif e[0] == "g":
                        ec, tt = e[1], e[2]
                        t.wait_ge(sV, v_end[tt])
                        t.wait_ge(sS, s_tok[("wg", 15)])
                        got = None
                        for dc in range(NDC):
                            p0 = xnp(dc, tt * TS, 0)
                            got = ti(t.matmul(
                                bank[:, :],
                                bigB[:, 8192 + dc * 1024 + ec * 128:
                                     8192 + dc * 1024 + (ec + 1) * 128],
                                xn[:, p0:p0 + TS],
                                start=(dc == 0), stop=(dc == NDC - 1)))
                    elif e[0] == "vtm":
                        tc, et = e[1], e[2]
                        t.wait_ge(sV, v_end[tc // 4])
                        if layer == 0:
                            t.wait_ge(sQg, q_w[1])
                            wsrc = projW
                        else:
                            t.wait_ge(sQg, q_w[0])
                            wsrc = bigA
                        got = None
                        for dc in range(NDC):
                            p0 = xnp(dc, tc * 128, shift)
                            got = ti(t.matmul(
                                bank[:, :], xn[:, p0:p0 + 128],
                                wsrc[:, dc * 1024 + et * 512: dc * 1024 + (et + 1) * 512],
                                start=(dc == 0), stop=(dc == NDC - 1)))
                    else:
                        tt = e[1]
                        t.wait_ge(sV, v_end[tt])
                        got = None
                        for dc in range(NDC):
                            p0 = xnp(dc, tt * TS, 0)
                            got = ti(t.matmul(bank[0:1, :], wd_sb[:, dc:dc + 1],
                                              xn[:, p0:p0 + TS],
                                              start=(dc == 0), stop=(dc == NDC - 1)))
                    assert got == t_tok[e]

            @blk.vector
            def _(v):
                if shift == 1:
                    for dc in range(NDC):
                        v.memset(xn[:, dc * CHS:dc * CHS + 1], 0.0)
                for j in range(NTT):
                    if j >= 2:
                        v.wait_ge(sT, t_s2[j - 2])
                    v.wait_ge((sDi if j % 2 == 0 else sDj), q_b[j])
                    bs, sq = bslot(j), sqslot(j)
                    got = None
                    for dc in range(NDC):
                        got = vi(v.tensor_mul(sq[:, dc * 512:(dc + 1) * 512],
                                              bs[:, dc * 512:(dc + 1) * 512],
                                              bs[:, dc * 512:(dc + 1) * 512]))
                    assert got == v_sq[j], ('vsq', j, got, v_sq)
                    v.wait_ge(sT, t_m[j])
                    got = vi(v.tensor_copy(mslot(j), pb[(j % 2) * 2][:, :]))
                    assert got == v_mcopy[j]
                    v.drain()
                    vi(v.tensor_mul(scrslot(j), mslot(j), mslot(j)))
                    v.drain()
                    v.wait_ge(sT, t_s2[j])
                    got = vi(v.tensor_sub(varslot(j), pb[(j % 2) * 2 + 1][:, :],
                                          scrslot(j)))
                    assert got == v_var[j]
                    v.wait_ge(sS, s_rstd[j])
                    bs = bslot(j)
                    for dc in range(NDC):
                        v.drain()
                        vi(v.tensor_sub(nscr(j), bs[:, dc * 512:(dc + 1) * 512],
                                        mslot(j)))
                        v.drain()
                        p0 = xnp(dc, j * TS, shift)
                        got = vi(v.tensor_mul(xn[:, p0:p0 + TS], nscr(j), rslot(j)))
                    assert got == v_end[j]

            @blk.scalar
            def _(s):
                for e in s_seq:
                    if e[0] == "rstd":
                        j = e[1]
                        s.wait_ge(sV, v_var[j])
                        s.drain()
                        si(s.activation(scrslot(j), varslot(j), AF.Ln,
                                        bias=sc[:, C_E5:C_E5 + 1]))
                        s.drain()
                        got = si(s.activation(rslot(j), scrslot(j), AF.Exp,
                                              scale=sc[:, C_NH:C_NH + 1]))
                        assert got == s_rstd[j]
                        continue
                    p = pidx[e]
                    bank = pb[4 + p % 4]
                    s.wait_ge(sT, t_tok[e])
                    s.drain()
                    if e[0] == "wg":
                        got = si(s.activation(pj_dst(e), bank[:, :], AF.Copy,
                                              scale=sc[:, C_I32:C_I32 + 1]))
                    elif e[0] in ("g", "vtm"):
                        got = si(s.copy(pj_dst(e), bank[:, :]))
                    else:
                        tt = e[1]
                        got = si(s.activation(statsT[0:1, tt * 512:(tt + 1) * 512],
                                              bank[0:1, :], AF.Sigmoid,
                                              bias=sc[0:1, C_BD:C_BD + 1]))
                    assert got == s_tok[e]
                if layer == 0:
                    s.drain()
                    si(s.activation(stmF[0:1, 0:2048], statsT[0:1, 0:2048],
                                    AF.Ln, bias=sc[0:1, C_E8:C_E8 + 1]))


    def PROJ0():
        ln_proj(0)

    def PROJ1():
        ln_proj(1)
        with nc.Block() as blk:
            @blk.gpsimd
            def _(g):
                for semh, key in ((sDb, "db"), (sDe, "de"), (sDf, "df"),
                                  (sDm, "dm"), (sDc, "dc_"), (sDd, "dd"),
                                  (sDg, "dg"), (sDh, "dh")):
                    g.wait_ge(semh, Q[key])

    PROJ0()

    QCARRY = {}

    def attn_tile(layer, j, blk, prev):
        shift = 0 if layer == 0 else 1
        if layer == 0:
            schunks = list(range(4 * j, NSC))

            def strad(i):
                return i < 4 * j + 4

            def cc(i):
                return float(128 * i - 512 * j - 1)
        else:
            schunks = list(range(0, 4 * j + 4))

            def strad(i):
                return i >= 4 * j

            def cc(i):
                return float(512 * j - 128 * i - 1)
        NS = len(schunks)

        v0, s0, t0 = Q["v"], Q["s"], Q["t"]
        prefA = []
        tot = 0
        for i in schunks:
            tot += 1 + (1 if strad(i) else 0)
            prefA.append(v0 + tot)
        totalA = tot
        if layer == 0:
            QUC = [7, 10, 8, 7]
            u0 = sum(QUC[:j])
            units_j = list(range(u0, u0 + QUC[j]))
            NH = 2 * len(units_j)
            NSL = [2, 3, 4, 4][j]
            anchors = ([("st", idx) for idx in range(NS)] + [("xa", 0)]
                       + [("R", ec) for ec in range(8)]
                       + [("res", dc) for dc in range(8)])
            NA = len(anchors)
            vplan = []
            qdone = 0
            for a, anc in enumerate(anchors):
                vplan.append(anc)
                want = min(NH, (NH * (a - 3)) // (NA - 7)) if a >= 4 else 0
                while qdone < want:
                    vplan.append(("q", qdone))
                    qdone += 1
            while qdone < NH:
                vplan.append(("q", qdone))
                qdone += 1
            vcnt = v0 + totalA
            v_mult = [0] * NS
            v_R = [0] * 8
            v_res = [0] * 8
            cmb_tok = [0] * NH
            v_xa = 0
            for e in vplan:
                vcnt += 1
                if e[0] == "q":
                    cmb_tok[e[1]] = vcnt
                elif e[0] == "st":
                    v_mult[e[1]] = vcnt
                elif e[0] == "xa":
                    v_xa = vcnt
                elif e[0] == "R":
                    v_R[e[1]] = vcnt
                else:
                    v_res[e[1]] = vcnt
            splan = []
            for e in vplan:
                if e[0] == "st":
                    splan.append(("m", e[1]))
                elif e[0] == "q":
                    splan.append(("g", e[1]))
            scnt = s0
            s_mask = [0] * NS
            sg_tok = [0] * NH
            for e in splan:
                if e[0] == "m":
                    scnt += 1
                    s_mask[e[1]] = scnt
                else:
                    scnt += 2
                    sg_tok[e[1]] = scnt
            t_xa = t0 + 8
            t_st = [t_xa + 8 * (idx + 1) for idx in range(NS)]
            t_R = [t_st[NS - 1] + NS * (ec + 1) for ec in range(8)]
            t_res = [t_R[7] + 17 * (dc + 1) for dc in range(8)]
            xbfF = xbf.ap().bitcast(F32)
            stmF2 = stm.ap().bitcast(F32)

            def qldF(h):
                s_ = h % NSL
                if s_ < 2:
                    return xbfF[:, s_ * 1024:s_ * 1024 + 1024]
                return stmF2[:, (s_ - 2) * 1024:(s_ - 2) * 1024 + 1024]

            def qout(h):
                s_ = h % NSL
                if s_ < 2:
                    return xbf[:, s_ * 2048:s_ * 2048 + 1024]
                return stm[:, (s_ - 2) * 2048:(s_ - 2) * 2048 + 1024]

            def qt1c(h):
                return statsB[:, (h % 2) * 2048:(h % 2) * 2048 + 1024]

            def qt2c(h):
                return statsB[:, (h % 2) * 2048 + 1024:(h % 2) * 2048 + 2048]

            def q_src(h):
                u = units_j[h // 2]
                half = h % 2
                if u >= 16:
                    dcq, hh = divmod(u - 16, 2)
                    return bass.AP(W1T.tensor,
                                   128 * dcq * DFF + hh * 2048 + half * 1024,
                                   [[DFF, 128], [1, 1024]])
                fc = 2 * u + half
                return bass.AP(W2T.tensor, 128 * fc * D, [[D, 128], [1, 1024]])

            def q_dst(h):
                u = units_j[h // 2]
                half = h % 2
                if u >= 16:
                    dcq, hh = divmod(u - 16, 2)
                    fc0 = hh * 16 + half * 8
                    return bass.AP(W1qS.tensor, fc0 * 131072 + dcq * 128,
                                   [[1024, 128], [131072, 8], [1, 128]])
                fc = 2 * u + half
                return bass.AP(W2qS.tensor, fc * 128,
                               [[4096, 128], [524288, 8], [1, 128]])

            def qld_q(h):
                return [(qbD, sDb), (qeD, sDe), (qfD, sDf), (qmD, sDm)][h % 4]

            def qst_q(h):
                return [(qiD, sDi), (qjD, sDj), (qkD, sDk), (qlD, sDl)][h % 4]

            qst_keys = ["di", "dj", "dk", "dl"]
            exp_qst = [Q[qst_keys[h % 4]] + 16 * (h // 4 + 1) for h in range(NH)]
        else:
            v_mult = [v0 + totalA + idx + 1 for idx in range(NS)]
            vp = v0 + totalA + NS
            v_R = [vp + ec + 1 for ec in range(8)]
            v_res = [vp + 8 + dc + 1 for dc in range(8)]
            t_st = [t0 + 8 * (idx + 1) for idx in range(NS)]
            t_R = [t_st[NS - 1] + NS * (ec + 1) for ec in range(8)]
            t_h = [t_R[7] + 8 * (fc + 1) for fc in range(NFC)]
            t_res = [t_h[NFC - 1] + 40 * (dc + 1) for dc in range(8)]
        if layer == 1:
            s_mask = [s0 + idx + 1 for idx in range(NS)]
            s_h = [s0 + NS + fc + 1 for fc in range(NFC)]

        q_x = [0]
        q_w1 = [0] * NFC
        q_w2 = [0] * 8
        q_qld = [0] * 64
        q_qst = [0] * 64
        if layer == 1:
            w2q_q = [(qiD, sDi), (qjD, sDj)]
            exp_w2 = [Q[["di", "dj"][dc % 2]] + 16 * (dc // 2 + 1)
                      for dc in range(8)]

        @blk.sync
        def _(s):
            s.wait_ge(sQg, Q["qg"])
            src = xT if layer == 0 else x1S
            q_x[0] = qsD(s, xf[:, :], tile_src(src, j))
            if layer == 0:
                if j > 0:
                    for kq, tokq in QCARRY["fin"]:
                        s.wait_ge(kq, tokq)
                stm_gd = [False]

                def qld_issue(h):
                    if prev and not stm_gd[0] and (h % NSL) >= 2:
                        s.wait_ge(sT, prev["t_R"][7])
                        stm_gd[0] = True
                    q_qld[h] = qld_q(h)[0](s, qldF(h), q_src(h))

                for h in range(min(NSL, NH)):
                    qld_issue(h)
                for h in range(NSL, NH):
                    s.wait_ge(qst_q(h - NSL)[1], exp_qst[h - NSL])
                    qld_issue(h)
                fin = {}
                for h in range(NH):
                    fin[qst_q(h)[1].name] = (qst_q(h)[1], exp_qst[h])
                QCARRY["fin"] = list(fin.values())
            if layer == 1:
                if j == 0:
                    s.wait_ge(sDi, Q["di"])
                    s.wait_ge(sDj, Q["dj"])
                    s.wait_ge(sDk, Q["dk"])
                    s.wait_ge(sDl, Q["dl"])
                for fc in range(NFC):
                    if fc >= 4:
                        s.wait_ge(sT, t_h[fc - 4])
                    elif prev:
                        s.wait_ge(sT, prev["t_h"][28 + fc])
                    qfn, _sh = slab4_q(fc)
                    q_w1[fc] = qfn(s,
                        xbf[:, (fc % 4) * 1024:(fc % 4) * 1024 + 1024],
                        bass.AP(W1qS.tensor, fc * 128 * 1024,
                                [[1024, 128], [1, 1024]]))

        @blk.vector
        def _(v):
            if prev:
                v.wait_ge(sS, prev["s_last"] if layer == 0
                          else prev["s_mask"][-1])
                v.wait_ge(sT, prev["t_R"][7])
            for idx, i in enumerate(schunks):
                scl = ld_cols[:, i:i + 1] if layer == 0 else sc[:, C_LG:C_LG + 1]
                vi(v.tensor_scalar_mul(biascols[:, i:i + 1], scl, cc(i)))
                if strad(i):
                    k = i - 4 * j
                    escl = (ld_cols[:, i:i + 1] if layer == 0
                            else sc[:, C_NLG:C_NLG + 1])
                    vi(v.scalar_tensor_tensor(
                        statsB[:, k * 512:(k + 1) * 512], Bcs[:, :], escl,
                        nm[:, k * 512:(k + 1) * 512], ALU.mult, ALU.add))
            assert Q["v"] == v0 + totalA
            if layer == 0:
                for e in vplan:
                    if e[0] == "q":
                        h = e[1]
                        v.wait_ge(sS, sg_tok[h])
                        got = vi(v.tensor_add(qout(h), qt1c(h), qt2c(h)))
                        assert got == cmb_tok[h]
                    elif e[0] == "st":
                        idx = e[1]
                        i = schunks[idx]
                        v.wait_ge(sT, t_st[idx])
                        v.wait_ge(sS, s_mask[idx])
                        got = vi(v.tensor_mul(
                            stm[:, i * 512:(i + 1) * 512], pb[idx % 4][:, :],
                            maskb[:, (idx % 4) * 512:(idx % 4) * 512 + 512]))
                        assert got == v_mult[idx]
                    elif e[0] == "xa":
                        v.wait_ge(sT, t_xa)
                        got = vi(v.tensor_copy(xa_bf[:, :], pb[4][0:16, :]))
                        assert got == v_xa
                    elif e[0] == "R":
                        ec = e[1]
                        v.wait_ge(sT, t_R[ec])
                        got = vi(v.tensor_scalar_mul(
                            rt[:, ec * 512:(ec + 1) * 512],
                            pb[4 + ec % 4][:, :], sc[:, C_OS:C_OS + 1]))
                        assert got == v_R[ec]
                    else:
                        dc = e[1]
                        v.wait_ge(sT, t_res[dc])
                        v.wait_ge(sQs, q_x[0])
                        got = vi(v.scalar_tensor_tensor(
                            xf[:, dc * 512:(dc + 1) * 512], pb[dc % 2][:, :],
                            biasL[:, dc:dc + 1],
                            xf[:, dc * 512:(dc + 1) * 512], ALU.add, ALU.add))
                        assert got == v_res[dc]
            else:
                for idx, i in enumerate(schunks):
                    v.wait_ge(sT, t_st[idx])
                    v.wait_ge(sS, s_mask[idx])
                    got = vi(v.tensor_mul(
                        stm[:, i * 512:(i + 1) * 512], pb[idx % 4][:, :],
                        maskb[:, (idx % 4) * 512:(idx % 4) * 512 + 512]))
                    assert got == v_mult[idx]
                for ec in range(8):
                    v.wait_ge(sT, t_R[ec])
                    got = vi(v.tensor_scalar_mul(
                        rt[:, ec * 512:(ec + 1) * 512], pb[4 + ec % 4][:, :],
                        sc[:, C_ALPHA:C_ALPHA + 1]))
                    assert got == v_R[ec]
                for dc in range(8):
                    v.wait_ge(sT, t_res[dc])
                    v.wait_ge(sQs, q_x[0])
                    got = vi(v.scalar_tensor_tensor(
                        xf[:, dc * 512:(dc + 1) * 512], pb[2 + dc % 2][:, :],
                        biasL[:, 40 + dc:41 + dc],
                        xf[:, dc * 512:(dc + 1) * 512], ALU.add, ALU.add))
                    assert got == v_res[dc]

        def emit_mask(s, idx):
            i = schunks[idx]
            s.wait_ge(sV, prefA[idx])
            if idx >= 4:
                s.wait_ge(sV, v_mult[idx - 4])
            elif prev:
                s.wait_ge(sV, prev["v_mult"][prev["NS"] - 4 + idx])
            mslot = maskb[:, (idx % 4) * 512:(idx % 4) * 512 + 512]
            if strad(i):
                k = i - 4 * j
                got = si(s.activation(mslot, statsB[:, k * 512:(k + 1) * 512],
                                      AF.Exp, bias=biascols[:, i:i + 1]))
            else:
                scl = (ld_cols[:, i:i + 1] if layer == 0
                       else sc[:, C_NLG:C_NLG + 1])
                got = si(s.activation(mslot, Bcs[:, :], AF.Exp,
                                      bias=biascols[:, i:i + 1], scale=scl))
            assert got == s_mask[idx]

        @blk.scalar
        def _(s):
            if layer == 0:
                for e in splan:
                    if e[0] == "m":
                        emit_mask(s, e[1])
                        continue
                    h = e[1]
                    s.wait_ge(qld_q(h)[1], q_qld[h])
                    if h >= 2:
                        s.wait_ge(sV, cmb_tok[h - 2])
                    elif prev and prev["NH"]:
                        s.wait_ge(sV, prev["cmb"][prev["NH"] - 2 + h])
                    u = units_j[h // 2]
                    rcol = (sc[:, C_R12:C_R12 + 1] if u >= 16
                            else sc[:, C_R12 + 1:C_R12 + 2])
                    s.drain()
                    si(s.activation(qt1c(h), qldF(h), AF.Sign,
                                    scale=rcol, bias=sc[:, C_NH:C_NH + 1]))
                    got = si(s.activation(qt2c(h), qldF(h), AF.Sign,
                                          scale=rcol,
                                          bias=sc[:, C_HALFP:C_HALFP + 1]))
                    assert got == sg_tok[h]
            else:
                for idx in range(NS):
                    emit_mask(s, idx)
            if layer == 1:
                for fc in range(NFC):
                    s.wait_ge(sT, t_h[fc])
                    got = si(s.activation(
                        bigA[:, fc * 512:(fc + 1) * 512], pb[fc % 2][:, :],
                        AF.Relu, bias=biasL[:, 48 + fc:49 + fc],
                        scale=sc[:, C_Q4:C_Q4 + 1]))
                    assert got == s_h[fc]

        @blk.tensor
        def _(t):
            if layer == 0:
                if prev:
                    t.wait_ge(sV, prev["v_R"][4])
                got = None
                for dc in range(NDC):
                    p0 = xnp(dc, j * TS, 0)
                    got = ti(t.matmul(pb[4][0:16, :], a_sb[:, dc * 16:(dc + 1) * 16],
                                      xn[:, p0:p0 + TS],
                                      start=(dc == 0), stop=(dc == NDC - 1)))
                assert got == t_xa
            for idx, i in enumerate(schunks):
                if idx >= 4:
                    t.wait_ge(sV, v_mult[idx - 4])
                elif prev:
                    if layer == 0:
                        pw = [("v", prev["v_res"][6]), ("v", prev["v_res"][7]),
                              ("v", prev["v_mult"][prev["NS"] - 2]),
                              ("v", prev["v_mult"][prev["NS"] - 1])][idx]
                    else:
                        pw = [("s", prev["s_h"][30]), ("s", prev["s_h"][31]),
                              ("v", prev["v_res"][6]),
                              ("v", prev["v_res"][7])][idx]
                    t.wait_ge(sV if pw[0] == "v" else sS, pw[1])
                bank = pb[idx % 4]
                got = None
                for ec in range(NDC):
                    if layer == 0:
                        p0 = xnp(ec, i * 128, 0)
                        lhs = xn[:, p0:p0 + 128]
                        rhs = bigA[:, ec * 2048 + j * 512: ec * 2048 + j * 512 + 512]
                    else:
                        lhs = xn[:, ec * CHS + i * 128: ec * CHS + i * 128 + 128]
                        rhs = xn[:, ec * CHS + 1 + j * 512: ec * CHS + 1 + j * 512 + 512]
                    got = ti(t.matmul(bank[:, :], lhs, rhs,
                                      start=(ec == 0), stop=(ec == NDC - 1)))
                assert got == t_st[idx]
            for ec in range(8):
                if ec == 0:
                    t.wait_ge(sV, v_mult[NS - 1])
                    if layer == 0:
                        t.wait_ge(sV, v_xa)
                if ec >= 4:
                    t.wait_ge(sV, v_R[ec - 4])
                elif prev:
                    t.wait_ge(sV, prev["v_R"][ec + 4])
                bank = pb[4 + ec % 4]
                got = None
                for n, i in enumerate(schunks):
                    got = ti(t.matmul(
                        bank[:, :],
                        bigB[:, i * 1024 + ec * 128: i * 1024 + (ec + 1) * 128],
                        stm[:, i * 512:(i + 1) * 512],
                        start=(n == 0), stop=(n == NS - 1)))
                assert got == t_R[ec]
            if layer == 1:
                for fc in range(NFC):
                    t.wait_ge(slab4_q(fc)[1], q_w1[fc])
                    if fc == 0:
                        t.wait_ge(sV, v_mult[NS - 1])
                    if fc >= 2:
                        t.wait_ge(sS, s_h[fc - 2])
                    bank = pb[fc % 2]
                    got = None
                    for dc in range(NDC):
                        p0 = xnp(dc, j * TS, 1)
                        got = ti(t.matmul(
                            bank[:, :],
                            xbf[:, (fc % 4) * 1024 + dc * 128:
                                 (fc % 4) * 1024 + (dc + 1) * 128],
                            xn[:, p0:p0 + TS],
                            start=(dc == 0), stop=(dc == NDC - 1)))
                    assert got == t_h[fc]
            rb = 0 if layer == 0 else 2
            for dc in range(8):
                if dc == 0:
                    t.wait_ge(sV, v_mult[NS - 1])
                    t.wait_ge(sV, v_R[7])
                    if layer == 1:
                        t.wait_ge(sS, s_h[NFC - 1])
                if dc >= 2:
                    t.wait_ge(sV, v_res[dc - 2])
                if layer == 1:
                    t.wait_ge(w2q_q[dc % 2][1], exp_w2[dc])
                bank = pb[rb + dc % 2]
                first = True
                got = None
                for ec in range(8):
                    got = ti(t.matmul(
                        bank[:, :],
                        projW[:, ec * 1024 + dc * 128: ec * 1024 + (dc + 1) * 128],
                        rt[:, ec * 512:(ec + 1) * 512],
                        start=first, stop=False))
                    first = False
                if layer == 0:
                    for d2 in range(8):
                        p0 = xnp(d2, j * TS, 0)
                        got = ti(t.matmul(
                            bank[:, :],
                            wqA[:, d2 * 1024 + dc * 128: d2 * 1024 + (dc + 1) * 128],
                            xn[:, p0:p0 + TS], start=False, stop=False))
                    got = ti(t.matmul(bank[:, :],
                                      b_sb[0:16, dc * 128:(dc + 1) * 128],
                                      xa_bf[0:16, :], start=False, stop=True))
                else:
                    for fc in range(NFC):
                        got = ti(t.matmul(
                            bank[:, :],
                            wqA[:, (dc % 2) * 4096 + fc * 128:
                                 (dc % 2) * 4096 + (fc + 1) * 128],
                            bigA[:, fc * 512:(fc + 1) * 512],
                            start=False, stop=(fc == NFC - 1)))
                assert got == t_res[dc]

        @blk.gpsimd
        def _(g):
            if layer == 1:
                for dc in range(8):
                    if dc >= 2:
                        g.wait_ge(sT, t_res[dc - 2])
                    elif prev:
                        g.wait_ge(sT, prev["t_res"][6 + dc])
                    got = w2q_q[dc % 2][0](g,
                        wqA[:, (dc % 2) * 4096:(dc % 2) * 4096 + 4096],
                        bass.AP(W2qS.tensor, dc * 128 * 4096,
                                [[4096, 128], [1, 4096]]))
                    assert got == exp_w2[dc], (dc, got, exp_w2[dc])
            if layer == 0:
                for h in range(NH):
                    if h >= 4:
                        g.wait_ge(qst_q(h - 4)[1], q_qst[h - 4])
                    g.wait_ge(sV, cmb_tok[h])
                    q_qst[h] = qst_q(h)[0](g, q_dst(h), qout(h))
                    assert q_qst[h] == exp_qst[h], (h, q_qst[h], exp_qst[h])
            g.wait_ge(sV, v_res[7])
            dst = x1S if layer == 0 else outT
            qgD(g, tile_dst(dst, j), xf[:, :])

        return dict(v_R=v_R, v_res=v_res, v_mult=v_mult,
                    v_xa=(v_xa if layer == 0 else 0),
                    s_mask=s_mask, s_h=(s_h if layer == 1 else None),
                    s_last=(scnt if layer == 0 else s_h[NFC - 1]),
                    cmb=(cmb_tok if layer == 0 else None),
                    NH=(NH if layer == 0 else 0),
                    t_R=t_R, t_res=t_res,
                    t_h=(t_h if layer == 1 else None), NS=NS)

    for jp in ((0, 1), (2, 3)):
        with nc.Block() as blk:
            prevt = None
            for j in jp:
                prevt = attn_tile(0, j, blk, prevt)

    PROJ1()

    for jp in ((0, 1), (2, 3)):
        with nc.Block() as blk:
            prevt = None
            for j in jp:
                prevt = attn_tile(1, j, blk, prevt)

    with nc.Block() as blk:
        @blk.gpsimd
        def _(g):
            g.wait_ge(sQg, Q["qg"])
            g.wait_ge(sQs, Q["qs"])

    for cm in reversed(ctx):
        cm.__exit__(None, None, None)
    return nc


def _host_consts():
    p = np.arange(128, dtype=np.float32)[:, None]
    jj = np.arange(512, dtype=np.float32)[None, :]
    Bc = (p - jj).astype(np.float32)
    nmA = np.empty((128, 2048), dtype=np.float32)
    nmB = np.empty((128, 2048), dtype=np.float32)
    for k in range(4):
        vA = np.where(p - jj + 128 * k - 1 >= 0, 0.0, NEG).astype(np.float32)
        vB = np.where(jj - p - 128 * k - 1 >= 0, 0.0, NEG).astype(np.float32)
        nmA[:, k * 512:(k + 1) * 512] = vA
        nmB[:, k * 512:(k + 1) * 512] = vB
    idP = np.eye(128, dtype=np.float32).astype(ml_dtypes.bfloat16)
    idN = (-np.eye(128, dtype=np.float32)).astype(ml_dtypes.bfloat16)
    inv1024 = np.full((128, 128), 1.0 / 1024.0, dtype=np.float32).astype(ml_dtypes.bfloat16)
    return Bc, nmA, nmB, idP, idN, inv1024


def kernel(L0_attn_x, L0_attn_Wq, L0_attn_Wk, L0_attn_Wv, L0_attn_Wo,
           L0_attn_Wd, L0_attn_bd, L0_attn_out_scale, L0_ffn_kernel,
           L0_ffn_bias, L0_ffn_a, L0_ffn_b, L1_attn_W_write, L1_attn_W_read,
           L1_attn_decay, L1_attn_log_alpha, L1_ffn_W1, L1_ffn_b1, L1_ffn_W2,
           L1_ffn_b2):
    f = np.float32

    def c(x):
        return np.ascontiguousarray(np.asarray(x, dtype=f))

    x = c(L0_attn_x)
    Bc, nmA, nmB, idP, idN, inv1024 = _host_consts()

    def rep(v):
        return np.full((128, 1), float(np.asarray(v, dtype=f).reshape(-1)[0]), dtype=f)

    cblobF = np.concatenate(
        [Bc, nmA,
         c(np.asarray(L0_ffn_bias).reshape(8, 128).T),
         c(np.asarray(L1_ffn_b1).reshape(32, 128).T),
         c(np.asarray(L1_ffn_b2).reshape(8, 128).T)], axis=1)
    scIn = np.concatenate(
        [rep(L0_attn_bd), rep(L0_attn_out_scale),
         rep(L1_attn_decay), rep(L1_attn_log_alpha)], axis=1)
    cblobB = np.concatenate([idP, idN, inv1024], axis=1)
    shared = {
        "Wq": c(L0_attn_Wq),
        "Wk": c(L0_attn_Wk),
        "WvT": c(np.asarray(L0_attn_Wv).T),
        "WdT": c(np.asarray(L0_attn_Wd).reshape(1, D).T),
        "WoT": c(np.asarray(L0_attn_Wo).T),
        "kern": c(L0_ffn_kernel),
        "a_m": c(L0_ffn_a),
        "b_m": c(L0_ffn_b),
        "WwrT": c(np.asarray(L1_attn_W_write).T),
        "WrdT": c(np.asarray(L1_attn_W_read).T),
        "W1T": c(np.asarray(L1_ffn_W1).T),
        "W2T": c(np.asarray(L1_ffn_W2).T),
        "cblobF": cblobF,
        "scIn": scIn,
        "nmB": nmB,
        "cblobB": cblobB,
    }

    n = x.shape[0]
    in_maps = []
    for ci in range(n):
        m = dict(shared)
        m["xT"] = np.ascontiguousarray(x[ci].T)
        in_maps.append(m)

    nc = build_nc()
    res = run_bass_kernel_spmd(nc, in_maps, core_ids=list(range(n)))
    out = np.stack([np.ascontiguousarray(np.asarray(r["outT"]).T)
                    for r in res.results], axis=0)
    return out.astype(np.float32)


# version 26
# speedup vs baseline: 1.0661x; 1.0280x over previous
import numpy as np
import ml_dtypes
import concourse.bass as bass
import concourse.mybir as mybir
from concourse.bass_utils import run_bass_kernel_spmd


F32 = mybir.dt.float32
BF16 = mybir.dt.bfloat16
AF = mybir.ActivationFunctionType
ALU = mybir.AluOpType
AX = mybir.AxisListType

D = 1024
T = 2048
NDC = 8
NTT = 4
TS = 512
NSC = 16
DFF = 4096
NFC = 32
CHS = 2064
NEG = -1.0e30
EPS_LN = 1.0e-5
EPS_BN = 1.0e-5
EPS_LD = 1.0e-8

C_BD, C_OS, C_DEC, C_LA, C_GAM, C_LG, C_NLG, C_ALPHA = 0, 1, 2, 3, 4, 5, 6, 7
C_G12, C_G12B = 8, 10
C_R12 = 12
C_WMX, C_WMXR = 14, 18
C_R0, C_S0 = 19, 20
C_TMP = 21
C_E5, C_NH, C_E8, C_I32, C_Q4, C_HALFP = 22, 23, 24, 25, 26, 27


def build_nc():
    nc = bass.Bass("TRN2", target_bir_lowering=False)
    Q = {"qg": 0, "qs": 0, "da": 0, "db": 0, "dc_": 0, "dd": 0,
         "de": 0, "df": 0, "dg": 0, "dh": 0, "di": 0, "dj": 0,
         "dk": 0, "dl": 0, "dm": 0, "dn": 0, "t": 0, "v": 0, "s": 0}

    def din(name, shape, dt=F32):
        return nc.dram_tensor(name, shape, dt, kind="ExternalInput").ap()

    xT = din("xT", [D, T])
    Wq = din("Wq", [D, D])
    Wk = din("Wk", [D, D])
    WvT = din("WvT", [D, D])
    WdT = din("WdT", [D, 1])
    WoT = din("WoT", [D, D])
    kern = din("kern", [D, D])
    a_m = din("a_m", [D, 16])
    b_m = din("b_m", [16, D])
    WwrT = din("WwrT", [D, D])
    WrdT = din("WrdT", [D, D])
    W1T = din("W1T", [D, DFF])
    W2T = din("W2T", [DFF, D])
    cblobF_in = din("cblobF", [128, 2608])
    scIn_in = din("scIn", [128, 4])
    nmB = din("nmB", [128, 2048])
    cblobB_in = din("cblobB", [128, 384], BF16)

    outT = nc.dram_tensor("outT", [D, T], F32, kind="ExternalOutput").ap()
    x1S = nc.dram_tensor("x1S", [D, T], F32, kind="Internal").ap()
    W1qS = nc.dram_tensor("W1qS", [NFC, 128, 1024], BF16, kind="Internal").ap()
    W2qS = nc.dram_tensor("W2qS", [NDC, 128, 4096], BF16, kind="Internal").ap()
    ldS = nc.dram_tensor("ldS", [1, T], F32, kind="Internal").ap()
    bncS = nc.dram_tensor("bncS", [1, 16], F32, kind="Internal").ap()

    ctx = []

    def sb(name, shape, dt):
        cm = nc.sbuf_tensor(name, shape, dt)
        h = cm.__enter__()
        ctx.append(cm)
        return h

    def psum(name, shape):
        cm = nc.psum_tensor(name, shape, F32)
        h = cm.__enter__()
        ctx.append(cm)
        return h

    def sem(name):
        cm = nc.semaphore(name)
        h = cm.__enter__()
        ctx.append(cm)
        return h

    xf = sb("xf", [128, 4096], F32)
    xbf = sb("xbf", [128, 4096], BF16)
    xn = sb("xn", [128, NDC * CHS], BF16)
    bigA = sb("bigA", [128, 16384], BF16)
    bigB = sb("bigB", [128, 16384], BF16)
    stm = sb("stm", [128, 8192], BF16)
    rt = sb("rt", [128, 4096], BF16)
    wqA = sb("wqA", [128, 8192], BF16)
    projW = sb("projW", [128, 8192], BF16)
    cstF = sb("cstF", [128, 2672], F32)
    Bcs = cstF.ap()[:, 0:512]
    nm = cstF.ap()[:, 512:2560]
    biasL = cstF.ap()[:, 2560:2640]
    sc = cstF.ap()[:, 2640:2672]
    cstB = sb("cstB", [128, 384], BF16)
    idP = cstB.ap()[:, 0:128]
    idN = cstB.ap()[:, 128:256]
    inv1024 = cstB.ap()[:, 256:384]
    a_sb = sb("a_sb", [128, 128], BF16)
    b_sb = sb("b_sb", [16, 1024], BF16)
    wd_sb = sb("wd_sb", [128, 8], BF16)
    ld_cols = sb("ld_cols", [128, 16], F32)
    biascols = sb("biascols", [128, 16], F32)
    statsT = sb("statsT", [128, 2048], F32)
    maskb = sb("maskb", [128, 2048], F32)
    xa_bf = sb("xa_bf", [16, 512], BF16)

    statsB = statsT.ap().bitcast(BF16)
    rtF = rt.ap().bitcast(F32)
    stmF = stm.ap().bitcast(F32)

    pb = [psum(f"pb{k}", [128, 512]) for k in range(8)]

    sQg = sem("sQg")
    sQs = sem("sQs")
    sDa = sem("sDa")
    sDb = sem("sDb")
    sDc = sem("sDc")
    sDd = sem("sDd")
    sDe = sem("sDe")
    sDf = sem("sDf")
    sDg = sem("sDg")
    sDh = sem("sDh")
    sDi = sem("sDi")
    sDj = sem("sDj")
    sDk = sem("sDk")
    sDl = sem("sDl")
    sDm = sem("sDm")
    sDn = sem("sDn")
    sT = sem("sT")
    sV = sem("sV")
    sS = sem("sS")
    sG = sem("sG")

    def _dma_track(key, semh, n0, inst):
        inst.then_inc(semh, 16)
        add = 0
        for nmx in nc.inst_names()[n0:]:
            ins2 = nc.find_inst_by_name(nmx)
            si2 = getattr(ins2, "sync_info", None)
            if si2 is None:
                continue
            for u in si2.on_update:
                if u.ant_name == semh.name:
                    add += u.update_value
        assert add >= 16 and add % 16 == 0, (key, add)
        Q[key] += add
        return Q[key]

    class _QIssue:
        def __init__(self, key, semh):
            self.key, self.semh = key, semh

        def __call__(self, eng, dst, src, **kw):
            n0 = nc.n_instructions()
            inst = eng.dma_start(dst, src, **kw)
            return _dma_track(self.key, self.semh, n0, inst)

    qgD = _QIssue("qg", sQg)
    qsD = _QIssue("qs", sQs)
    qaD = _QIssue("da", sDa)
    qbD = _QIssue("db", sDb)
    qcD = _QIssue("dc_", sDc)
    qdD = _QIssue("dd", sDd)

    def slab_q(u):
        return (qaD, sDa) if u % 2 == 0 else (qbD, sDb)

    def store_q(u):
        return (qcD, sDc) if u % 2 == 0 else (qdD, sDd)

    qeD = _QIssue("de", sDe)
    qfD = _QIssue("df", sDf)

    def slab4_q(u):
        return [(qaD, sDa), (qbD, sDb), (qeD, sDe), (qfD, sDf)][u % 4]

    qgD2 = _QIssue("dg", sDg)
    qhD2 = _QIssue("dh", sDh)
    qiD = _QIssue("di", sDi)
    qjD = _QIssue("dj", sDj)
    qkD = _QIssue("dk", sDk)
    qlD = _QIssue("dl", sDl)
    qmD = _QIssue("dm", sDm)
    qnD = _QIssue("dn", sDn)

    def gam4_q(u):
        return [(qiD, sDi), (qjD, sDj), (qkD, sDk), (qlD, sDl)][u % 4]


    def store4_q(u):
        return [(qcD, sDc), (qdD, sDd), (qgD2, sDg), (qhD2, sDh)][u % 4]

    def ti(inst):
        Q["t"] += 1
        inst.then_inc(sT, 1)
        return Q["t"]

    def vi(inst):
        Q["v"] += 1
        inst.then_inc(sV, 1)
        return Q["v"]

    def si(inst):
        Q["s"] += 1
        inst.then_inc(sS, 1)
        return Q["s"]

    def xnp(dc, t0, shift):
        return dc * CHS + shift + t0

    def full_mat_src(t):
        return bass.AP(t.tensor, 0, [[D, 128], [128 * D, 8], [1, D]])

    def tile_src(t, j):
        return bass.AP(t.tensor, j * TS, [[T, 128], [T * 128, 8], [1, TS]])

    def tile_dst(t, j):
        return bass.AP(t.tensor, j * TS, [[T, 128], [T * 128, 8], [1, TS]])

    bigAF = bigA.ap().bitcast(F32)

    def kern_quarter_src(qr):
        return bass.AP(kern.tensor, qr * 2 * 128 * D, [[D, 128], [128 * D, 2], [1, D]])

    def unit_srcs(u):
        if u >= 16:
            dc, hh = divmod(u - 16, 2)
            return [(bass.AP(W1T.tensor, 128 * dc * DFF + hh * 2048,
                             [[DFF, 128], [1, 2048]]), 0, 2048)]
        fc = 2 * u
        return [(bass.AP(W2T.tensor, 128 * fc * D, [[D, 128], [1, 1024]]), 0, 1024),
                (bass.AP(W2T.tensor, 128 * (fc + 1) * D, [[D, 128], [1, 1024]]),
                 1024, 1024)]

    def unit_stores(u):
        if u >= 16:
            dc, hh = divmod(u - 16, 2)
            out = []
            for piece in range(2):
                fc0 = hh * 16 + piece * 8
                out.append((bass.AP(W1qS.tensor, fc0 * 131072 + dc * 128,
                                    [[1024, 128], [131072, 8], [1, 128]]),
                            piece * 1024, 1024))
            return out
        fc = 2 * u
        return [(bass.AP(W2qS.tensor, fc * 128,
                         [[4096, 128], [524288, 8], [1, 128]]), 0, 1024),
                (bass.AP(W2qS.tensor, (fc + 1) * 128,
                         [[4096, 128], [524288, 8], [1, 128]]), 1024, 1024)]

    NU = 32
    THRS = [m + 0.5 for m in range(-7, 7)]

    with nc.Block() as blk:
        v0, s0, t0 = Q["v"], Q["s"], Q["t"]
        qg_c = [0]
        qb_mx = [0]
        qb_gm = [0]
        q_mx = [0] * 4
        q_cm = [0] * 4
        q_gm = [0] * NU
        v_qred = [v0 + qr + 1 for qr in range(4)]
        v_comb = v0 + 5
        v_r0 = v0 + 8
        def v_cmp(ci):
            q, m = divmod(ci, 14)
            return v0 + 8 + 18 * q + m + 1
        v_qev = [v0 + 8 + 18 * q + 18 for q in range(4)]
        v_gred = v0 + 8 + 72 + 2
        s_tok = [s0 + 3 + u + 1 for u in range(NU)]
        def t_grp(q, m):
            return t0 + 56 * q + 4 * (m + 1)

        @blk.gpsimd
        def _(g):
            qgD(g, cstF.ap()[:, 0:2608], cblobF_in)
            qgD(g, cstB[:, :], cblobB_in)
            qgD(g, a_sb[:, :],
                bass.AP(a_m.tensor, 0, [[16, 128], [2048, 8], [1, 16]]))
            qgD(g, b_sb[:, :], b_m)
            qgD(g, wd_sb[:, :], bass.AP(WdT.tensor, 0, [[1, 128], [128, 8]]),
                allow_slow_non_contiguous=True)
            qg_c[0] = qgD(g, sc[:, 0:4], scIn_in)
            for u in range(NU):
                if u == 8:
                    g.wait_ge(sV, v_comb)
                    g.tensor_reduce(sc[0:1, C_WMXR:C_WMXR + 1],
                                    sc[:, C_TMP:C_TMP + 1], axis=AX.C, op=ALU.max)
                    g.drain()
                    tb1 = qgD(g, bncS[0:1, 0:1], sc[0:1, C_WMXR:C_WMXR + 1])
                    g.wait_ge(sQg, tb1)
                    qb_mx[0] = qgD(g, sc[:, C_WMXR:C_WMXR + 1],
                                   bass.AP(bncS.tensor, 0, [[0, 128], [1, 1]]))
                if u >= 4:
                    g.wait_ge(sS, s_tok[u - 4])
                slot = (u % 4) * 2048
                qfn, _sh = gam4_q(u)
                tok = None
                for srcu, so, w in unit_srcs(u):
                    tok = qfn(g, bigA[:, slot + so:slot + so + w], srcu)
                q_gm[u] = tok
            g.wait_ge(sV, v_gred)
            g.tensor_reduce(sc[0:1, C_G12B:C_G12B + 2], sc[:, C_G12:C_G12 + 2],
                            axis=AX.C, op=ALU.add)
            g.drain()
            tb2 = qgD(g, bncS[0:1, 2:4], sc[0:1, C_G12B:C_G12B + 2])
            g.wait_ge(sQg, tb2)
            qb_gm[0] = qgD(g, sc[:, C_G12B:C_G12B + 2],
                           bass.AP(bncS.tensor, 2, [[0, 128], [1, 2]]))
            g.wait_ge(sQg, qb_gm[0])

        xfF = xf.ap().bitcast(F32)

        def kq_slot(qr):
            if qr < 2:
                return stmF[:, qr * 2048:qr * 2048 + 2048]
            return xfF[:, (qr - 2) * 2048:(qr - 2) * 2048 + 2048]

        kq_q = [(qsD, sQs), (qaD, sDa), (qbD, sDb), (qcD, sDc)]

        @blk.sync
        def _(s):
            for qr in range(4):
                q_mx[qr] = kq_q[qr][0](s, kq_slot(qr), kern_quarter_src(qr))

        @blk.scalar
        def _(s):
            s.wait_ge(sQg, qg_c[0])
            si(s.activation(sc[:, C_GAM:C_GAM + 1], sc[:, C_DEC:C_DEC + 1], AF.Sigmoid))
            s.drain()
            si(s.activation(sc[:, C_LG:C_LG + 1], sc[:, C_GAM:C_GAM + 1], AF.Ln))
            s.drain()
            si(s.activation(sc[:, C_ALPHA:C_ALPHA + 1], sc[:, C_LA:C_LA + 1], AF.Exp))
            for u in range(NU):
                s.wait_ge(gam4_q(u)[1], q_gm[u])
                slot = (u % 4) * 2048
                s.drain()
                got = si(s.activation(
                    bigA[:, 8192:8192 + 2048], bigA[:, slot:slot + 2048], AF.Abs,
                    accum_out=statsT[:, u:u + 1]))
                assert got == s_tok[u]

        @blk.vector
        def _(v):
            v.memset(sc[:, C_E5:C_E5 + 1], EPS_LN)
            v.memset(sc[:, C_NH:C_NH + 1], -0.5)
            v.memset(sc[:, C_E8:C_E8 + 1], EPS_LD)
            v.memset(sc[:, C_I32:C_I32 + 1], 1.0 / 32.0)
            v.memset(sc[:, C_Q4:C_Q4 + 1], 0.25)
            v.memset(sc[:, C_HALFP:C_HALFP + 1], 0.5)
            v.wait_ge(sQg, qg_c[0])
            v.tensor_scalar_mul(biasL[:, 48:80], biasL[:, 8:40], 0.5)
            v.drain()
            for qr in range(4):
                v.wait_ge(kq_q[qr][1], q_mx[qr])
                got = vi(v.reduce_max(
                    sc[:, C_WMX + qr:C_WMX + qr + 1],
                    kq_slot(qr),
                    axis=AX.X, apply_absolute_value=True))
                assert got == v_qred[qr]
            v.drain()
            got = vi(v.reduce_max(sc[:, C_TMP:C_TMP + 1], sc[:, C_WMX:C_WMX + 4],
                                  axis=AX.X))
            assert got == v_comb
            v.wait_ge(sQg, qb_mx[0])
            vi(v.reciprocal(sc[:, C_TMP:C_TMP + 1], sc[:, C_WMXR:C_WMXR + 1]))
            v.drain()
            vi(v.tensor_scalar_mul(sc[:, C_R0:C_R0 + 1], sc[:, C_TMP:C_TMP + 1], 7.0))
            got = vi(v.tensor_scalar_mul(sc[:, C_S0:C_S0 + 1],
                                         sc[:, C_WMXR:C_WMXR + 1], 1.0 / 7.0))
            assert got == v_r0
            v.wait_ge(sS, s0 + 2)
            v.tensor_scalar_mul(sc[:, C_NLG:C_NLG + 1], sc[:, C_LG:C_LG + 1], -1.0)
            v.drain()
            for qr in range(4):
                for m in range(14):
                    ci = 14 * qr + m
                    if ci >= 4:
                        q2, m2 = divmod(ci - 4, 14)
                        v.wait_ge(sT, t_grp(q2, m2))
                    cslot = (rt[:, (ci % 4 // 2) * 2048:(ci % 4 // 2) * 2048 + 2048]
                             if ci % 2 == 0 else
                             xbf[:, (ci % 4 // 2) * 2048:(ci % 4 // 2) * 2048 + 2048])
                    got = vi(v.tensor_scalar(
                        cslot, kq_slot(qr),
                        sc[:, C_R0:C_R0 + 1], THRS[m], ALU.mult, ALU.is_gt))
                    assert got == v_cmp(ci)
                v.wait_ge(sT, t_grp(qr, 13))
                for h in range(4):
                    got = vi(v.tensor_scalar(
                        wqA[:, qr * 2048 + h * 512: qr * 2048 + (h + 1) * 512],
                        pb[h][:, :], -7.0, sc[:, C_S0:C_S0 + 1], ALU.add, ALU.mult))
                assert got == v_qev[qr]
            v.wait_ge(sS, s_tok[NU - 1])
            vi(v.reduce_sum(sc[:, C_G12:C_G12 + 1], statsT[:, 16:32], axis=AX.X))
            got = vi(v.reduce_sum(sc[:, C_G12 + 1:C_G12 + 2], statsT[:, 0:16],
                                  axis=AX.X))
            assert got == v_gred
            v.wait_ge(sQg, qb_gm[0])
            v.tensor_scalar(sc[:, C_TMP:C_TMP + 2], sc[:, C_G12B:C_G12B + 2],
                            1.0 / (DFF * D), EPS_BN, ALU.mult, ALU.add)
            v.drain()
            vi(v.reciprocal(sc[:, C_R12:C_R12 + 2], sc[:, C_TMP:C_TMP + 2]))

        @blk.tensor
        def _(t):
            for qr in range(4):
                if qr >= 1:
                    t.wait_ge(sV, v_qev[qr - 1])
                for m in range(14):
                    ci = 14 * qr + m
                    t.wait_ge(sV, v_cmp(ci))
                    cslot = (rt[:, (ci % 4 // 2) * 2048:(ci % 4 // 2) * 2048 + 2048]
                             if ci % 2 == 0 else
                             xbf[:, (ci % 4 // 2) * 2048:(ci % 4 // 2) * 2048 + 2048])
                    got = None
                    for h in range(4):
                        got = ti(t.matmul(pb[h][:, :], idP[:, :],
                                          cslot[:, h * 512:(h + 1) * 512],
                                          start=(m == 0), stop=(m == 13)))
                    assert got == t_grp(qr, m)

    def ln_proj(layer):
        shift = 0 if layer == 0 else 1
        src_t = xT if layer == 0 else x1S
        with nc.Block() as blk:
            v0, s0, t0 = Q["v"], Q["s"], Q["t"]
            qg_snap = Q["qg"]
            v_sq = [v0 + 27 * j + 8 for j in range(NTT)]
            v_mcopy = [v0 + 27 * j + 9 for j in range(NTT)]
            v_var = [v0 + 27 * j + 11 for j in range(NTT)]
            v_end = [v0 + 27 * j + 27 for j in range(NTT)]
            q_x = [0] * NTT
            q_b = [0] * NTT
            q_w = [0] * 3

            plan = []
            if layer == 0:
                plan += [("lnm", 0), ("lns", 0), ("lnm", 1), ("lns", 1)]
                plan += [("wg", gi) for gi in range(16)]
                plan += [("lnm", 2), ("lns", 2)]
                plan += [("g", ec, 0) for ec in range(8)]
                plan += [("lnm", 3), ("lns", 3)]
                plan += [("vtm", tc, et) for tc in range(0, 4) for et in range(2)]
                plan += [("g", ec, 1) for ec in range(8)]
                plan += [("vtm", tc, et) for tc in range(4, 8) for et in range(2)]
                plan += [("g", ec, 2) for ec in range(8)]
                plan += [("g", ec, 3) for ec in range(8)]
                plan += [("vtm", tc, et) for tc in range(8, 16) for et in range(2)]
                plan += [("ld", tt) for tt in range(4)]
            else:
                plan += [("lnm", 0), ("lns", 0), ("lnm", 1), ("lns", 1)]
                plan += [("vtm", tc, et) for tc in range(0, 4) for et in range(2)]
                plan += [("lnm", 2), ("lns", 2)]
                plan += [("vtm", tc, et) for tc in range(4, 8) for et in range(2)]
                plan += [("lnm", 3), ("lns", 3)]
                plan += [("vtm", tc, et) for tc in range(8, 16) for et in range(2)]
            t_tok = {}
            for i, e in enumerate(plan):
                t_tok[e] = t0 + 8 * (i + 1)
            t_m = [t_tok[("lnm", j)] for j in range(NTT)]
            t_s2 = [t_tok[("lns", j)] for j in range(NTT)]
            pseq = [e for e in plan if e[0] not in ("lnm", "lns")]
            pidx = {e: i for i, e in enumerate(pseq)}
            s_seq = []
            for e in plan:
                if e[0] == "lns":
                    s_seq.append(("rstd", e[1]))
                elif e[0] != "lnm":
                    s_seq.append(e)
            s_tok = {}
            cnt = s0
            for e in s_seq:
                cnt += 2 if e[0] == "rstd" else 1
                s_tok[e] = cnt
            s_rstd = [s_tok[("rstd", j)] for j in range(NTT)]

            NUQ = 32
            q_ld = [0] * NUQ
            st_tok = [0] * NUQ
            sg_tok = [0] * NUQ
            vc_tok = [0] * NUQ

            def ld_q(u):
                return [(qbD, sDb), (qeD, sDe), (qfD, sDf), (qmD, sDm)][u % 4]

            def ldslot(u):
                return bigAF[:, (u % 4) * 2048:(u % 4) * 2048 + 2048]

            def qslot_b(u):
                return bigA[:, (u % 4) * 4096:(u % 4) * 4096 + 2048]

            def tslot(u, k):
                if u < 16:
                    return wqA[:, ((u % 2) * 2 + k) * 2048:
                               ((u % 2) * 2 + k) * 2048 + 2048]
                p = (u - 16) % 4
                if p < 2:
                    return wqA[:, (p * 2 + k) * 2048:(p * 2 + k) * 2048 + 2048]
                return stm[:, ((p - 2) * 2 + k) * 2048:((p - 2) * 2 + k) * 2048 + 2048]


            def xslot(j):
                return xf[:, :]

            xfB = xf.ap().bitcast(BF16)

            def bslot(j):
                if j == 0:
                    return xbf[:, 0:4096]
                if j == 1:
                    return rt[:, 0:4096]
                return xfB[:, (j - 2) * 4096:(j - 2) * 4096 + 4096]

            def sqslot(j):
                return stm[:, (j % 2) * 4096:(j % 2) * 4096 + 4096]

            def mslot(j):
                return statsT[:, (j % 2) * 512:(j % 2) * 512 + 512]

            def rslot(j):
                return statsB[:, 2048 + (j % 2) * 512:2048 + (j % 2) * 512 + 512]

            maskbB = maskb.ap().bitcast(BF16)

            def nscr(j):
                return maskbB[:, 3072 + (j % 2) * 512:3072 + (j % 2) * 512 + 512]

            def varslot(j):
                return maskb[:, (j % 2) * 512:(j % 2) * 512 + 512]

            def scrslot(j):
                return maskb[:, 1024 + (j % 2) * 512:1024 + (j % 2) * 512 + 512]

            def pj_dst(e):
                if e[0] == "wg":
                    dc, et = divmod(e[1], 2)
                    return bigB[:, 8192 + dc * 1024 + et * 512:
                                8192 + dc * 1024 + (et + 1) * 512]
                if e[0] == "g":
                    ec, tt = e[1], e[2]
                    return bigA[:, ec * 2048 + tt * 512: ec * 2048 + (tt + 1) * 512]
                if e[0] == "vtm":
                    tc, et = e[1], e[2]
                    return bigB[:, tc * 1024 + et * 512: tc * 1024 + (et + 1) * 512]
                return None

            @blk.gpsimd
            def _(g):
                if shift == 1:
                    g.wait_ge(sQg, qg_snap)
                    g.wait_ge(sDi, Q["di"])
                    g.wait_ge(sDj, Q["dj"])
                    g.wait_ge(sDk, Q["dk"])
                    g.wait_ge(sDl, Q["dl"])
                qfn0 = [qiD, qjD]
                q_b[0] = qfn0[0](g, bslot(0), tile_src(src_t, 0))
                q_b[1] = qfn0[1](g, bslot(1), tile_src(src_t, 1))
                if layer == 0:
                    qgD(g, bigB[:, 0:8192], full_mat_src(Wq))
                    q_w[0] = qgD(g, projW[:, :], full_mat_src(Wk))
                else:
                    q_w[0] = qgD(g, bigA[:, 0:8192], full_mat_src(WwrT))
                    q_w[1] = qnD(g, projW[:, :], full_mat_src(WrdT))
                    q_w[2] = qlD(g, nm[:, :], nmB)
                for j in (1, 2, 3):
                    if j < 3:
                        jj = j + 1
                        g.wait_ge((sDi if jj % 2 == 0 else sDj), q_b[jj - 2])
                        q_b[jj] = qfn0[jj % 2](g, bslot(jj), tile_src(src_t, jj))
                if layer == 0:
                    g.wait_ge(sT, t_tok[("wg", 15)])
                    q_w[1] = qgD(g, projW[:, :], full_mat_src(WvT))
                    g.wait_ge(sS, s_tok[("ld", 3)] + 1)
                    tb = qgD(g, ldS[0:1, :], stmF[0:1, 0:2048])
                    g.wait_ge(sQg, tb)
                    tb = qgD(g, ld_cols[:, :],
                             bass.AP(ldS.tensor, 0, [[1, 128], [128, 16]]),
                             allow_slow_non_contiguous=True)
                    g.wait_ge(sQg, tb)
                else:
                    g.wait_ge(sDn, q_w[1])
                    g.wait_ge(sDl, q_w[2])

            @blk.sync
            def _(s):
                return

            @blk.tensor
            def _(t):
                np_ = 0
                for e in plan:
                    if e[0] == "lnm":
                        j = e[1]
                        t.wait_ge((sDi if j % 2 == 0 else sDj), q_b[j])
                        if j >= 2:
                            t.wait_ge(sV, v_mcopy[j - 2])
                        bs = bslot(j)
                        got = None
                        for dc in range(NDC):
                            got = ti(t.matmul(pb[(j % 2) * 2][:, :], inv1024[:, :],
                                              bs[:, dc * 512:(dc + 1) * 512],
                                              start=(dc == 0), stop=(dc == NDC - 1)))
                        assert got == t_m[j]
                        continue
                    if e[0] == "lns":
                        j = e[1]
                        t.wait_ge(sV, v_sq[j])
                        if j >= 2:
                            t.wait_ge(sV, v_var[j - 2])
                        sq = sqslot(j)
                        got = None
                        for dc in range(NDC):
                            got = ti(t.matmul(pb[(j % 2) * 2 + 1][:, :], inv1024[:, :],
                                              sq[:, dc * 512:(dc + 1) * 512],
                                              start=(dc == 0), stop=(dc == NDC - 1)))
                        assert got == t_s2[j]
                        continue
                    p = pidx[e]
                    if p >= 4:
                        t.wait_ge(sS, s_tok[pseq[p - 4]])
                    bank = pb[4 + p % 4]
                    if e[0] == "wg":
                        dc, et = divmod(e[1], 2)
                        t.wait_ge(sQg, q_w[0])
                        got = None
                        for fc in range(NDC):
                            got = ti(t.matmul(
                                bank[:, :],
                                bigB[:, fc * 1024 + dc * 128: fc * 1024 + (dc + 1) * 128],
                                projW[:, fc * 1024 + et * 512: fc * 1024 + (et + 1) * 512],
                                start=(fc == 0), stop=(fc == NDC - 1)))
                    elif e[0] == "g":
                        ec, tt = e[1], e[2]
                        t.wait_ge(sV, v_end[tt])
                        t.wait_ge(sS, s_tok[("wg", 15)])
                        got = None
                        for dc in range(NDC):
                            p0 = xnp(dc, tt * TS, 0)
                            got = ti(t.matmul(
                                bank[:, :],
                                bigB[:, 8192 + dc * 1024 + ec * 128:
                                     8192 + dc * 1024 + (ec + 1) * 128],
                                xn[:, p0:p0 + TS],
                                start=(dc == 0), stop=(dc == NDC - 1)))
                    elif e[0] == "vtm":
                        tc, et = e[1], e[2]
                        t.wait_ge(sV, v_end[tc // 4])
                        if layer == 0:
                            t.wait_ge(sQg, q_w[1])
                            wsrc = projW
                        else:
                            t.wait_ge(sQg, q_w[0])
                            wsrc = bigA
                        got = None
                        for dc in range(NDC):
                            p0 = xnp(dc, tc * 128, shift)
                            got = ti(t.matmul(
                                bank[:, :], xn[:, p0:p0 + 128],
                                wsrc[:, dc * 1024 + et * 512: dc * 1024 + (et + 1) * 512],
                                start=(dc == 0), stop=(dc == NDC - 1)))
                    else:
                        tt = e[1]
                        t.wait_ge(sV, v_end[tt])
                        got = None
                        for dc in range(NDC):
                            p0 = xnp(dc, tt * TS, 0)
                            got = ti(t.matmul(bank[0:1, :], wd_sb[:, dc:dc + 1],
                                              xn[:, p0:p0 + TS],
                                              start=(dc == 0), stop=(dc == NDC - 1)))
                    assert got == t_tok[e]

            @blk.vector
            def _(v):
                if shift == 1:
                    for dc in range(NDC):
                        v.memset(xn[:, dc * CHS:dc * CHS + 1], 0.0)
                for j in range(NTT):
                    if j >= 2:
                        v.wait_ge(sT, t_s2[j - 2])
                    v.wait_ge((sDi if j % 2 == 0 else sDj), q_b[j])
                    bs, sq = bslot(j), sqslot(j)
                    got = None
                    for dc in range(NDC):
                        got = vi(v.tensor_mul(sq[:, dc * 512:(dc + 1) * 512],
                                              bs[:, dc * 512:(dc + 1) * 512],
                                              bs[:, dc * 512:(dc + 1) * 512]))
                    assert got == v_sq[j], ('vsq', j, got, v_sq)
                    v.wait_ge(sT, t_m[j])
                    got = vi(v.tensor_copy(mslot(j), pb[(j % 2) * 2][:, :]))
                    assert got == v_mcopy[j]
                    v.drain()
                    vi(v.tensor_mul(scrslot(j), mslot(j), mslot(j)))
                    v.drain()
                    v.wait_ge(sT, t_s2[j])
                    got = vi(v.tensor_sub(varslot(j), pb[(j % 2) * 2 + 1][:, :],
                                          scrslot(j)))
                    assert got == v_var[j]
                    v.wait_ge(sS, s_rstd[j])
                    bs = bslot(j)
                    for dc in range(NDC):
                        v.drain()
                        vi(v.tensor_sub(nscr(j), bs[:, dc * 512:(dc + 1) * 512],
                                        mslot(j)))
                        v.drain()
                        p0 = xnp(dc, j * TS, shift)
                        got = vi(v.tensor_mul(xn[:, p0:p0 + TS], nscr(j), rslot(j)))
                    assert got == v_end[j]

            @blk.scalar
            def _(s):
                for e in s_seq:
                    if e[0] == "rstd":
                        j = e[1]
                        s.wait_ge(sV, v_var[j])
                        s.drain()
                        si(s.activation(scrslot(j), varslot(j), AF.Ln,
                                        bias=sc[:, C_E5:C_E5 + 1]))
                        s.drain()
                        got = si(s.activation(rslot(j), scrslot(j), AF.Exp,
                                              scale=sc[:, C_NH:C_NH + 1]))
                        assert got == s_rstd[j]
                        continue
                    p = pidx[e]
                    bank = pb[4 + p % 4]
                    s.wait_ge(sT, t_tok[e])
                    s.drain()
                    if e[0] == "wg":
                        got = si(s.activation(pj_dst(e), bank[:, :], AF.Copy,
                                              scale=sc[:, C_I32:C_I32 + 1]))
                    elif e[0] in ("g", "vtm"):
                        got = si(s.copy(pj_dst(e), bank[:, :]))
                    else:
                        tt = e[1]
                        got = si(s.activation(statsT[0:1, tt * 512:(tt + 1) * 512],
                                              bank[0:1, :], AF.Sigmoid,
                                              bias=sc[0:1, C_BD:C_BD + 1]))
                    assert got == s_tok[e]
                if layer == 0:
                    s.drain()
                    si(s.activation(stmF[0:1, 0:2048], statsT[0:1, 0:2048],
                                    AF.Ln, bias=sc[0:1, C_E8:C_E8 + 1]))


    def PROJ0():
        ln_proj(0)

    def PROJ1():
        ln_proj(1)
        with nc.Block() as blk:
            @blk.gpsimd
            def _(g):
                for semh, key in ((sDb, "db"), (sDe, "de"), (sDf, "df"),
                                  (sDm, "dm"), (sDc, "dc_"), (sDd, "dd"),
                                  (sDg, "dg"), (sDh, "dh")):
                    g.wait_ge(semh, Q[key])

    PROJ0()

    QCARRY = {}

    def attn_tile(layer, j, blk, prev):
        shift = 0 if layer == 0 else 1
        if layer == 0:
            schunks = list(range(4 * j, NSC))

            def strad(i):
                return i < 4 * j + 4

            def cc(i):
                return float(128 * i - 512 * j - 1)
        else:
            schunks = list(range(0, 4 * j + 4))

            def strad(i):
                return i >= 4 * j

            def cc(i):
                return float(512 * j - 128 * i - 1)
        NS = len(schunks)

        v0, s0, t0 = Q["v"], Q["s"], Q["t"]
        prefA = []
        tot = 0
        for i in schunks:
            tot += 1 + (1 if strad(i) else 0)
            prefA.append(v0 + tot)
        totalA = tot
        if layer == 0:
            QUC = [6, 9, 9, 8]
            u0 = sum(QUC[:j])
            units_j = list(range(u0, u0 + QUC[j]))
            NH = 2 * len(units_j)
            NSL = [2, 3, 4, 4][j]
            anchors = ([("st", idx) for idx in range(NS)] + [("xa", 0)]
                       + [("R", ec) for ec in range(8)]
                       + [("res", dc) for dc in range(8)])
            NA = len(anchors)
            vplan = []
            qdone = 0
            for a, anc in enumerate(anchors):
                vplan.append(anc)
                want = min(NH, (NH * (a - 3)) // (NA - 7)) if a >= 4 else 0
                while qdone < want:
                    vplan.append(("q", qdone))
                    qdone += 1
            while qdone < NH:
                vplan.append(("q", qdone))
                qdone += 1
            vcnt = v0 + totalA
            v_mult = [0] * NS
            v_R = [0] * 8
            v_res = [0] * 8
            cmb_tok = [0] * NH
            v_xa = 0
            for e in vplan:
                vcnt += 1
                if e[0] == "q":
                    cmb_tok[e[1]] = vcnt
                elif e[0] == "st":
                    v_mult[e[1]] = vcnt
                elif e[0] == "xa":
                    v_xa = vcnt
                elif e[0] == "R":
                    v_R[e[1]] = vcnt
                else:
                    v_res[e[1]] = vcnt
            splan = []
            for e in vplan:
                if e[0] == "st":
                    splan.append(("m", e[1]))
                elif e[0] == "q":
                    splan.append(("g", e[1]))
            scnt = s0
            s_mask = [0] * NS
            sg_tok = [0] * NH
            for e in splan:
                if e[0] == "m":
                    scnt += 1
                    s_mask[e[1]] = scnt
                else:
                    scnt += 2
                    sg_tok[e[1]] = scnt
            t_xa = t0 + 8
            t_st = [t_xa + 8 * (idx + 1) for idx in range(NS)]
            t_R = [t_st[NS - 1] + NS * (ec + 1) for ec in range(8)]
            t_res = [t_R[7] + 17 * (dc + 1) for dc in range(8)]
            xbfF = xbf.ap().bitcast(F32)
            stmF2 = stm.ap().bitcast(F32)

            def qldF(h):
                s_ = h % NSL
                if s_ < 2:
                    return xbfF[:, s_ * 1024:s_ * 1024 + 1024]
                return stmF2[:, (s_ - 2) * 1024:(s_ - 2) * 1024 + 1024]

            def qout(h):
                s_ = h % NSL
                if s_ < 2:
                    return xbf[:, s_ * 2048:s_ * 2048 + 1024]
                return stm[:, (s_ - 2) * 2048:(s_ - 2) * 2048 + 1024]

            def qt1c(h):
                return statsB[:, (h % 2) * 2048:(h % 2) * 2048 + 1024]

            def qt2c(h):
                return statsB[:, (h % 2) * 2048 + 1024:(h % 2) * 2048 + 2048]

            def q_src(h):
                u = units_j[h // 2]
                half = h % 2
                if u >= 16:
                    dcq, hh = divmod(u - 16, 2)
                    return bass.AP(W1T.tensor,
                                   128 * dcq * DFF + hh * 2048 + half * 1024,
                                   [[DFF, 128], [1, 1024]])
                fc = 2 * u + half
                return bass.AP(W2T.tensor, 128 * fc * D, [[D, 128], [1, 1024]])

            def q_dst(h):
                u = units_j[h // 2]
                half = h % 2
                if u >= 16:
                    dcq, hh = divmod(u - 16, 2)
                    fc0 = hh * 16 + half * 8
                    return bass.AP(W1qS.tensor, fc0 * 131072 + dcq * 128,
                                   [[1024, 128], [131072, 8], [1, 128]])
                fc = 2 * u + half
                return bass.AP(W2qS.tensor, fc * 128,
                               [[4096, 128], [524288, 8], [1, 128]])

            def qld_q(h):
                return [(qbD, sDb), (qeD, sDe), (qfD, sDf), (qmD, sDm)][h % 4]

            def qst_q(h):
                return [(qiD, sDi), (qjD, sDj), (qkD, sDk), (qlD, sDl)][h % 4]

            qst_keys = ["di", "dj", "dk", "dl"]
            exp_qst = [Q[qst_keys[h % 4]] + 16 * (h // 4 + 1) for h in range(NH)]
        else:
            v_mult = [v0 + totalA + idx + 1 for idx in range(NS)]
            vp = v0 + totalA + NS
            v_R = [vp + ec + 1 for ec in range(8)]
            v_res = [vp + 8 + dc + 1 for dc in range(8)]
            t_st = [t0 + 8 * (idx + 1) for idx in range(NS)]
            t_R = [t_st[NS - 1] + NS * (ec + 1) for ec in range(8)]
            t_h = [t_R[7] + 8 * (fc + 1) for fc in range(NFC)]
            t_res = [t_h[NFC - 1] + 40 * (dc + 1) for dc in range(8)]
        if layer == 1:
            s_mask = [s0 + idx + 1 for idx in range(NS)]
            s_h = [s0 + NS + fc + 1 for fc in range(NFC)]

        q_x = [0]
        q_w1 = [0] * NFC
        q_w2 = [0] * 8
        q_qld = [0] * 64
        q_qst = [0] * 64
        if layer == 1:
            w2q_q = [(qiD, sDi), (qjD, sDj)]
            exp_w2 = [Q[["di", "dj"][dc % 2]] + 16 * (dc // 2 + 1)
                      for dc in range(8)]

        @blk.sync
        def _(s):
            s.wait_ge(sQg, Q["qg"])
            src = xT if layer == 0 else x1S
            q_x[0] = qsD(s, xf[:, :], tile_src(src, j))
            if layer == 0:
                if j > 0:
                    for kq, tokq in QCARRY["fin"]:
                        s.wait_ge(kq, tokq)
                stm_gd = [False]

                def qld_issue(h):
                    if prev and not stm_gd[0] and (h % NSL) >= 2:
                        s.wait_ge(sT, prev["t_R"][7])
                        stm_gd[0] = True
                    q_qld[h] = qld_q(h)[0](s, qldF(h), q_src(h))

                for h in range(min(NSL, NH)):
                    qld_issue(h)
                for h in range(NSL, NH):
                    s.wait_ge(qst_q(h - NSL)[1], exp_qst[h - NSL])
                    qld_issue(h)
                fin = {}
                for h in range(NH):
                    fin[qst_q(h)[1].name] = (qst_q(h)[1], exp_qst[h])
                QCARRY["fin"] = list(fin.values())
            if layer == 1:
                if j == 0:
                    s.wait_ge(sDi, Q["di"])
                    s.wait_ge(sDj, Q["dj"])
                    s.wait_ge(sDk, Q["dk"])
                    s.wait_ge(sDl, Q["dl"])
                for fc in range(NFC):
                    if fc >= 4:
                        s.wait_ge(sT, t_h[fc - 4])
                    elif prev:
                        s.wait_ge(sT, prev["t_h"][28 + fc])
                    qfn, _sh = slab4_q(fc)
                    q_w1[fc] = qfn(s,
                        xbf[:, (fc % 4) * 1024:(fc % 4) * 1024 + 1024],
                        bass.AP(W1qS.tensor, fc * 128 * 1024,
                                [[1024, 128], [1, 1024]]))

        @blk.vector
        def _(v):
            if prev:
                v.wait_ge(sS, prev["s_last"] if layer == 0
                          else prev["s_mask"][-1])
                v.wait_ge(sT, prev["t_R"][7])
            for idx, i in enumerate(schunks):
                scl = ld_cols[:, i:i + 1] if layer == 0 else sc[:, C_LG:C_LG + 1]
                vi(v.tensor_scalar_mul(biascols[:, i:i + 1], scl, cc(i)))
                if strad(i):
                    k = i - 4 * j
                    escl = (ld_cols[:, i:i + 1] if layer == 0
                            else sc[:, C_NLG:C_NLG + 1])
                    vi(v.scalar_tensor_tensor(
                        statsB[:, k * 512:(k + 1) * 512], Bcs[:, :], escl,
                        nm[:, k * 512:(k + 1) * 512], ALU.mult, ALU.add))
            assert Q["v"] == v0 + totalA
            if layer == 0:
                for e in vplan:
                    if e[0] == "q":
                        h = e[1]
                        v.wait_ge(sS, sg_tok[h])
                        got = vi(v.tensor_add(qout(h), qt1c(h), qt2c(h)))
                        assert got == cmb_tok[h]
                    elif e[0] == "st":
                        idx = e[1]
                        i = schunks[idx]
                        v.wait_ge(sT, t_st[idx])
                        v.wait_ge(sS, s_mask[idx])
                        got = vi(v.tensor_mul(
                            stm[:, i * 512:(i + 1) * 512], pb[idx % 4][:, :],
                            maskb[:, (idx % 4) * 512:(idx % 4) * 512 + 512]))
                        assert got == v_mult[idx]
                    elif e[0] == "xa":
                        v.wait_ge(sT, t_xa)
                        got = vi(v.tensor_copy(xa_bf[:, :], pb[4][0:16, :]))
                        assert got == v_xa
                    elif e[0] == "R":
                        ec = e[1]
                        v.wait_ge(sT, t_R[ec])
                        got = vi(v.tensor_scalar_mul(
                            rt[:, ec * 512:(ec + 1) * 512],
                            pb[4 + ec % 4][:, :], sc[:, C_OS:C_OS + 1]))
                        assert got == v_R[ec]
                    else:
                        dc = e[1]
                        v.wait_ge(sT, t_res[dc])
                        v.wait_ge(sQs, q_x[0])
                        got = vi(v.scalar_tensor_tensor(
                            xf[:, dc * 512:(dc + 1) * 512], pb[dc % 2][:, :],
                            biasL[:, dc:dc + 1],
                            xf[:, dc * 512:(dc + 1) * 512], ALU.add, ALU.add))
                        assert got == v_res[dc]
            else:
                for idx, i in enumerate(schunks):
                    v.wait_ge(sT, t_st[idx])
                    v.wait_ge(sS, s_mask[idx])
                    got = vi(v.tensor_mul(
                        stm[:, i * 512:(i + 1) * 512], pb[idx % 4][:, :],
                        maskb[:, (idx % 4) * 512:(idx % 4) * 512 + 512]))
                    assert got == v_mult[idx]
                for ec in range(8):
                    v.wait_ge(sT, t_R[ec])
                    got = vi(v.tensor_scalar_mul(
                        rt[:, ec * 512:(ec + 1) * 512], pb[4 + ec % 4][:, :],
                        sc[:, C_ALPHA:C_ALPHA + 1]))
                    assert got == v_R[ec]
                for dc in range(8):
                    v.wait_ge(sT, t_res[dc])
                    v.wait_ge(sQs, q_x[0])
                    got = vi(v.scalar_tensor_tensor(
                        xf[:, dc * 512:(dc + 1) * 512], pb[2 + dc % 2][:, :],
                        biasL[:, 40 + dc:41 + dc],
                        xf[:, dc * 512:(dc + 1) * 512], ALU.add, ALU.add))
                    assert got == v_res[dc]

        def emit_mask(s, idx):
            i = schunks[idx]
            s.wait_ge(sV, prefA[idx])
            if idx >= 4:
                s.wait_ge(sV, v_mult[idx - 4])
            elif prev:
                s.wait_ge(sV, prev["v_mult"][prev["NS"] - 4 + idx])
            mslot = maskb[:, (idx % 4) * 512:(idx % 4) * 512 + 512]
            if strad(i):
                k = i - 4 * j
                got = si(s.activation(mslot, statsB[:, k * 512:(k + 1) * 512],
                                      AF.Exp, bias=biascols[:, i:i + 1]))
            else:
                scl = (ld_cols[:, i:i + 1] if layer == 0
                       else sc[:, C_NLG:C_NLG + 1])
                got = si(s.activation(mslot, Bcs[:, :], AF.Exp,
                                      bias=biascols[:, i:i + 1], scale=scl))
            assert got == s_mask[idx]

        @blk.scalar
        def _(s):
            if layer == 0:
                for e in splan:
                    if e[0] == "m":
                        emit_mask(s, e[1])
                        continue
                    h = e[1]
                    s.wait_ge(qld_q(h)[1], q_qld[h])
                    if h >= 2:
                        s.wait_ge(sV, cmb_tok[h - 2])
                    elif prev and prev["NH"]:
                        s.wait_ge(sV, prev["cmb"][prev["NH"] - 2 + h])
                    u = units_j[h // 2]
                    rcol = (sc[:, C_R12:C_R12 + 1] if u >= 16
                            else sc[:, C_R12 + 1:C_R12 + 2])
                    s.drain()
                    si(s.activation(qt1c(h), qldF(h), AF.Sign,
                                    scale=rcol, bias=sc[:, C_NH:C_NH + 1]))
                    got = si(s.activation(qt2c(h), qldF(h), AF.Sign,
                                          scale=rcol,
                                          bias=sc[:, C_HALFP:C_HALFP + 1]))
                    assert got == sg_tok[h]
            else:
                for idx in range(NS):
                    emit_mask(s, idx)
            if layer == 1:
                for fc in range(NFC):
                    s.wait_ge(sT, t_h[fc])
                    got = si(s.activation(
                        bigA[:, fc * 512:(fc + 1) * 512], pb[fc % 2][:, :],
                        AF.Relu, bias=biasL[:, 48 + fc:49 + fc],
                        scale=sc[:, C_Q4:C_Q4 + 1]))
                    assert got == s_h[fc]

        @blk.tensor
        def _(t):
            if layer == 0:
                if prev:
                    t.wait_ge(sV, prev["v_R"][4])
                got = None
                for dc in range(NDC):
                    p0 = xnp(dc, j * TS, 0)
                    got = ti(t.matmul(pb[4][0:16, :], a_sb[:, dc * 16:(dc + 1) * 16],
                                      xn[:, p0:p0 + TS],
                                      start=(dc == 0), stop=(dc == NDC - 1)))
                assert got == t_xa
            for idx, i in enumerate(schunks):
                if idx >= 4:
                    t.wait_ge(sV, v_mult[idx - 4])
                elif prev:
                    if layer == 0:
                        pw = [("v", prev["v_res"][6]), ("v", prev["v_res"][7]),
                              ("v", prev["v_mult"][prev["NS"] - 2]),
                              ("v", prev["v_mult"][prev["NS"] - 1])][idx]
                    else:
                        pw = [("s", prev["s_h"][30]), ("s", prev["s_h"][31]),
                              ("v", prev["v_res"][6]),
                              ("v", prev["v_res"][7])][idx]
                    t.wait_ge(sV if pw[0] == "v" else sS, pw[1])
                bank = pb[idx % 4]
                got = None
                for ec in range(NDC):
                    if layer == 0:
                        p0 = xnp(ec, i * 128, 0)
                        lhs = xn[:, p0:p0 + 128]
                        rhs = bigA[:, ec * 2048 + j * 512: ec * 2048 + j * 512 + 512]
                    else:
                        lhs = xn[:, ec * CHS + i * 128: ec * CHS + i * 128 + 128]
                        rhs = xn[:, ec * CHS + 1 + j * 512: ec * CHS + 1 + j * 512 + 512]
                    got = ti(t.matmul(bank[:, :], lhs, rhs,
                                      start=(ec == 0), stop=(ec == NDC - 1)))
                assert got == t_st[idx]
            for ec in range(8):
                if ec == 0:
                    t.wait_ge(sV, v_mult[NS - 1])
                    if layer == 0:
                        t.wait_ge(sV, v_xa)
                if ec >= 4:
                    t.wait_ge(sV, v_R[ec - 4])
                elif prev:
                    t.wait_ge(sV, prev["v_R"][ec + 4])
                bank = pb[4 + ec % 4]
                got = None
                for n, i in enumerate(schunks):
                    got = ti(t.matmul(
                        bank[:, :],
                        bigB[:, i * 1024 + ec * 128: i * 1024 + (ec + 1) * 128],
                        stm[:, i * 512:(i + 1) * 512],
                        start=(n == 0), stop=(n == NS - 1)))
                assert got == t_R[ec]
            if layer == 1:
                for fc in range(NFC):
                    t.wait_ge(slab4_q(fc)[1], q_w1[fc])
                    if fc == 0:
                        t.wait_ge(sV, v_mult[NS - 1])
                    if fc >= 2:
                        t.wait_ge(sS, s_h[fc - 2])
                    bank = pb[fc % 2]
                    got = None
                    for dc in range(NDC):
                        p0 = xnp(dc, j * TS, 1)
                        got = ti(t.matmul(
                            bank[:, :],
                            xbf[:, (fc % 4) * 1024 + dc * 128:
                                 (fc % 4) * 1024 + (dc + 1) * 128],
                            xn[:, p0:p0 + TS],
                            start=(dc == 0), stop=(dc == NDC - 1)))
                    assert got == t_h[fc]
            rb = 0 if layer == 0 else 2
            for dc in range(8):
                if dc == 0:
                    t.wait_ge(sV, v_mult[NS - 1])
                    t.wait_ge(sV, v_R[7])
                    if layer == 1:
                        t.wait_ge(sS, s_h[NFC - 1])
                if dc >= 2:
                    t.wait_ge(sV, v_res[dc - 2])
                if layer == 1:
                    t.wait_ge(w2q_q[dc % 2][1], exp_w2[dc])
                bank = pb[rb + dc % 2]
                first = True
                got = None
                for ec in range(8):
                    got = ti(t.matmul(
                        bank[:, :],
                        projW[:, ec * 1024 + dc * 128: ec * 1024 + (dc + 1) * 128],
                        rt[:, ec * 512:(ec + 1) * 512],
                        start=first, stop=False))
                    first = False
                if layer == 0:
                    for d2 in range(8):
                        p0 = xnp(d2, j * TS, 0)
                        got = ti(t.matmul(
                            bank[:, :],
                            wqA[:, d2 * 1024 + dc * 128: d2 * 1024 + (dc + 1) * 128],
                            xn[:, p0:p0 + TS], start=False, stop=False))
                    got = ti(t.matmul(bank[:, :],
                                      b_sb[0:16, dc * 128:(dc + 1) * 128],
                                      xa_bf[0:16, :], start=False, stop=True))
                else:
                    for fc in range(NFC):
                        got = ti(t.matmul(
                            bank[:, :],
                            wqA[:, (dc % 2) * 4096 + fc * 128:
                                 (dc % 2) * 4096 + (fc + 1) * 128],
                            bigA[:, fc * 512:(fc + 1) * 512],
                            start=False, stop=(fc == NFC - 1)))
                assert got == t_res[dc]

        @blk.gpsimd
        def _(g):
            if layer == 1:
                for dc in range(8):
                    if dc >= 2:
                        g.wait_ge(sT, t_res[dc - 2])
                    elif prev:
                        g.wait_ge(sT, prev["t_res"][6 + dc])
                    got = w2q_q[dc % 2][0](g,
                        wqA[:, (dc % 2) * 4096:(dc % 2) * 4096 + 4096],
                        bass.AP(W2qS.tensor, dc * 128 * 4096,
                                [[4096, 128], [1, 4096]]))
                    assert got == exp_w2[dc], (dc, got, exp_w2[dc])
            if layer == 0:
                for h in range(NH):
                    if h >= 4:
                        g.wait_ge(qst_q(h - 4)[1], q_qst[h - 4])
                    g.wait_ge(sV, cmb_tok[h])
                    q_qst[h] = qst_q(h)[0](g, q_dst(h), qout(h))
                    assert q_qst[h] == exp_qst[h], (h, q_qst[h], exp_qst[h])
            g.wait_ge(sV, v_res[7])
            dst = x1S if layer == 0 else outT
            qgD(g, tile_dst(dst, j), xf[:, :])

        return dict(v_R=v_R, v_res=v_res, v_mult=v_mult,
                    v_xa=(v_xa if layer == 0 else 0),
                    s_mask=s_mask, s_h=(s_h if layer == 1 else None),
                    s_last=(scnt if layer == 0 else s_h[NFC - 1]),
                    cmb=(cmb_tok if layer == 0 else None),
                    NH=(NH if layer == 0 else 0),
                    t_R=t_R, t_res=t_res,
                    t_h=(t_h if layer == 1 else None), NS=NS)

    for jp in ((0, 1, 2, 3),):
        with nc.Block() as blk:
            prevt = None
            for j in jp:
                prevt = attn_tile(0, j, blk, prevt)

    PROJ1()

    for jp in ((0, 1, 2, 3),):
        with nc.Block() as blk:
            prevt = None
            for j in jp:
                prevt = attn_tile(1, j, blk, prevt)

    with nc.Block() as blk:
        @blk.gpsimd
        def _(g):
            g.wait_ge(sQg, Q["qg"])
            g.wait_ge(sQs, Q["qs"])

    for cm in reversed(ctx):
        cm.__exit__(None, None, None)
    return nc


def _host_consts():
    p = np.arange(128, dtype=np.float32)[:, None]
    jj = np.arange(512, dtype=np.float32)[None, :]
    Bc = (p - jj).astype(np.float32)
    nmA = np.empty((128, 2048), dtype=np.float32)
    nmB = np.empty((128, 2048), dtype=np.float32)
    for k in range(4):
        vA = np.where(p - jj + 128 * k - 1 >= 0, 0.0, NEG).astype(np.float32)
        vB = np.where(jj - p - 128 * k - 1 >= 0, 0.0, NEG).astype(np.float32)
        nmA[:, k * 512:(k + 1) * 512] = vA
        nmB[:, k * 512:(k + 1) * 512] = vB
    idP = np.eye(128, dtype=np.float32).astype(ml_dtypes.bfloat16)
    idN = (-np.eye(128, dtype=np.float32)).astype(ml_dtypes.bfloat16)
    inv1024 = np.full((128, 128), 1.0 / 1024.0, dtype=np.float32).astype(ml_dtypes.bfloat16)
    return Bc, nmA, nmB, idP, idN, inv1024


def kernel(L0_attn_x, L0_attn_Wq, L0_attn_Wk, L0_attn_Wv, L0_attn_Wo,
           L0_attn_Wd, L0_attn_bd, L0_attn_out_scale, L0_ffn_kernel,
           L0_ffn_bias, L0_ffn_a, L0_ffn_b, L1_attn_W_write, L1_attn_W_read,
           L1_attn_decay, L1_attn_log_alpha, L1_ffn_W1, L1_ffn_b1, L1_ffn_W2,
           L1_ffn_b2):
    f = np.float32

    def c(x):
        return np.ascontiguousarray(np.asarray(x, dtype=f))

    x = c(L0_attn_x)
    Bc, nmA, nmB, idP, idN, inv1024 = _host_consts()

    def rep(v):
        return np.full((128, 1), float(np.asarray(v, dtype=f).reshape(-1)[0]), dtype=f)

    cblobF = np.concatenate(
        [Bc, nmA,
         c(np.asarray(L0_ffn_bias).reshape(8, 128).T),
         c(np.asarray(L1_ffn_b1).reshape(32, 128).T),
         c(np.asarray(L1_ffn_b2).reshape(8, 128).T)], axis=1)
    scIn = np.concatenate(
        [rep(L0_attn_bd), rep(L0_attn_out_scale),
         rep(L1_attn_decay), rep(L1_attn_log_alpha)], axis=1)
    cblobB = np.concatenate([idP, idN, inv1024], axis=1)
    shared = {
        "Wq": c(L0_attn_Wq),
        "Wk": c(L0_attn_Wk),
        "WvT": c(np.asarray(L0_attn_Wv).T),
        "WdT": c(np.asarray(L0_attn_Wd).reshape(1, D).T),
        "WoT": c(np.asarray(L0_attn_Wo).T),
        "kern": c(L0_ffn_kernel),
        "a_m": c(L0_ffn_a),
        "b_m": c(L0_ffn_b),
        "WwrT": c(np.asarray(L1_attn_W_write).T),
        "WrdT": c(np.asarray(L1_attn_W_read).T),
        "W1T": c(np.asarray(L1_ffn_W1).T),
        "W2T": c(np.asarray(L1_ffn_W2).T),
        "cblobF": cblobF,
        "scIn": scIn,
        "nmB": nmB,
        "cblobB": cblobB,
    }

    n = x.shape[0]
    in_maps = []
    for ci in range(n):
        m = dict(shared)
        m["xT"] = np.ascontiguousarray(x[ci].T)
        in_maps.append(m)

    nc = build_nc()
    res = run_bass_kernel_spmd(nc, in_maps, core_ids=list(range(n)))
    out = np.stack([np.ascontiguousarray(np.asarray(r["outT"]).T)
                    for r in res.results], axis=0)
    return out.astype(np.float32)


# version 27
# speedup vs baseline: 1.0770x; 1.0103x over previous
import numpy as np
import ml_dtypes
import concourse.bass as bass
import concourse.mybir as mybir
from concourse.bass_utils import run_bass_kernel_spmd


F32 = mybir.dt.float32
BF16 = mybir.dt.bfloat16
AF = mybir.ActivationFunctionType
ALU = mybir.AluOpType
AX = mybir.AxisListType

D = 1024
T = 2048
NDC = 8
NTT = 4
TS = 512
NSC = 16
DFF = 4096
NFC = 32
CHS = 2064
NEG = -1.0e30
EPS_LN = 1.0e-5
EPS_BN = 1.0e-5
EPS_LD = 1.0e-8

C_BD, C_OS, C_DEC, C_LA, C_GAM, C_LG, C_NLG, C_ALPHA = 0, 1, 2, 3, 4, 5, 6, 7
C_G12, C_G12B = 8, 10
C_R12 = 12
C_WMX, C_WMXR = 14, 18
C_R0, C_S0 = 19, 20
C_TMP = 21
C_E5, C_NH, C_E8, C_I32, C_Q4, C_HALFP = 22, 23, 24, 25, 26, 27


def build_nc():
    nc = bass.Bass("TRN2", target_bir_lowering=False)
    Q = {"qg": 0, "qs": 0, "da": 0, "db": 0, "dc_": 0, "dd": 0,
         "de": 0, "df": 0, "dg": 0, "dh": 0, "di": 0, "dj": 0,
         "dk": 0, "dl": 0, "dm": 0, "dn": 0, "t": 0, "v": 0, "s": 0}

    def din(name, shape, dt=F32):
        return nc.dram_tensor(name, shape, dt, kind="ExternalInput").ap()

    xT = din("xT", [D, T])
    Wq = din("Wq", [D, D])
    Wk = din("Wk", [D, D])
    WvT = din("WvT", [D, D])
    WdT = din("WdT", [D, 1])
    WoT = din("WoT", [D, D])
    kern = din("kern", [D, D])
    a_m = din("a_m", [D, 16])
    b_m = din("b_m", [16, D])
    WwrT = din("WwrT", [D, D])
    WrdT = din("WrdT", [D, D])
    W1T = din("W1T", [D, DFF])
    W2T = din("W2T", [DFF, D])
    cblobF_in = din("cblobF", [128, 2608])
    scIn_in = din("scIn", [128, 4])
    nmB = din("nmB", [128, 2048])
    cblobB_in = din("cblobB", [128, 384], BF16)

    outT = nc.dram_tensor("outT", [D, T], F32, kind="ExternalOutput").ap()
    x1S = nc.dram_tensor("x1S", [D, T], F32, kind="Internal").ap()
    W1qS = nc.dram_tensor("W1qS", [NFC, 128, 1024], BF16, kind="Internal").ap()
    W2qS = nc.dram_tensor("W2qS", [NDC, 128, 4096], BF16, kind="Internal").ap()
    ldS = nc.dram_tensor("ldS", [1, T], F32, kind="Internal").ap()
    bncS = nc.dram_tensor("bncS", [1, 16], F32, kind="Internal").ap()

    ctx = []

    def sb(name, shape, dt):
        cm = nc.sbuf_tensor(name, shape, dt)
        h = cm.__enter__()
        ctx.append(cm)
        return h

    def psum(name, shape):
        cm = nc.psum_tensor(name, shape, F32)
        h = cm.__enter__()
        ctx.append(cm)
        return h

    def sem(name):
        cm = nc.semaphore(name)
        h = cm.__enter__()
        ctx.append(cm)
        return h

    xf = sb("xf", [128, 4096], F32)
    xbf = sb("xbf", [128, 4096], BF16)
    xn = sb("xn", [128, NDC * CHS], BF16)
    bigA = sb("bigA", [128, 16384], BF16)
    bigB = sb("bigB", [128, 16384], BF16)
    stm = sb("stm", [128, 8192], BF16)
    rt = sb("rt", [128, 4096], BF16)
    wqA = sb("wqA", [128, 8192], BF16)
    projW = sb("projW", [128, 8192], BF16)
    cstF = sb("cstF", [128, 2672], F32)
    Bcs = cstF.ap()[:, 0:512]
    nm = cstF.ap()[:, 512:2560]
    biasL = cstF.ap()[:, 2560:2640]
    sc = cstF.ap()[:, 2640:2672]
    cstB = sb("cstB", [128, 384], BF16)
    idP = cstB.ap()[:, 0:128]
    idN = cstB.ap()[:, 128:256]
    inv1024 = cstB.ap()[:, 256:384]
    a_sb = sb("a_sb", [128, 128], BF16)
    b_sb = sb("b_sb", [16, 1024], BF16)
    wd_sb = sb("wd_sb", [128, 8], BF16)
    ld_cols = sb("ld_cols", [128, 16], F32)
    biascols = sb("biascols", [128, 16], F32)
    statsT = sb("statsT", [128, 2048], F32)
    maskb = sb("maskb", [128, 2048], F32)
    xa_bf = sb("xa_bf", [16, 512], BF16)

    statsB = statsT.ap().bitcast(BF16)
    rtF = rt.ap().bitcast(F32)
    stmF = stm.ap().bitcast(F32)

    pb = [psum(f"pb{k}", [128, 512]) for k in range(8)]

    sQg = sem("sQg")
    sQs = sem("sQs")
    sDa = sem("sDa")
    sDb = sem("sDb")
    sDc = sem("sDc")
    sDd = sem("sDd")
    sDe = sem("sDe")
    sDf = sem("sDf")
    sDg = sem("sDg")
    sDh = sem("sDh")
    sDi = sem("sDi")
    sDj = sem("sDj")
    sDk = sem("sDk")
    sDl = sem("sDl")
    sDm = sem("sDm")
    sDn = sem("sDn")
    sT = sem("sT")
    sV = sem("sV")
    sS = sem("sS")
    sG = sem("sG")

    def _dma_track(key, semh, n0, inst):
        inst.then_inc(semh, 16)
        add = 0
        for nmx in nc.inst_names()[n0:]:
            ins2 = nc.find_inst_by_name(nmx)
            si2 = getattr(ins2, "sync_info", None)
            if si2 is None:
                continue
            for u in si2.on_update:
                if u.ant_name == semh.name:
                    add += u.update_value
        assert add >= 16 and add % 16 == 0, (key, add)
        Q[key] += add
        return Q[key]

    class _QIssue:
        def __init__(self, key, semh):
            self.key, self.semh = key, semh

        def __call__(self, eng, dst, src, **kw):
            n0 = nc.n_instructions()
            inst = eng.dma_start(dst, src, **kw)
            return _dma_track(self.key, self.semh, n0, inst)

    qgD = _QIssue("qg", sQg)
    qsD = _QIssue("qs", sQs)
    qaD = _QIssue("da", sDa)
    qbD = _QIssue("db", sDb)
    qcD = _QIssue("dc_", sDc)
    qdD = _QIssue("dd", sDd)

    def slab_q(u):
        return (qaD, sDa) if u % 2 == 0 else (qbD, sDb)

    def store_q(u):
        return (qcD, sDc) if u % 2 == 0 else (qdD, sDd)

    qeD = _QIssue("de", sDe)
    qfD = _QIssue("df", sDf)

    def slab4_q(u):
        return [(qaD, sDa), (qbD, sDb), (qeD, sDe), (qfD, sDf)][u % 4]

    qgD2 = _QIssue("dg", sDg)
    qhD2 = _QIssue("dh", sDh)
    qiD = _QIssue("di", sDi)
    qjD = _QIssue("dj", sDj)
    qkD = _QIssue("dk", sDk)
    qlD = _QIssue("dl", sDl)
    qmD = _QIssue("dm", sDm)
    qnD = _QIssue("dn", sDn)

    def gam4_q(u):
        return [(qiD, sDi), (qjD, sDj), (qkD, sDk), (qlD, sDl)][u % 4]


    def store4_q(u):
        return [(qcD, sDc), (qdD, sDd), (qgD2, sDg), (qhD2, sDh)][u % 4]

    def ti(inst):
        Q["t"] += 1
        inst.then_inc(sT, 1)
        return Q["t"]

    def vi(inst):
        Q["v"] += 1
        inst.then_inc(sV, 1)
        return Q["v"]

    def si(inst):
        Q["s"] += 1
        inst.then_inc(sS, 1)
        return Q["s"]

    def xnp(dc, t0, shift):
        return dc * CHS + shift + t0

    def full_mat_src(t):
        return bass.AP(t.tensor, 0, [[D, 128], [128 * D, 8], [1, D]])

    def tile_src(t, j):
        return bass.AP(t.tensor, j * TS, [[T, 128], [T * 128, 8], [1, TS]])

    def tile_dst(t, j):
        return bass.AP(t.tensor, j * TS, [[T, 128], [T * 128, 8], [1, TS]])

    bigAF = bigA.ap().bitcast(F32)

    def kern_quarter_src(qr):
        return bass.AP(kern.tensor, qr * 2 * 128 * D, [[D, 128], [128 * D, 2], [1, D]])

    def unit_srcs(u):
        if u >= 16:
            dc, hh = divmod(u - 16, 2)
            return [(bass.AP(W1T.tensor, 128 * dc * DFF + hh * 2048,
                             [[DFF, 128], [1, 2048]]), 0, 2048)]
        fc = 2 * u
        return [(bass.AP(W2T.tensor, 128 * fc * D, [[D, 128], [1, 1024]]), 0, 1024),
                (bass.AP(W2T.tensor, 128 * (fc + 1) * D, [[D, 128], [1, 1024]]),
                 1024, 1024)]

    def unit_stores(u):
        if u >= 16:
            dc, hh = divmod(u - 16, 2)
            out = []
            for piece in range(2):
                fc0 = hh * 16 + piece * 8
                out.append((bass.AP(W1qS.tensor, fc0 * 131072 + dc * 128,
                                    [[1024, 128], [131072, 8], [1, 128]]),
                            piece * 1024, 1024))
            return out
        fc = 2 * u
        return [(bass.AP(W2qS.tensor, fc * 128,
                         [[4096, 128], [524288, 8], [1, 128]]), 0, 1024),
                (bass.AP(W2qS.tensor, (fc + 1) * 128,
                         [[4096, 128], [524288, 8], [1, 128]]), 1024, 1024)]

    NU = 32
    THRS = [m + 0.5 for m in range(-7, 7)]

    with nc.Block() as blk:
        v0, s0, t0 = Q["v"], Q["s"], Q["t"]
        qg_c = [0]
        qb_mx = [0]
        qb_gm = [0]
        q_mx = [0] * 4
        q_cm = [0] * 4
        q_gm = [0] * NU
        v_qred = [v0 + qr + 1 for qr in range(4)]
        v_comb = v0 + 5
        v_r0 = v0 + 8
        def v_cmp(ci):
            q, m = divmod(ci, 14)
            return v0 + 8 + 18 * q + m + 1
        v_qev = [v0 + 8 + 18 * q + 18 for q in range(4)]
        v_gred = v0 + 8 + 72 + 2
        s_tok = [s0 + 3 + u + 1 for u in range(NU)]
        def t_grp(q, m):
            return t0 + 56 * q + 4 * (m + 1)

        @blk.gpsimd
        def _(g):
            qgD(g, cstF.ap()[:, 0:2608], cblobF_in)
            qgD(g, cstB[:, :], cblobB_in)
            qgD(g, a_sb[:, :],
                bass.AP(a_m.tensor, 0, [[16, 128], [2048, 8], [1, 16]]))
            qgD(g, b_sb[:, :], b_m)
            qgD(g, wd_sb[:, :], bass.AP(WdT.tensor, 0, [[1, 128], [128, 8]]),
                allow_slow_non_contiguous=True)
            qg_c[0] = qgD(g, sc[:, 0:4], scIn_in)
            for u in range(NU):
                if u == 8:
                    g.wait_ge(sV, v_comb)
                    g.tensor_reduce(sc[0:1, C_WMXR:C_WMXR + 1],
                                    sc[:, C_TMP:C_TMP + 1], axis=AX.C, op=ALU.max)
                    g.drain()
                    tb1 = qgD(g, bncS[0:1, 0:1], sc[0:1, C_WMXR:C_WMXR + 1])
                    g.wait_ge(sQg, tb1)
                    qb_mx[0] = qgD(g, sc[:, C_WMXR:C_WMXR + 1],
                                   bass.AP(bncS.tensor, 0, [[0, 128], [1, 1]]))
                if u >= 4:
                    g.wait_ge(sS, s_tok[u - 4])
                slot = (u % 4) * 2048
                qfn, _sh = gam4_q(u)
                tok = None
                for srcu, so, w in unit_srcs(u):
                    tok = qfn(g, bigA[:, slot + so:slot + so + w], srcu)
                q_gm[u] = tok
            g.wait_ge(sV, v_gred)
            g.tensor_reduce(sc[0:1, C_G12B:C_G12B + 2], sc[:, C_G12:C_G12 + 2],
                            axis=AX.C, op=ALU.add)
            g.drain()
            tb2 = qgD(g, bncS[0:1, 2:4], sc[0:1, C_G12B:C_G12B + 2])
            g.wait_ge(sQg, tb2)
            qb_gm[0] = qgD(g, sc[:, C_G12B:C_G12B + 2],
                           bass.AP(bncS.tensor, 2, [[0, 128], [1, 2]]))
            g.wait_ge(sQg, qb_gm[0])

        xfF = xf.ap().bitcast(F32)

        def kq_slot(qr):
            if qr < 2:
                return stmF[:, qr * 2048:qr * 2048 + 2048]
            return xfF[:, (qr - 2) * 2048:(qr - 2) * 2048 + 2048]

        kq_q = [(qsD, sQs), (qaD, sDa), (qbD, sDb), (qcD, sDc)]

        @blk.sync
        def _(s):
            for qr in range(4):
                q_mx[qr] = kq_q[qr][0](s, kq_slot(qr), kern_quarter_src(qr))

        @blk.scalar
        def _(s):
            s.wait_ge(sQg, qg_c[0])
            si(s.activation(sc[:, C_GAM:C_GAM + 1], sc[:, C_DEC:C_DEC + 1], AF.Sigmoid))
            s.drain()
            si(s.activation(sc[:, C_LG:C_LG + 1], sc[:, C_GAM:C_GAM + 1], AF.Ln))
            s.drain()
            si(s.activation(sc[:, C_ALPHA:C_ALPHA + 1], sc[:, C_LA:C_LA + 1], AF.Exp))
            for u in range(NU):
                s.wait_ge(gam4_q(u)[1], q_gm[u])
                slot = (u % 4) * 2048
                s.drain()
                got = si(s.activation(
                    bigA[:, 8192:8192 + 2048], bigA[:, slot:slot + 2048], AF.Abs,
                    accum_out=statsT[:, u:u + 1]))
                assert got == s_tok[u]

        @blk.vector
        def _(v):
            v.memset(sc[:, C_E5:C_E5 + 1], EPS_LN)
            v.memset(sc[:, C_NH:C_NH + 1], -0.5)
            v.memset(sc[:, C_E8:C_E8 + 1], EPS_LD)
            v.memset(sc[:, C_I32:C_I32 + 1], 1.0 / 32.0)
            v.memset(sc[:, C_Q4:C_Q4 + 1], 0.25)
            v.memset(sc[:, C_HALFP:C_HALFP + 1], 0.5)
            v.wait_ge(sQg, qg_c[0])
            v.tensor_scalar_mul(biasL[:, 48:80], biasL[:, 8:40], 0.5)
            v.drain()
            for qr in range(4):
                v.wait_ge(kq_q[qr][1], q_mx[qr])
                got = vi(v.reduce_max(
                    sc[:, C_WMX + qr:C_WMX + qr + 1],
                    kq_slot(qr),
                    axis=AX.X, apply_absolute_value=True))
                assert got == v_qred[qr]
            v.drain()
            got = vi(v.reduce_max(sc[:, C_TMP:C_TMP + 1], sc[:, C_WMX:C_WMX + 4],
                                  axis=AX.X))
            assert got == v_comb
            v.wait_ge(sQg, qb_mx[0])
            vi(v.reciprocal(sc[:, C_TMP:C_TMP + 1], sc[:, C_WMXR:C_WMXR + 1]))
            v.drain()
            vi(v.tensor_scalar_mul(sc[:, C_R0:C_R0 + 1], sc[:, C_TMP:C_TMP + 1], 7.0))
            got = vi(v.tensor_scalar_mul(sc[:, C_S0:C_S0 + 1],
                                         sc[:, C_WMXR:C_WMXR + 1], 1.0 / 7.0))
            assert got == v_r0
            v.wait_ge(sS, s0 + 2)
            v.tensor_scalar_mul(sc[:, C_NLG:C_NLG + 1], sc[:, C_LG:C_LG + 1], -1.0)
            v.drain()
            for qr in range(4):
                for m in range(14):
                    ci = 14 * qr + m
                    if ci >= 4:
                        q2, m2 = divmod(ci - 4, 14)
                        v.wait_ge(sT, t_grp(q2, m2))
                    cslot = (rt[:, (ci % 4 // 2) * 2048:(ci % 4 // 2) * 2048 + 2048]
                             if ci % 2 == 0 else
                             xbf[:, (ci % 4 // 2) * 2048:(ci % 4 // 2) * 2048 + 2048])
                    got = vi(v.tensor_scalar(
                        cslot, kq_slot(qr),
                        sc[:, C_R0:C_R0 + 1], THRS[m], ALU.mult, ALU.is_gt))
                    assert got == v_cmp(ci)
                v.wait_ge(sT, t_grp(qr, 13))
                for h in range(4):
                    got = vi(v.tensor_scalar(
                        wqA[:, qr * 2048 + h * 512: qr * 2048 + (h + 1) * 512],
                        pb[h][:, :], -7.0, sc[:, C_S0:C_S0 + 1], ALU.add, ALU.mult))
                assert got == v_qev[qr]
            v.wait_ge(sS, s_tok[NU - 1])
            vi(v.reduce_sum(sc[:, C_G12:C_G12 + 1], statsT[:, 16:32], axis=AX.X))
            got = vi(v.reduce_sum(sc[:, C_G12 + 1:C_G12 + 2], statsT[:, 0:16],
                                  axis=AX.X))
            assert got == v_gred
            v.wait_ge(sQg, qb_gm[0])
            v.tensor_scalar(sc[:, C_TMP:C_TMP + 2], sc[:, C_G12B:C_G12B + 2],
                            1.0 / (DFF * D), EPS_BN, ALU.mult, ALU.add)
            v.drain()
            vi(v.reciprocal(sc[:, C_R12:C_R12 + 2], sc[:, C_TMP:C_TMP + 2]))

        @blk.tensor
        def _(t):
            for qr in range(4):
                if qr >= 1:
                    t.wait_ge(sV, v_qev[qr - 1])
                for m in range(14):
                    ci = 14 * qr + m
                    t.wait_ge(sV, v_cmp(ci))
                    cslot = (rt[:, (ci % 4 // 2) * 2048:(ci % 4 // 2) * 2048 + 2048]
                             if ci % 2 == 0 else
                             xbf[:, (ci % 4 // 2) * 2048:(ci % 4 // 2) * 2048 + 2048])
                    got = None
                    for h in range(4):
                        got = ti(t.matmul(pb[h][:, :], idP[:, :],
                                          cslot[:, h * 512:(h + 1) * 512],
                                          start=(m == 0), stop=(m == 13)))
                    assert got == t_grp(qr, m)

    def ln_proj(layer):
        shift = 0 if layer == 0 else 1
        src_t = xT if layer == 0 else x1S
        with nc.Block() as blk:
            v0, s0, t0 = Q["v"], Q["s"], Q["t"]
            qg_snap = Q["qg"]
            v_sq = [v0 + 27 * j + 8 for j in range(NTT)]
            v_mcopy = [v0 + 27 * j + 9 for j in range(NTT)]
            v_var = [v0 + 27 * j + 11 for j in range(NTT)]
            v_end = [v0 + 27 * j + 27 for j in range(NTT)]
            q_x = [0] * NTT
            q_b = [0] * NTT
            q_w = [0] * 3

            plan = []
            if layer == 0:
                plan += [("lnm", 0), ("lns", 0), ("lnm", 1), ("lns", 1)]
                plan += [("wg", gi) for gi in range(16)]
                plan += [("lnm", 2), ("lns", 2)]
                plan += [("g", ec, 0) for ec in range(8)]
                plan += [("lnm", 3), ("lns", 3)]
                plan += [("vtm", tc, et) for tc in range(0, 4) for et in range(2)]
                plan += [("g", ec, 1) for ec in range(8)]
                plan += [("ld", tt) for tt in range(4)]
                plan += [("vtm", tc, et) for tc in range(4, 8) for et in range(2)]
                plan += [("g", ec, 2) for ec in range(8)]
                plan += [("g", ec, 3) for ec in range(8)]
                plan += [("vtm", tc, et) for tc in range(8, 16) for et in range(2)]
            else:
                plan += [("lnm", 0), ("lns", 0), ("lnm", 1), ("lns", 1)]
                plan += [("vtm", tc, et) for tc in range(0, 4) for et in range(2)]
                plan += [("lnm", 2), ("lns", 2)]
                plan += [("vtm", tc, et) for tc in range(4, 8) for et in range(2)]
                plan += [("lnm", 3), ("lns", 3)]
                plan += [("vtm", tc, et) for tc in range(8, 16) for et in range(2)]
            t_tok = {}
            for i, e in enumerate(plan):
                t_tok[e] = t0 + 8 * (i + 1)
            t_m = [t_tok[("lnm", j)] for j in range(NTT)]
            t_s2 = [t_tok[("lns", j)] for j in range(NTT)]
            pseq = [e for e in plan if e[0] not in ("lnm", "lns")]
            pidx = {e: i for i, e in enumerate(pseq)}
            s_seq = []
            for e in plan:
                if e[0] == "lns":
                    s_seq.append(("rstd", e[1]))
                elif e[0] != "lnm":
                    s_seq.append(e)
            s_tok = {}
            cnt = s0
            for e in s_seq:
                cnt += 2 if e[0] == "rstd" else 1
                s_tok[e] = cnt
                if e == ("ld", 3):
                    cnt += 1
            s_rstd = [s_tok[("rstd", j)] for j in range(NTT)]

            NUQ = 32
            q_ld = [0] * NUQ
            st_tok = [0] * NUQ
            sg_tok = [0] * NUQ
            vc_tok = [0] * NUQ

            def ld_q(u):
                return [(qbD, sDb), (qeD, sDe), (qfD, sDf), (qmD, sDm)][u % 4]

            def ldslot(u):
                return bigAF[:, (u % 4) * 2048:(u % 4) * 2048 + 2048]

            def qslot_b(u):
                return bigA[:, (u % 4) * 4096:(u % 4) * 4096 + 2048]

            def tslot(u, k):
                if u < 16:
                    return wqA[:, ((u % 2) * 2 + k) * 2048:
                               ((u % 2) * 2 + k) * 2048 + 2048]
                p = (u - 16) % 4
                if p < 2:
                    return wqA[:, (p * 2 + k) * 2048:(p * 2 + k) * 2048 + 2048]
                return stm[:, ((p - 2) * 2 + k) * 2048:((p - 2) * 2 + k) * 2048 + 2048]


            def xslot(j):
                return xf[:, :]

            xfB = xf.ap().bitcast(BF16)

            def bslot(j):
                if j == 0:
                    return xbf[:, 0:4096]
                if j == 1:
                    return rt[:, 0:4096]
                return xfB[:, (j - 2) * 4096:(j - 2) * 4096 + 4096]

            def sqslot(j):
                return stm[:, (j % 2) * 4096:(j % 2) * 4096 + 4096]

            def mslot(j):
                return statsT[:, (j % 2) * 512:(j % 2) * 512 + 512]

            def rslot(j):
                return statsB[:, 2048 + (j % 2) * 512:2048 + (j % 2) * 512 + 512]

            maskbB = maskb.ap().bitcast(BF16)

            def nscr(j):
                return maskbB[:, 3072 + (j % 2) * 512:3072 + (j % 2) * 512 + 512]

            def varslot(j):
                return maskb[:, (j % 2) * 512:(j % 2) * 512 + 512]

            def scrslot(j):
                return maskb[:, 1024 + (j % 2) * 512:1024 + (j % 2) * 512 + 512]

            def pj_dst(e):
                if e[0] == "wg":
                    dc, et = divmod(e[1], 2)
                    return bigB[:, 8192 + dc * 1024 + et * 512:
                                8192 + dc * 1024 + (et + 1) * 512]
                if e[0] == "g":
                    ec, tt = e[1], e[2]
                    return bigA[:, ec * 2048 + tt * 512: ec * 2048 + (tt + 1) * 512]
                if e[0] == "vtm":
                    tc, et = e[1], e[2]
                    return bigB[:, tc * 1024 + et * 512: tc * 1024 + (et + 1) * 512]
                return None

            @blk.gpsimd
            def _(g):
                if shift == 1:
                    g.wait_ge(sQg, qg_snap)
                    g.wait_ge(sDi, Q["di"])
                    g.wait_ge(sDj, Q["dj"])
                    g.wait_ge(sDk, Q["dk"])
                    g.wait_ge(sDl, Q["dl"])
                qfn0 = [qiD, qjD]
                q_b[0] = qfn0[0](g, bslot(0), tile_src(src_t, 0))
                q_b[1] = qfn0[1](g, bslot(1), tile_src(src_t, 1))
                if layer == 0:
                    qgD(g, bigB[:, 0:8192], full_mat_src(Wq))
                    q_w[0] = qgD(g, projW[:, :], full_mat_src(Wk))
                else:
                    q_w[0] = qgD(g, bigA[:, 0:8192], full_mat_src(WwrT))
                    q_w[1] = qnD(g, projW[:, :], full_mat_src(WrdT))
                    q_w[2] = qlD(g, nm[:, :], nmB)
                for j in (1, 2, 3):
                    if j < 3:
                        jj = j + 1
                        g.wait_ge((sDi if jj % 2 == 0 else sDj), q_b[jj - 2])
                        q_b[jj] = qfn0[jj % 2](g, bslot(jj), tile_src(src_t, jj))
                if layer == 0:
                    g.wait_ge(sT, t_tok[("wg", 15)])
                    q_w[1] = qgD(g, projW[:, :], full_mat_src(WvT))
                    g.wait_ge(sS, s_tok[("ld", 3)] + 1)
                    tb = qgD(g, ldS[0:1, :], stmF[0:1, 0:2048])
                    g.wait_ge(sQg, tb)
                    tb = qgD(g, ld_cols[:, :],
                             bass.AP(ldS.tensor, 0, [[1, 128], [128, 16]]),
                             allow_slow_non_contiguous=True)
                    g.wait_ge(sQg, tb)
                else:
                    g.wait_ge(sDn, q_w[1])
                    g.wait_ge(sDl, q_w[2])

            @blk.sync
            def _(s):
                return

            @blk.tensor
            def _(t):
                np_ = 0
                for e in plan:
                    if e[0] == "lnm":
                        j = e[1]
                        t.wait_ge((sDi if j % 2 == 0 else sDj), q_b[j])
                        if j >= 2:
                            t.wait_ge(sV, v_mcopy[j - 2])
                        bs = bslot(j)
                        got = None
                        for dc in range(NDC):
                            got = ti(t.matmul(pb[(j % 2) * 2][:, :], inv1024[:, :],
                                              bs[:, dc * 512:(dc + 1) * 512],
                                              start=(dc == 0), stop=(dc == NDC - 1)))
                        assert got == t_m[j]
                        continue
                    if e[0] == "lns":
                        j = e[1]
                        t.wait_ge(sV, v_sq[j])
                        if j >= 2:
                            t.wait_ge(sV, v_var[j - 2])
                        sq = sqslot(j)
                        got = None
                        for dc in range(NDC):
                            got = ti(t.matmul(pb[(j % 2) * 2 + 1][:, :], inv1024[:, :],
                                              sq[:, dc * 512:(dc + 1) * 512],
                                              start=(dc == 0), stop=(dc == NDC - 1)))
                        assert got == t_s2[j]
                        continue
                    p = pidx[e]
                    if p >= 4:
                        t.wait_ge(sS, s_tok[pseq[p - 4]])
                    bank = pb[4 + p % 4]
                    if e[0] == "wg":
                        dc, et = divmod(e[1], 2)
                        t.wait_ge(sQg, q_w[0])
                        got = None
                        for fc in range(NDC):
                            got = ti(t.matmul(
                                bank[:, :],
                                bigB[:, fc * 1024 + dc * 128: fc * 1024 + (dc + 1) * 128],
                                projW[:, fc * 1024 + et * 512: fc * 1024 + (et + 1) * 512],
                                start=(fc == 0), stop=(fc == NDC - 1)))
                    elif e[0] == "g":
                        ec, tt = e[1], e[2]
                        t.wait_ge(sV, v_end[tt])
                        t.wait_ge(sS, s_tok[("wg", 15)])
                        got = None
                        for dc in range(NDC):
                            p0 = xnp(dc, tt * TS, 0)
                            got = ti(t.matmul(
                                bank[:, :],
                                bigB[:, 8192 + dc * 1024 + ec * 128:
                                     8192 + dc * 1024 + (ec + 1) * 128],
                                xn[:, p0:p0 + TS],
                                start=(dc == 0), stop=(dc == NDC - 1)))
                    elif e[0] == "vtm":
                        tc, et = e[1], e[2]
                        t.wait_ge(sV, v_end[tc // 4])
                        if layer == 0:
                            t.wait_ge(sQg, q_w[1])
                            wsrc = projW
                        else:
                            t.wait_ge(sQg, q_w[0])
                            wsrc = bigA
                        got = None
                        for dc in range(NDC):
                            p0 = xnp(dc, tc * 128, shift)
                            got = ti(t.matmul(
                                bank[:, :], xn[:, p0:p0 + 128],
                                wsrc[:, dc * 1024 + et * 512: dc * 1024 + (et + 1) * 512],
                                start=(dc == 0), stop=(dc == NDC - 1)))
                    else:
                        tt = e[1]
                        t.wait_ge(sV, v_end[tt])
                        got = None
                        for dc in range(NDC):
                            p0 = xnp(dc, tt * TS, 0)
                            got = ti(t.matmul(bank[0:1, :], wd_sb[:, dc:dc + 1],
                                              xn[:, p0:p0 + TS],
                                              start=(dc == 0), stop=(dc == NDC - 1)))
                    assert got == t_tok[e]

            @blk.vector
            def _(v):
                if shift == 1:
                    for dc in range(NDC):
                        v.memset(xn[:, dc * CHS:dc * CHS + 1], 0.0)
                for j in range(NTT):
                    if j >= 2:
                        v.wait_ge(sT, t_s2[j - 2])
                    v.wait_ge((sDi if j % 2 == 0 else sDj), q_b[j])
                    bs, sq = bslot(j), sqslot(j)
                    got = None
                    for dc in range(NDC):
                        got = vi(v.tensor_mul(sq[:, dc * 512:(dc + 1) * 512],
                                              bs[:, dc * 512:(dc + 1) * 512],
                                              bs[:, dc * 512:(dc + 1) * 512]))
                    assert got == v_sq[j], ('vsq', j, got, v_sq)
                    v.wait_ge(sT, t_m[j])
                    got = vi(v.tensor_copy(mslot(j), pb[(j % 2) * 2][:, :]))
                    assert got == v_mcopy[j]
                    v.drain()
                    vi(v.tensor_mul(scrslot(j), mslot(j), mslot(j)))
                    v.drain()
                    v.wait_ge(sT, t_s2[j])
                    got = vi(v.tensor_sub(varslot(j), pb[(j % 2) * 2 + 1][:, :],
                                          scrslot(j)))
                    assert got == v_var[j]
                    v.wait_ge(sS, s_rstd[j])
                    bs = bslot(j)
                    for dc in range(NDC):
                        v.drain()
                        vi(v.tensor_sub(nscr(j), bs[:, dc * 512:(dc + 1) * 512],
                                        mslot(j)))
                        v.drain()
                        p0 = xnp(dc, j * TS, shift)
                        got = vi(v.tensor_mul(xn[:, p0:p0 + TS], nscr(j), rslot(j)))
                    assert got == v_end[j]

            @blk.scalar
            def _(s):
                for e in s_seq:
                    if e[0] == "rstd":
                        j = e[1]
                        s.wait_ge(sV, v_var[j])
                        s.drain()
                        si(s.activation(scrslot(j), varslot(j), AF.Ln,
                                        bias=sc[:, C_E5:C_E5 + 1]))
                        s.drain()
                        got = si(s.activation(rslot(j), scrslot(j), AF.Exp,
                                              scale=sc[:, C_NH:C_NH + 1]))
                        assert got == s_rstd[j]
                        continue
                    p = pidx[e]
                    bank = pb[4 + p % 4]
                    s.wait_ge(sT, t_tok[e])
                    s.drain()
                    if e[0] == "wg":
                        got = si(s.activation(pj_dst(e), bank[:, :], AF.Copy,
                                              scale=sc[:, C_I32:C_I32 + 1]))
                    elif e[0] in ("g", "vtm"):
                        got = si(s.copy(pj_dst(e), bank[:, :]))
                    else:
                        tt = e[1]
                        s.wait_ge(sV, v_end[3])
                        got = si(s.activation(statsT[0:1, tt * 512:(tt + 1) * 512],
                                              bank[0:1, :], AF.Sigmoid,
                                              bias=sc[0:1, C_BD:C_BD + 1]))
                    assert got == s_tok[e]
                    if e == ("ld", 3):
                        s.drain()
                        si(s.activation(stmF[0:1, 0:2048], statsT[0:1, 0:2048],
                                        AF.Ln, bias=sc[0:1, C_E8:C_E8 + 1]))


    def PROJ0():
        ln_proj(0)

    def PROJ1():
        ln_proj(1)
        with nc.Block() as blk:
            @blk.gpsimd
            def _(g):
                for semh, key in ((sDb, "db"), (sDe, "de"), (sDf, "df"),
                                  (sDm, "dm"), (sDc, "dc_"), (sDd, "dd"),
                                  (sDg, "dg"), (sDh, "dh")):
                    g.wait_ge(semh, Q[key])

    PROJ0()

    QCARRY = {}

    def attn_tile(layer, j, blk, prev):
        shift = 0 if layer == 0 else 1
        if layer == 0:
            schunks = list(range(4 * j, NSC))

            def strad(i):
                return i < 4 * j + 4

            def cc(i):
                return float(128 * i - 512 * j - 1)
        else:
            schunks = list(range(0, 4 * j + 4))

            def strad(i):
                return i >= 4 * j

            def cc(i):
                return float(512 * j - 128 * i - 1)
        NS = len(schunks)

        v0, s0, t0 = Q["v"], Q["s"], Q["t"]
        prefA = []
        tot = 0
        for i in schunks:
            tot += 1 + (1 if strad(i) else 0)
            prefA.append(v0 + tot)
        totalA = tot
        if layer == 0:
            QUC = [6, 9, 9, 8]
            u0 = sum(QUC[:j])
            units_j = list(range(u0, u0 + QUC[j]))
            NH = 2 * len(units_j)
            NSL = [2, 3, 4, 4][j]
            anchors = ([("st", idx) for idx in range(NS)] + [("xa", 0)]
                       + [("R", ec) for ec in range(8)]
                       + [("res", dc) for dc in range(8)])
            NA = len(anchors)
            vplan = []
            qdone = 0
            for a, anc in enumerate(anchors):
                vplan.append(anc)
                want = min(NH, (NH * (a - 3)) // (NA - 7)) if a >= 4 else 0
                while qdone < want:
                    vplan.append(("q", qdone))
                    qdone += 1
            while qdone < NH:
                vplan.append(("q", qdone))
                qdone += 1
            vcnt = v0 + totalA
            v_mult = [0] * NS
            v_R = [0] * 8
            v_res = [0] * 8
            cmb_tok = [0] * NH
            v_xa = 0
            for e in vplan:
                vcnt += 1
                if e[0] == "q":
                    cmb_tok[e[1]] = vcnt
                elif e[0] == "st":
                    v_mult[e[1]] = vcnt
                elif e[0] == "xa":
                    v_xa = vcnt
                elif e[0] == "R":
                    v_R[e[1]] = vcnt
                else:
                    v_res[e[1]] = vcnt
            splan = []
            for e in vplan:
                if e[0] == "st":
                    splan.append(("m", e[1]))
                elif e[0] == "q":
                    splan.append(("g", e[1]))
            scnt = s0
            s_mask = [0] * NS
            sg_tok = [0] * NH
            for e in splan:
                if e[0] == "m":
                    scnt += 1
                    s_mask[e[1]] = scnt
                else:
                    scnt += 2
                    sg_tok[e[1]] = scnt
            t_xa = t0 + 8
            t_st = [t_xa + 8 * (idx + 1) for idx in range(NS)]
            t_R = [t_st[NS - 1] + NS * (ec + 1) for ec in range(8)]
            t_res = [t_R[7] + 17 * (dc + 1) for dc in range(8)]
            xbfF = xbf.ap().bitcast(F32)
            stmF2 = stm.ap().bitcast(F32)

            def qldF(h):
                s_ = h % NSL
                if s_ < 2:
                    return xbfF[:, s_ * 1024:s_ * 1024 + 1024]
                return stmF2[:, (s_ - 2) * 1024:(s_ - 2) * 1024 + 1024]

            def qout(h):
                s_ = h % NSL
                if s_ < 2:
                    return xbf[:, s_ * 2048:s_ * 2048 + 1024]
                return stm[:, (s_ - 2) * 2048:(s_ - 2) * 2048 + 1024]

            def qt1c(h):
                return statsB[:, (h % 2) * 2048:(h % 2) * 2048 + 1024]

            def qt2c(h):
                return statsB[:, (h % 2) * 2048 + 1024:(h % 2) * 2048 + 2048]

            def q_src(h):
                u = units_j[h // 2]
                half = h % 2
                if u >= 16:
                    dcq, hh = divmod(u - 16, 2)
                    return bass.AP(W1T.tensor,
                                   128 * dcq * DFF + hh * 2048 + half * 1024,
                                   [[DFF, 128], [1, 1024]])
                fc = 2 * u + half
                return bass.AP(W2T.tensor, 128 * fc * D, [[D, 128], [1, 1024]])

            def q_dst(h):
                u = units_j[h // 2]
                half = h % 2
                if u >= 16:
                    dcq, hh = divmod(u - 16, 2)
                    fc0 = hh * 16 + half * 8
                    return bass.AP(W1qS.tensor, fc0 * 131072 + dcq * 128,
                                   [[1024, 128], [131072, 8], [1, 128]])
                fc = 2 * u + half
                return bass.AP(W2qS.tensor, fc * 128,
                               [[4096, 128], [524288, 8], [1, 128]])

            def qld_q(h):
                return [(qbD, sDb), (qeD, sDe), (qfD, sDf), (qmD, sDm)][h % 4]

            def qst_q(h):
                return [(qiD, sDi), (qjD, sDj), (qkD, sDk), (qlD, sDl)][h % 4]

            qst_keys = ["di", "dj", "dk", "dl"]
            exp_qst = [Q[qst_keys[h % 4]] + 16 * (h // 4 + 1) for h in range(NH)]
        else:
            v_mult = [v0 + totalA + idx + 1 for idx in range(NS)]
            vp = v0 + totalA + NS
            v_R = [vp + ec + 1 for ec in range(8)]
            v_res = [vp + 8 + dc + 1 for dc in range(8)]
            t_st = [t0 + 8 * (idx + 1) for idx in range(NS)]
            t_R = [t_st[NS - 1] + NS * (ec + 1) for ec in range(8)]
            t_h = [t_R[7] + 8 * (fc + 1) for fc in range(NFC)]
            t_res = [t_h[NFC - 1] + 40 * (dc + 1) for dc in range(8)]
        if layer == 1:
            s_mask = [s0 + idx + 1 for idx in range(NS)]
            s_h = [s0 + NS + fc + 1 for fc in range(NFC)]

        q_x = [0]
        q_w1 = [0] * NFC
        q_w2 = [0] * 8
        q_qld = [0] * 64
        q_qst = [0] * 64
        if layer == 1:
            w2q_q = [(qiD, sDi), (qjD, sDj)]
            exp_w2 = [Q[["di", "dj"][dc % 2]] + 16 * (dc // 2 + 1)
                      for dc in range(8)]

        @blk.sync
        def _(s):
            s.wait_ge(sQg, Q["qg"])
            src = xT if layer == 0 else x1S
            q_x[0] = qsD(s, xf[:, :], tile_src(src, j))
            if layer == 0:
                if j > 0:
                    for kq, tokq in QCARRY["fin"]:
                        s.wait_ge(kq, tokq)
                stm_gd = [False]

                def qld_issue(h):
                    if prev and not stm_gd[0] and (h % NSL) >= 2:
                        s.wait_ge(sT, prev["t_R"][7])
                        stm_gd[0] = True
                    q_qld[h] = qld_q(h)[0](s, qldF(h), q_src(h))

                for h in range(min(NSL, NH)):
                    qld_issue(h)
                for h in range(NSL, NH):
                    s.wait_ge(qst_q(h - NSL)[1], exp_qst[h - NSL])
                    qld_issue(h)
                fin = {}
                for h in range(NH):
                    fin[qst_q(h)[1].name] = (qst_q(h)[1], exp_qst[h])
                QCARRY["fin"] = list(fin.values())
            if layer == 1:
                if j == 0:
                    s.wait_ge(sDi, Q["di"])
                    s.wait_ge(sDj, Q["dj"])
                    s.wait_ge(sDk, Q["dk"])
                    s.wait_ge(sDl, Q["dl"])
                for fc in range(NFC):
                    if fc >= 4:
                        s.wait_ge(sT, t_h[fc - 4])
                    elif prev:
                        s.wait_ge(sT, prev["t_h"][28 + fc])
                    qfn, _sh = slab4_q(fc)
                    q_w1[fc] = qfn(s,
                        xbf[:, (fc % 4) * 1024:(fc % 4) * 1024 + 1024],
                        bass.AP(W1qS.tensor, fc * 128 * 1024,
                                [[1024, 128], [1, 1024]]))

        @blk.vector
        def _(v):
            if prev:
                v.wait_ge(sS, prev["s_last"] if layer == 0
                          else prev["s_mask"][-1])
                v.wait_ge(sT, prev["t_R"][7])
            for idx, i in enumerate(schunks):
                scl = ld_cols[:, i:i + 1] if layer == 0 else sc[:, C_LG:C_LG + 1]
                vi(v.tensor_scalar_mul(biascols[:, i:i + 1], scl, cc(i)))
                if strad(i):
                    k = i - 4 * j
                    escl = (ld_cols[:, i:i + 1] if layer == 0
                            else sc[:, C_NLG:C_NLG + 1])
                    vi(v.scalar_tensor_tensor(
                        statsB[:, k * 512:(k + 1) * 512], Bcs[:, :], escl,
                        nm[:, k * 512:(k + 1) * 512], ALU.mult, ALU.add))
            assert Q["v"] == v0 + totalA
            if layer == 0:
                for e in vplan:
                    if e[0] == "q":
                        h = e[1]
                        v.wait_ge(sS, sg_tok[h])
                        got = vi(v.tensor_add(qout(h), qt1c(h), qt2c(h)))
                        assert got == cmb_tok[h]
                    elif e[0] == "st":
                        idx = e[1]
                        i = schunks[idx]
                        v.wait_ge(sT, t_st[idx])
                        v.wait_ge(sS, s_mask[idx])
                        got = vi(v.tensor_mul(
                            stm[:, i * 512:(i + 1) * 512], pb[idx % 4][:, :],
                            maskb[:, (idx % 4) * 512:(idx % 4) * 512 + 512]))
                        assert got == v_mult[idx]
                    elif e[0] == "xa":
                        v.wait_ge(sT, t_xa)
                        got = vi(v.tensor_copy(xa_bf[:, :], pb[4][0:16, :]))
                        assert got == v_xa
                    elif e[0] == "R":
                        ec = e[1]
                        v.wait_ge(sT, t_R[ec])
                        got = vi(v.tensor_scalar_mul(
                            rt[:, ec * 512:(ec + 1) * 512],
                            pb[4 + ec % 4][:, :], sc[:, C_OS:C_OS + 1]))
                        assert got == v_R[ec]
                    else:
                        dc = e[1]
                        v.wait_ge(sT, t_res[dc])
                        v.wait_ge(sQs, q_x[0])
                        got = vi(v.scalar_tensor_tensor(
                            xf[:, dc * 512:(dc + 1) * 512], pb[dc % 2][:, :],
                            biasL[:, dc:dc + 1],
                            xf[:, dc * 512:(dc + 1) * 512], ALU.add, ALU.add))
                        assert got == v_res[dc]
            else:
                for idx, i in enumerate(schunks):
                    v.wait_ge(sT, t_st[idx])
                    v.wait_ge(sS, s_mask[idx])
                    got = vi(v.tensor_mul(
                        stm[:, i * 512:(i + 1) * 512], pb[idx % 4][:, :],
                        maskb[:, (idx % 4) * 512:(idx % 4) * 512 + 512]))
                    assert got == v_mult[idx]
                for ec in range(8):
                    v.wait_ge(sT, t_R[ec])
                    got = vi(v.tensor_scalar_mul(
                        rt[:, ec * 512:(ec + 1) * 512], pb[4 + ec % 4][:, :],
                        sc[:, C_ALPHA:C_ALPHA + 1]))
                    assert got == v_R[ec]
                for dc in range(8):
                    v.wait_ge(sT, t_res[dc])
                    v.wait_ge(sQs, q_x[0])
                    got = vi(v.scalar_tensor_tensor(
                        xf[:, dc * 512:(dc + 1) * 512], pb[2 + dc % 2][:, :],
                        biasL[:, 40 + dc:41 + dc],
                        xf[:, dc * 512:(dc + 1) * 512], ALU.add, ALU.add))
                    assert got == v_res[dc]

        def emit_mask(s, idx):
            i = schunks[idx]
            s.wait_ge(sV, prefA[idx])
            if idx >= 4:
                s.wait_ge(sV, v_mult[idx - 4])
            elif prev:
                s.wait_ge(sV, prev["v_mult"][prev["NS"] - 4 + idx])
            mslot = maskb[:, (idx % 4) * 512:(idx % 4) * 512 + 512]
            if strad(i):
                k = i - 4 * j
                got = si(s.activation(mslot, statsB[:, k * 512:(k + 1) * 512],
                                      AF.Exp, bias=biascols[:, i:i + 1]))
            else:
                scl = (ld_cols[:, i:i + 1] if layer == 0
                       else sc[:, C_NLG:C_NLG + 1])
                got = si(s.activation(mslot, Bcs[:, :], AF.Exp,
                                      bias=biascols[:, i:i + 1], scale=scl))
            assert got == s_mask[idx]

        @blk.scalar
        def _(s):
            if layer == 0:
                for e in splan:
                    if e[0] == "m":
                        emit_mask(s, e[1])
                        continue
                    h = e[1]
                    s.wait_ge(qld_q(h)[1], q_qld[h])
                    if h >= 2:
                        s.wait_ge(sV, cmb_tok[h - 2])
                    elif prev and prev["NH"]:
                        s.wait_ge(sV, prev["cmb"][prev["NH"] - 2 + h])
                    u = units_j[h // 2]
                    rcol = (sc[:, C_R12:C_R12 + 1] if u >= 16
                            else sc[:, C_R12 + 1:C_R12 + 2])
                    s.drain()
                    si(s.activation(qt1c(h), qldF(h), AF.Sign,
                                    scale=rcol, bias=sc[:, C_NH:C_NH + 1]))
                    got = si(s.activation(qt2c(h), qldF(h), AF.Sign,
                                          scale=rcol,
                                          bias=sc[:, C_HALFP:C_HALFP + 1]))
                    assert got == sg_tok[h]
            else:
                for idx in range(NS):
                    emit_mask(s, idx)
            if layer == 1:
                for fc in range(NFC):
                    s.wait_ge(sT, t_h[fc])
                    got = si(s.activation(
                        bigA[:, fc * 512:(fc + 1) * 512], pb[fc % 2][:, :],
                        AF.Relu, bias=biasL[:, 48 + fc:49 + fc],
                        scale=sc[:, C_Q4:C_Q4 + 1]))
                    assert got == s_h[fc]

        @blk.tensor
        def _(t):
            if layer == 0:
                if prev:
                    t.wait_ge(sV, prev["v_R"][4])
                got = None
                for dc in range(NDC):
                    p0 = xnp(dc, j * TS, 0)
                    got = ti(t.matmul(pb[4][0:16, :], a_sb[:, dc * 16:(dc + 1) * 16],
                                      xn[:, p0:p0 + TS],
                                      start=(dc == 0), stop=(dc == NDC - 1)))
                assert got == t_xa
            for idx, i in enumerate(schunks):
                if idx >= 4:
                    t.wait_ge(sV, v_mult[idx - 4])
                elif prev:
                    if layer == 0:
                        pw = [("v", prev["v_res"][6]), ("v", prev["v_res"][7]),
                              ("v", prev["v_mult"][prev["NS"] - 2]),
                              ("v", prev["v_mult"][prev["NS"] - 1])][idx]
                    else:
                        pw = [("s", prev["s_h"][30]), ("s", prev["s_h"][31]),
                              ("v", prev["v_res"][6]),
                              ("v", prev["v_res"][7])][idx]
                    t.wait_ge(sV if pw[0] == "v" else sS, pw[1])
                bank = pb[idx % 4]
                got = None
                for ec in range(NDC):
                    if layer == 0:
                        p0 = xnp(ec, i * 128, 0)
                        lhs = xn[:, p0:p0 + 128]
                        rhs = bigA[:, ec * 2048 + j * 512: ec * 2048 + j * 512 + 512]
                    else:
                        lhs = xn[:, ec * CHS + i * 128: ec * CHS + i * 128 + 128]
                        rhs = xn[:, ec * CHS + 1 + j * 512: ec * CHS + 1 + j * 512 + 512]
                    got = ti(t.matmul(bank[:, :], lhs, rhs,
                                      start=(ec == 0), stop=(ec == NDC - 1)))
                assert got == t_st[idx]
            for ec in range(8):
                if ec == 0:
                    t.wait_ge(sV, v_mult[NS - 1])
                    if layer == 0:
                        t.wait_ge(sV, v_xa)
                if ec >= 4:
                    t.wait_ge(sV, v_R[ec - 4])
                elif prev:
                    t.wait_ge(sV, prev["v_R"][ec + 4])
                bank = pb[4 + ec % 4]
                got = None
                for n, i in enumerate(schunks):
                    got = ti(t.matmul(
                        bank[:, :],
                        bigB[:, i * 1024 + ec * 128: i * 1024 + (ec + 1) * 128],
                        stm[:, i * 512:(i + 1) * 512],
                        start=(n == 0), stop=(n == NS - 1)))
                assert got == t_R[ec]
            if layer == 1:
                for fc in range(NFC):
                    t.wait_ge(slab4_q(fc)[1], q_w1[fc])
                    if fc == 0:
                        t.wait_ge(sV, v_mult[NS - 1])
                    if fc >= 2:
                        t.wait_ge(sS, s_h[fc - 2])
                    bank = pb[fc % 2]
                    got = None
                    for dc in range(NDC):
                        p0 = xnp(dc, j * TS, 1)
                        got = ti(t.matmul(
                            bank[:, :],
                            xbf[:, (fc % 4) * 1024 + dc * 128:
                                 (fc % 4) * 1024 + (dc + 1) * 128],
                            xn[:, p0:p0 + TS],
                            start=(dc == 0), stop=(dc == NDC - 1)))
                    assert got == t_h[fc]
            rb = 0 if layer == 0 else 2
            for dc in range(8):
                if dc == 0:
                    t.wait_ge(sV, v_mult[NS - 1])
                    t.wait_ge(sV, v_R[7])
                    if layer == 1:
                        t.wait_ge(sS, s_h[NFC - 1])
                if dc >= 2:
                    t.wait_ge(sV, v_res[dc - 2])
                if layer == 1:
                    t.wait_ge(w2q_q[dc % 2][1], exp_w2[dc])
                bank = pb[rb + dc % 2]
                first = True
                got = None
                for ec in range(8):
                    got = ti(t.matmul(
                        bank[:, :],
                        projW[:, ec * 1024 + dc * 128: ec * 1024 + (dc + 1) * 128],
                        rt[:, ec * 512:(ec + 1) * 512],
                        start=first, stop=False))
                    first = False
                if layer == 0:
                    for d2 in range(8):
                        p0 = xnp(d2, j * TS, 0)
                        got = ti(t.matmul(
                            bank[:, :],
                            wqA[:, d2 * 1024 + dc * 128: d2 * 1024 + (dc + 1) * 128],
                            xn[:, p0:p0 + TS], start=False, stop=False))
                    got = ti(t.matmul(bank[:, :],
                                      b_sb[0:16, dc * 128:(dc + 1) * 128],
                                      xa_bf[0:16, :], start=False, stop=True))
                else:
                    for fc in range(NFC):
                        got = ti(t.matmul(
                            bank[:, :],
                            wqA[:, (dc % 2) * 4096 + fc * 128:
                                 (dc % 2) * 4096 + (fc + 1) * 128],
                            bigA[:, fc * 512:(fc + 1) * 512],
                            start=False, stop=(fc == NFC - 1)))
                assert got == t_res[dc]

        @blk.gpsimd
        def _(g):
            if layer == 1:
                for dc in range(8):
                    if dc >= 2:
                        g.wait_ge(sT, t_res[dc - 2])
                    elif prev:
                        g.wait_ge(sT, prev["t_res"][6 + dc])
                    got = w2q_q[dc % 2][0](g,
                        wqA[:, (dc % 2) * 4096:(dc % 2) * 4096 + 4096],
                        bass.AP(W2qS.tensor, dc * 128 * 4096,
                                [[4096, 128], [1, 4096]]))
                    assert got == exp_w2[dc], (dc, got, exp_w2[dc])
            if layer == 0:
                for h in range(NH):
                    if h >= 4:
                        g.wait_ge(qst_q(h - 4)[1], q_qst[h - 4])
                    g.wait_ge(sV, cmb_tok[h])
                    q_qst[h] = qst_q(h)[0](g, q_dst(h), qout(h))
                    assert q_qst[h] == exp_qst[h], (h, q_qst[h], exp_qst[h])
            g.wait_ge(sV, v_res[7])
            dst = x1S if layer == 0 else outT
            qgD(g, tile_dst(dst, j), xf[:, :])

        return dict(v_R=v_R, v_res=v_res, v_mult=v_mult,
                    v_xa=(v_xa if layer == 0 else 0),
                    s_mask=s_mask, s_h=(s_h if layer == 1 else None),
                    s_last=(scnt if layer == 0 else s_h[NFC - 1]),
                    cmb=(cmb_tok if layer == 0 else None),
                    NH=(NH if layer == 0 else 0),
                    t_R=t_R, t_res=t_res,
                    t_h=(t_h if layer == 1 else None), NS=NS)

    for jp in ((0, 1), (2, 3)):
        with nc.Block() as blk:
            prevt = None
            for j in jp:
                prevt = attn_tile(0, j, blk, prevt)

    PROJ1()

    for jp in ((0, 1), (2, 3)):
        with nc.Block() as blk:
            prevt = None
            for j in jp:
                prevt = attn_tile(1, j, blk, prevt)

    with nc.Block() as blk:
        @blk.gpsimd
        def _(g):
            g.wait_ge(sQg, Q["qg"])
            g.wait_ge(sQs, Q["qs"])

    for cm in reversed(ctx):
        cm.__exit__(None, None, None)
    return nc


def _host_consts():
    p = np.arange(128, dtype=np.float32)[:, None]
    jj = np.arange(512, dtype=np.float32)[None, :]
    Bc = (p - jj).astype(np.float32)
    nmA = np.empty((128, 2048), dtype=np.float32)
    nmB = np.empty((128, 2048), dtype=np.float32)
    for k in range(4):
        vA = np.where(p - jj + 128 * k - 1 >= 0, 0.0, NEG).astype(np.float32)
        vB = np.where(jj - p - 128 * k - 1 >= 0, 0.0, NEG).astype(np.float32)
        nmA[:, k * 512:(k + 1) * 512] = vA
        nmB[:, k * 512:(k + 1) * 512] = vB
    idP = np.eye(128, dtype=np.float32).astype(ml_dtypes.bfloat16)
    idN = (-np.eye(128, dtype=np.float32)).astype(ml_dtypes.bfloat16)
    inv1024 = np.full((128, 128), 1.0 / 1024.0, dtype=np.float32).astype(ml_dtypes.bfloat16)
    return Bc, nmA, nmB, idP, idN, inv1024


def kernel(L0_attn_x, L0_attn_Wq, L0_attn_Wk, L0_attn_Wv, L0_attn_Wo,
           L0_attn_Wd, L0_attn_bd, L0_attn_out_scale, L0_ffn_kernel,
           L0_ffn_bias, L0_ffn_a, L0_ffn_b, L1_attn_W_write, L1_attn_W_read,
           L1_attn_decay, L1_attn_log_alpha, L1_ffn_W1, L1_ffn_b1, L1_ffn_W2,
           L1_ffn_b2):
    f = np.float32

    def c(x):
        return np.ascontiguousarray(np.asarray(x, dtype=f))

    x = c(L0_attn_x)
    Bc, nmA, nmB, idP, idN, inv1024 = _host_consts()

    def rep(v):
        return np.full((128, 1), float(np.asarray(v, dtype=f).reshape(-1)[0]), dtype=f)

    cblobF = np.concatenate(
        [Bc, nmA,
         c(np.asarray(L0_ffn_bias).reshape(8, 128).T),
         c(np.asarray(L1_ffn_b1).reshape(32, 128).T),
         c(np.asarray(L1_ffn_b2).reshape(8, 128).T)], axis=1)
    scIn = np.concatenate(
        [rep(L0_attn_bd), rep(L0_attn_out_scale),
         rep(L1_attn_decay), rep(L1_attn_log_alpha)], axis=1)
    cblobB = np.concatenate([idP, idN, inv1024], axis=1)
    shared = {
        "Wq": c(L0_attn_Wq),
        "Wk": c(L0_attn_Wk),
        "WvT": c(np.asarray(L0_attn_Wv).T),
        "WdT": c(np.asarray(L0_attn_Wd).reshape(1, D).T),
        "WoT": c(np.asarray(L0_attn_Wo).T),
        "kern": c(L0_ffn_kernel),
        "a_m": c(L0_ffn_a),
        "b_m": c(L0_ffn_b),
        "WwrT": c(np.asarray(L1_attn_W_write).T),
        "WrdT": c(np.asarray(L1_attn_W_read).T),
        "W1T": c(np.asarray(L1_ffn_W1).T),
        "W2T": c(np.asarray(L1_ffn_W2).T),
        "cblobF": cblobF,
        "scIn": scIn,
        "nmB": nmB,
        "cblobB": cblobB,
    }

    n = x.shape[0]
    in_maps = []
    for ci in range(n):
        m = dict(shared)
        m["xT"] = np.ascontiguousarray(x[ci].T)
        in_maps.append(m)

    nc = build_nc()
    res = run_bass_kernel_spmd(nc, in_maps, core_ids=list(range(n)))
    out = np.stack([np.ascontiguousarray(np.asarray(r["outT"]).T)
                    for r in res.results], axis=0)
    return out.astype(np.float32)
